# version 03
# baseline (speedup 1.0000x reference)
import numpy as np
import ml_dtypes
from contextlib import ExitStack

import concourse.bass as bass
import concourse.mybir as mybir
from concourse.bass_utils import run_bass_kernel_spmd
from concourse.aot_env import allow_ici_all_to_all

F32 = mybir.dt.float32
BF16 = mybir.dt.bfloat16
I32 = mybir.dt.int32
ALU = mybir.AluOpType
AF = mybir.ActivationFunctionType
AX = mybir.AxisListType.X

D = 4096
NCH = 32
TOK = 1024
GTOK = 8192
NCORE = 8
SEQ = 2048
RSQ = 0.08838834764831845
EXPB = -5.0
LNEPS = 1e-5
DENEPS = 1e-6

ENGS = ("pe", "dve", "act", "pool", "sync")
TAPS = set()
DBG_SKIP_X1 = False
DBG_ATT_DRAM = False


class Tile:
    __slots__ = ("h", "p", "c0", "w", "name")

    def __init__(self, h, p, c0, w, name=""):
        self.h, self.p, self.c0, self.w, self.name = h, p, c0, w, name

    def __repr__(self):
        return f"T[{self.name}]"

    def ap(self, c=None, p=None):
        p0, p1 = (0, self.p) if p is None else p
        c0, c1 = (0, self.w) if c is None else c
        return self.h[p0:p1, self.c0 + c0:self.c0 + c1]


class Arena:
    def __init__(self, h, cols, name):
        self.h, self.cols, self.name = h, cols, name
        self.off = 0
        self.marks = []
        self.peak = 0

    def alloc(self, w, p=128, name=""):
        w_al = (w + 127) & ~127
        t = Tile(self.h, p, self.off, w, name)
        self.off += w_al
        self.peak = max(self.peak, self.off)
        assert self.off <= self.cols, \
            f"arena {self.name} overflow at {name}: {self.off} > {self.cols}"
        return t

    def allocs(self, n, w, p=128, name=""):
        return [self.alloc(w, p, f"{name}{i}") for i in range(n)]

    def push(self):
        self.marks.append(self.off)

    def pop(self):
        self.off = self.marks.pop()


class Prog:

    def __init__(self):
        self.cnt = {e: 0 for e in ENGS}
        self.known = {e: {} for e in ENGS}
        self.ops = {e: [] for e in ENGS}
        self.res = {}
        self.extra_sems = []

    def _r(self, key):
        r = self.res.get(key)
        if r is None:
            r = {"w": None, "r": {}}
            self.res[key] = r
        return r

    def _wait(self, eng, sem, val):
        if sem == eng:
            assert val <= self.cnt[eng]
        if self.known[eng].get(sem, 0) >= val:
            return
        self.known[eng][sem] = val
        self.ops[eng].append(("wait", sem, val))

    def op(self, eng, emit, ins=(), outs=(), inc=None, sem=None):
        if inc is None:
            inc = 16 if eng == "sync" else 1
        sem_name = sem or eng
        best = {}
        for k in ins:
            r = self._r(k)
            if r["w"]:
                s, v = r["w"]
                if best.get(s, 0) < v:
                    best[s] = v
        for k in outs:
            r = self._r(k)
            if r["w"]:
                s, v = r["w"]
                if best.get(s, 0) < v:
                    best[s] = v
            for s, v in r["r"].items():
                if best.get(s, 0) < v:
                    best[s] = v
        for s, v in sorted(best.items(), key=str):
            self._wait(eng, s, v)
        if sem is None:
            self.cnt[eng] += inc
            val = self.cnt[eng]
        else:
            val = inc
        self.ops[eng].append(("emit", emit, inc, sem_name))
        for k in ins:
            self._r(k)["r"][sem_name] = val
        for k in outs:
            r = self._r(k)
            r["w"] = (sem_name, val)
            r["r"] = {}
        return val

    def dma(self, q, dst, src, ins=(), outs=()):
        def emit(e, dst=dst, src=src):
            return e.dma_start(out=dst, in_=src)
        self.op(q, emit, ins=ins, outs=outs, inc=16)

    def barrier(self):
        for e in ENGS:
            for d in ENGS:
                if d != e and self.cnt[d] > 0:
                    self._wait(e, d, self.cnt[d])

    def realize(self, nc, stack):
        sems = {}
        for e in ENGS:
            sems[e] = stack.enter_context(nc.semaphore(f"s_{e}"))
        for s in self.extra_sems:
            sems[s] = stack.enter_context(nc.semaphore(f"s_{s}"))
        block = stack.enter_context(nc.Block())

        def runner(engname):
            def run(eng):
                for item in self.ops[engname]:
                    if item[0] == "wait":
                        eng.wait_ge(sems[item[1]], item[2])
                    else:
                        _, fn, inc, sem_name = item
                        fn(eng).then_inc(sems[sem_name], inc)
            return run

        block.tensor(runner("pe"))
        block.vector(runner("dve"))
        block.scalar(runner("act"))
        block.gpsimd(runner("pool"))
        block.sync(runner("sync"))


def _mmgroup(P, out_ap, pairs, ins, outs, first=True, last=True, sgc=False):
    n = len(pairs)

    def emit(e, pairs=pairs, out_ap=out_ap, n=n, first=first, last=last, sgc=sgc):
        lastins = None
        for i, (l, r) in enumerate(pairs):
            lastins = e.matmul(out_ap, l, r, start=(first and i == 0),
                               stop=(last and i == n - 1), skip_group_check=sgc)
        return lastins
    P.op("pe", emit, ins=ins, outs=outs)


def build_program():
    nc = bass.Bass("TRN2", target_bir_lowering=False, debug=False)

    di = {}

    def inp(name, shape, dt=F32):
        di[name] = nc.dram_tensor(name, list(shape), dt, kind="ExternalInput").ap()
        return di[name]

    xT = inp("xT", (D, TOK))
    w_qkv = inp("w_qkv", (D, 1536))
    b_qkv = inp("b_qkv", (128, 12))
    w_o0 = inp("w_o0", (D, D))
    b_o0 = inp("b_o0", (128, 32))
    w_A0 = inp("w_A0", (D, 1024))
    w_B0 = inp("w_B0", (1024, D))
    w_r0 = inp("w_r0", (D, 64))
    w_q1 = inp("w_q1", (D, 1024))
    b_q1 = inp("b_q1", (128, 8))
    w_M1 = inp("w_M1", (1024, 128))
    w_z1 = inp("w_z1", (128, 8))
    w_o1 = inp("w_o1", (D, D))
    w_b1 = inp("w_b1", (D, D))
    b_b1 = inp("b_b1", (128, 32))
    w_r1 = inp("w_r1", (D, 15))
    b_r1 = inp("b_r1", (15, 1))
    b_r1row = inp("b_r1row", (1, 15))
    w_A1 = inp("w_A1", (D, 128))
    w_B1 = inp("w_B1", (128, D))
    emap = inp("emap", (15, 1), I32)
    ident_bf = inp("ident_bf", (128, 128), BF16)
    ident_f = inp("ident_f", (128, 128), F32)
    masktri = inp("masktri", (128, 128), BF16)
    E0c = inp("E0c", (64, 1024), BF16)
    E1c = inp("E1c", (8, 128), BF16)
    iota15 = inp("iota15", (15, 8), F32)
    cstc = inp("cstc", (128, 2), F32)
    dbg_attin = inp("dbg_attin", (D, TOK), BF16) if DBG_ATT_DRAM else None

    outT = nc.dram_tensor("outT", [D, TOK], F32, kind="ExternalOutput").ap()

    tap_list = []

    def scr(name, shape, dt, shared=False):
        kw = dict(addr_space="Shared") if shared else {}
        ap = nc.dram_tensor(name, list(shape), dt, **kw).ap()
        if name in TAPS:
            tap_list.append((name, shape, dt, ap))
        return ap

    g1_in = scr("g1_in", (D, TOK), BF16)
    g1_out = scr("g1_out", (NCORE * D, TOK), BF16, shared=("g1_out" not in TAPS))
    qkvT_d = scr("qkvT_d", (1536, GTOK), BF16)
    a2a_in = scr("a2a_in", (GTOK, 512), BF16)
    a2a_out = scr("a2a_out", (GTOK, 512), BF16)
    scr_f0 = scr("scr_f0", (D, TOK), F32)
    scr_x1 = scr("scr_x1", (D, TOK), F32)
    dbg_gt = scr("dbg_gt", (64, TOK), BF16) if "dbg_gt" in TAPS else None
    dbg_h = scr("dbg_h", (1024, TOK), BF16) if "dbg_h" in TAPS else None
    dbg_hg = scr("dbg_hg", (1024, TOK), BF16) if "dbg_hg" in TAPS else None
    dbg_sc = scr("dbg_sc", (64, TOK), F32) if "dbg_sc" in TAPS else None
    dbg_gtok = scr("dbg_gtok", (TOK, 64), F32) if "dbg_gtok" in TAPS else None
    dbg_stok = scr("dbg_stok", (TOK, 64), F32) if "dbg_stok" in TAPS else None
    dbg_cur = scr("dbg_cur", (TOK, 64), F32) if "dbg_cur" in TAPS else None
    dbg_e = scr("dbg_e", (TOK, 64), F32) if "dbg_e" in TAPS else None
    dbg_mx = scr("dbg_mx", (TOK, 8), F32) if "dbg_mx" in TAPS else None
    dbg_at = scr("dbg_at", (D, TOK), F32) if "dbg_at" in TAPS else None
    dbg_attT = scr("dbg_attT", (D, 512), BF16) if "dbg_attT" in TAPS else None

    P = Prog()
    P.extra_sems = ["cc1", "cc2"]

    with ExitStack() as stack:
        E = stack.enter_context
        af_h = E(nc.sbuf_tensor("arena_f", [128, 10240], F32))
        ab_h = E(nc.sbuf_tensor("arena_b", [128, 69632], BF16))
        AFa = Arena(af_h, 10240, "f32")
        ABa = Arena(ab_h, 69632, "bf16")

        id_bf = Tile(E(nc.sbuf_tensor("id_bf", [128, 128], BF16)), 128, 0, 128)
        id_f = Tile(E(nc.sbuf_tensor("id_f", [128, 128], F32)), 128, 0, 128)
        mtri = Tile(E(nc.sbuf_tensor("mtri", [128, 128], BF16)), 128, 0, 128)
        e0_sb = Tile(E(nc.sbuf_tensor("e0_sb", [64, 1024], BF16)), 64, 0, 1024)
        e1_sb = Tile(E(nc.sbuf_tensor("e1_sb", [8, 128], BF16)), 8, 0, 128)
        iota_sb = Tile(E(nc.sbuf_tensor("iota_sb", [15, 8], F32)), 15, 0, 8)
        cst = Tile(E(nc.sbuf_tensor("cst", [128, 2], F32)), 128, 0, 2)
        em_i = Tile(E(nc.sbuf_tensor("em_i", [15, 1], I32)), 15, 0, 1)
        em_f = Tile(E(nc.sbuf_tensor("em_f", [15, 1], F32)), 15, 0, 1)
        onehot = Tile(E(nc.sbuf_tensor("onehot", [15, 8], F32)), 15, 0, 8)
        onehot_bf = Tile(E(nc.sbuf_tensor("onehot_bf", [15, 8], BF16)), 15, 0, 8)
        ones_bf = Tile(E(nc.sbuf_tensor("ones_b", [1, 128], BF16)), 1, 0, 128)
        onec_bf = Tile(E(nc.sbuf_tensor("onec_b", [128, 1], BF16)), 128, 0, 1)
        ones_f = Tile(E(nc.sbuf_tensor("ones_f", [1, 128], F32)), 1, 0, 128)
        bqkv_sb = Tile(E(nc.sbuf_tensor("bqkv", [128, 12], F32)), 128, 0, 12)
        bo0_sb = Tile(E(nc.sbuf_tensor("bo0", [128, 32], F32)), 128, 0, 32)
        bq1_sb = Tile(E(nc.sbuf_tensor("bq1", [128, 8], F32)), 128, 0, 8)
        bb1_sb = Tile(E(nc.sbuf_tensor("bb1", [128, 32], F32)), 128, 0, 32)

        z1_f = Tile(E(nc.sbuf_tensor("z1f", [128, 8], F32)), 128, 0, 8)
        z1_bf = Tile(E(nc.sbuf_tensor("z1b", [128, 8], BF16)), 128, 0, 8)
        m1_bf = Tile(E(nc.sbuf_tensor("m1b", [128, 1024], BF16)), 128, 0, 1024)
        wr1_f = Tile(E(nc.sbuf_tensor("wr1f", [128, 480], F32)), 128, 0, 480, "wr1f")
        onec_f = Tile(E(nc.sbuf_tensor("onec_f", [128, 1], F32)), 128, 0, 1, "onecf")
        s0b = Tile(E(nc.sbuf_tensor("s0b", [128, 64], F32)), 128, 0, 64, "s0b")
        s1b = Tile(E(nc.sbuf_tensor("s1b", [128, 15], F32)), 128, 0, 15, "s1b")
        rb1b = Tile(E(nc.sbuf_tensor("rb1b", [128, 15], F32)), 128, 0, 15, "rb1b")
        srow = Tile(E(nc.sbuf_tensor("srow", [1, 80], F32)), 1, 0, 80, "srow")
        sc0tok = Tile(E(nc.sbuf_tensor("sc0tok", [128, 8 * 64], F32)), 128, 0, 512, "sc0tok")
        abcol = Tile(E(nc.sbuf_tensor("abcol", [128, 8], F32)), 128, 0, 8, "abcol")
        sc0v = [Tile(sc0tok.h, 128, sc0tok.c0 + i * 64, 64, f"sc0v{i}")
                for i in range(8)]
        lg1tok = Tile(E(nc.sbuf_tensor("lg1tok", [128, 64], F32)), 128, 0, 64, "lg1t")
        lg1v = [Tile(lg1tok.h, 128, lg1tok.c0 + i * 16, 16, f"lg1v{i}")
                for i in range(4)]
        a1_bf = Tile(E(nc.sbuf_tensor("a1b", [128, 4096], BF16)), 128, 0, 4096)
        b1t_bf = Tile(E(nc.sbuf_tensor("b1tb", [128, 4096], BF16)), 128, 0, 4096)

        pA = Tile(E(nc.psum_tensor("pA", [128, 512], F32)), 128, 0, 512)
        pB = Tile(E(nc.psum_tensor("pB", [128, 512], F32)), 128, 0, 512)
        pC = Tile(E(nc.psum_tensor("pC", [128, 512], F32)), 128, 0, 512)
        pD = Tile(E(nc.psum_tensor("pD", [128, 512], F32)), 128, 0, 512)
        pE_ = Tile(E(nc.psum_tensor("pE", [128, 512], F32)), 128, 0, 512)
        pF_ = Tile(E(nc.psum_tensor("pF", [128, 512], F32)), 128, 0, 512)
        pT = Tile(E(nc.psum_tensor("pT", [128, 512], BF16)), 128, 0, 512)

        P.dma("sync", id_bf.ap(), ident_bf, outs=[id_bf])
        P.dma("sync", id_f.ap(), ident_f, outs=[id_f])
        P.dma("sync", mtri.ap(), masktri, outs=[mtri])
        P.dma("sync", e0_sb.ap(), E0c, outs=[e0_sb])
        P.dma("sync", e1_sb.ap(), E1c, outs=[e1_sb])
        P.dma("sync", iota_sb.ap(), iota15, outs=[iota_sb])
        P.dma("sync", cst.ap(), cstc, outs=[cst])
        P.dma("sync", em_i.ap(), emap, outs=[em_i])
        P.dma("sync", bqkv_sb.ap(), b_qkv, outs=[bqkv_sb])
        P.dma("sync", bo0_sb.ap(), b_o0, outs=[bo0_sb])
        P.dma("sync", bq1_sb.ap(), b_q1, outs=[bq1_sb])
        P.dma("sync", bb1_sb.ap(), b_b1, outs=[bb1_sb])

        P.dma("sync", z1_f.ap(), w_z1, outs=[z1_f])
        P.op("dve", lambda e: e.tensor_copy(em_f.ap(), em_i.ap()), ins=[em_i], outs=[em_f])
        P.op("dve", lambda e: e.tensor_scalar(onehot.ap(), iota_sb.ap(), em_f.ap(), None,
                                              ALU.is_equal),
             ins=[iota_sb, em_f], outs=[onehot])
        P.op("dve", lambda e: e.tensor_copy(onehot_bf.ap(), onehot.ap()),
             ins=[onehot], outs=[onehot_bf])
        P.op("dve", lambda e: e.memset(ones_bf.ap(), 1.0), outs=[ones_bf])
        P.op("dve", lambda e: e.memset(onec_bf.ap(), 1.0), outs=[onec_bf])
        P.op("dve", lambda e: e.memset(ones_f.ap(), 1.0), outs=[ones_f])
        P.op("dve", lambda e: e.memset(onec_f.ap(), 1.0), outs=[onec_f])
        P.op("dve", lambda e: e.tensor_copy(z1_bf.ap(), z1_f.ap()), ins=[z1_f], outs=[z1_bf])

        AFa.push()
        wst0 = AFa.alloc(2048, name="ph0stage")

        def stage_convert(dram, nrows, ncols, dst_tile, tag, eng="pool"):
            nch = nrows // 128
            per = max(1, 2048 // ncols)
            s = 0
            rnd = 0
            while s < nch:
                cur = min(per, nch - s)
                src = dram[s * 128:(s + cur) * 128, 0:ncols].rearrange(
                    "(c p) n -> p c n", p=128)
                P.dma("sync", wst0.ap(c=(0, cur * ncols)).rearrange(
                    "p (c n) -> p c n", n=ncols), src, outs=[wst0])
                P.op(eng, lambda e, s=s, cur=cur: e.tensor_copy(
                    dst_tile.ap(c=(s * ncols, (s + cur) * ncols)),
                    wst0.ap(c=(0, cur * ncols))),
                     ins=[wst0], outs=[(dst_tile, "c", s)])
                s += cur
                rnd += 1

        srcw = w_r1.rearrange("(c p) n -> p c n", p=128)
        P.dma("sync", wr1_f.ap().rearrange("p (c n) -> p c n", n=15), srcw,
              outs=[wr1_f])
        def emit_s1(e):
            last = None
            for i in range(NCH):
                last = e.matmul(pD.ap(p=(0, 1), c=(0, 15)), onec_f.ap(),
                                wr1_f.ap(c=(i * 15, (i + 1) * 15)),
                                start=(i == 0), stop=(i == NCH - 1))
            return last
        P.op("pe", emit_s1, ins=[onec_f, wr1_f], outs=[pD])
        P.op("dve", lambda e: e.tensor_copy(srow.ap(p=(0, 1), c=(64, 79)),
                                            pD.ap(p=(0, 1), c=(0, 15))),
             ins=[pD], outs=[(srow, 1)])
        P.op("pe", lambda e: e.matmul(pD.ap(c=(0, 15)), ones_f.ap(),
                                      srow.ap(p=(0, 1), c=(64, 79)),
                                      start=True, stop=True),
             ins=[ones_f, (srow, 1)], outs=[pD])
        P.op("act", lambda e: e.activation(s1b.ap(), pD.ap(c=(0, 15)), AF.Identity),
             ins=[pD], outs=[s1b])
        P.dma("sync", srow.ap(p=(0, 1), c=(64, 79)), b_r1row, outs=[(srow, 1)])
        P.op("pe", lambda e: e.matmul(pC.ap(c=(0, 15)), ones_f.ap(),
                                      srow.ap(p=(0, 1), c=(64, 79)),
                                      start=True, stop=True),
             ins=[ones_f, (srow, 1)], outs=[pC])
        P.op("act", lambda e: e.activation(rb1b.ap(), pC.ap(c=(0, 15)), AF.Identity),
             ins=[pC], outs=[rb1b])
        stage_convert(w_A1, D, 128, a1_bf, "a1")
        for s2 in range(2):
            P.dma("sync", wst0.ap(), w_B1[:, s2 * 2048:(s2 + 1) * 2048], outs=[wst0])
            P.op("pool", lambda e, s2=s2: e.tensor_copy(
                b1t_bf.ap(c=(s2 * 2048, (s2 + 1) * 2048)), wst0.ap()),
                 ins=[wst0], outs=[(b1t_bf, "c", 0)])
        stage_convert(w_M1, 1024, 128, m1_bf, "m1")
        AFa.pop()
        AFa.push()
        wr0_f = AFa.alloc(2048, name="wr0f")
        for s4 in range(4):
            srcw = w_r0[s4 * 1024:(s4 + 1) * 1024, :].rearrange("(c p) n -> p c n", p=128)
            P.dma("sync", wr0_f.ap(c=(s4 * 512, (s4 + 1) * 512)).rearrange(
                "p (c n) -> p c n", n=64), srcw, outs=[(wr0_f, s4)])
        def emit_s0(e):
            last = None
            for i in range(NCH):
                last = e.matmul(pC.ap(p=(0, 1), c=(0, 64)), onec_f.ap(),
                                wr0_f.ap(c=(i * 64, (i + 1) * 64)),
                                start=(i == 0), stop=(i == NCH - 1))
            return last
        P.op("pe", emit_s0, ins=[onec_f] + [(wr0_f, s4) for s4 in range(4)], outs=[pC])
        P.op("dve", lambda e: e.tensor_copy(srow.ap(p=(0, 1), c=(0, 64)),
                                            pC.ap(p=(0, 1), c=(0, 64))),
             ins=[pC], outs=[(srow, 0)])
        P.op("pe", lambda e: e.matmul(pC.ap(c=(0, 64)), ones_f.ap(),
                                      srow.ap(p=(0, 1), c=(0, 64)),
                                      start=True, stop=True),
             ins=[ones_f, (srow, 0)], outs=[pC])
        P.op("act", lambda e: e.activation(s0b.ap(), pC.ap(c=(0, 64)), AF.Identity),
             ins=[pC], outs=[s0b])
        P.barrier()

        def rk(t, nrows, ncols):
            nch = nrows // 128
            per = max(1, 2048 // ncols)
            return [(t, "c", s) for s in range(0, nch, per)]

        A1K = rk(a1_bf, D, 128)
        B1K = rk(b1t_bf, 128, D)
        M1K = rk(m1_bf, 1024, 128)

        def layer_norm(load_chunk, write_xn, stats_done=False, router=None,
                       sub_out=None):
            xa = AFa.allocs(2, 512, name="lnx")
            sqa = ABa.allocs(2, 512, name="lnsq")
            xba = ABa.allocs(2, 512, name="lnxb")
            stat = AFa.alloc(2048, p=1, name="lnstat")
            bca = AFa.alloc(512, name="lnbca")
            bcb = AFa.alloc(512, name="lnbcb")
            tmp = AFa.alloc(512, name="lntmp")
            if not stats_done:
                for i in range(NCH):
                    xt = xa[i % 2]
                    load_chunk(i, xt)
                    sq = sqa[i % 2]
                    xb = xba[i % 2]
                    P.op("act", lambda e, s=sq, x=xt: e.activation(s.ap(), x.ap(), AF.Square),
                         ins=[xt], outs=[sq])
                    P.op("pool", lambda e, b=xb, x=xt: e.tensor_copy(b.ap(), x.ap()),
                         ins=[xt], outs=[xb])

                    def emit_s(e, xb=xb, i=i):
                        return e.matmul(pE_.ap(p=(0, 1)), onec_bf.ap(), xb.ap(),
                                        start=(i == 0), stop=(i == NCH - 1),
                                        skip_group_check=True)
                    P.op("pe", emit_s, ins=[onec_bf, xb],
                         outs=[pE_] if i in (0, NCH - 1) else ())

                    def emit_q(e, sq=sq, i=i):
                        return e.matmul(pF_.ap(p=(0, 1)), onec_bf.ap(), sq.ap(),
                                        start=(i == 0), stop=(i == NCH - 1),
                                        skip_group_check=True)
                    P.op("pe", emit_q, ins=[onec_bf, sq],
                         outs=[pF_] if i in (0, NCH - 1) else ())
            sm, sq_, sv, sr = (0, 512), (512, 1024), (1024, 1536), (1536, 2048)
            P.op("dve", lambda e: e.tensor_scalar_mul(stat.ap(c=sm, p=(0, 1)),
                                                      pE_.ap(p=(0, 1)), 1.0 / D),
                 ins=[pE_], outs=[(stat, "m")])
            P.op("dve", lambda e: e.tensor_scalar_mul(stat.ap(c=sq_, p=(0, 1)),
                                                      pF_.ap(p=(0, 1)), 1.0 / D),
                 ins=[pF_], outs=[(stat, "q")])
            P.op("act", lambda e: e.activation(stat.ap(c=sv, p=(0, 1)),
                                               stat.ap(c=sm, p=(0, 1)), AF.Square),
                 ins=[(stat, "m")], outs=[(stat, "v")])
            P.op("dve", lambda e: e.tensor_sub(stat.ap(c=sv, p=(0, 1)),
                                               stat.ap(c=sq_, p=(0, 1)),
                                               stat.ap(c=sv, p=(0, 1))),
                 ins=[(stat, "q"), (stat, "v")], outs=[(stat, "v")])
            P.op("act", lambda e: e.activation(stat.ap(c=sr, p=(0, 1)),
                                               stat.ap(c=sv, p=(0, 1)), AF.Ln,
                                               bias=cst.ap(p=(0, 1), c=(0, 1))),
                 ins=[(stat, "v"), cst], outs=[(stat, "r")])
            P.op("act", lambda e: e.activation(stat.ap(c=sr, p=(0, 1)),
                                               stat.ap(c=sr, p=(0, 1)), AF.Exp, scale=-0.5),
                 ins=[(stat, "r")], outs=[(stat, "r")])
            P.op("dve", lambda e: e.scalar_tensor_tensor(stat.ap(c=sv, p=(0, 1)),
                                                         stat.ap(c=sm, p=(0, 1)), -1.0,
                                                         stat.ap(c=sr, p=(0, 1)),
                                                         ALU.mult, ALU.mult),
                 ins=[(stat, "m"), (stat, "r")], outs=[(stat, "v")])
            P.op("pe", lambda e: e.matmul(pE_.ap(), ones_f.ap(),
                                          stat.ap(c=sr, p=(0, 1)),
                                          start=True, stop=True),
                 ins=[ones_f, (stat, "r")], outs=[pE_])
            P.op("pe", lambda e: e.matmul(pF_.ap(), ones_f.ap(),
                                          stat.ap(c=sv, p=(0, 1)),
                                          start=True, stop=True),
                 ins=[ones_f, (stat, "v")], outs=[pF_])
            P.op("act", lambda e: e.activation(bca.ap(), pE_.ap(), AF.Identity),
                 ins=[pE_], outs=[bca])
            P.op("act", lambda e: e.activation(bcb.ap(), pF_.ap(), AF.Identity),
                 ins=[pF_], outs=[bcb])
            for i in range(NCH):
                xt = xa[i % 2]
                load_chunk(i, xt)
                if router is not None:
                    wrt, ncols = router
                    def emit_r(e, xt=xt, i=i, wrt=wrt, ncols=ncols):
                        return e.matmul(pC.ap(p=(0, ncols), c=(0, 512)),
                                        wrt.ap(c=(i * ncols, (i + 1) * ncols)),
                                        xt.ap(), start=(i == 0), stop=(i == NCH - 1),
                                        skip_group_check=True)
                    P.op("pe", emit_r, ins=[wrt, xt],
                         outs=[pC] if i in (0, NCH - 1) else ())
                P.op("dve", lambda e, x=xt: e.tensor_mul(tmp.ap(), x.ap(), bca.ap()),
                     ins=[xt, bca], outs=[tmp])
                write_xn(i, tmp, bcb)
            if router is not None:
                wrt, ncols = router
                rT = AFa.alloc(512, p=ncols, name="routT")
                rtok = AFa.alloc(64, name="rtok")
                t1r = AFa.alloc(64, name="t1r")
                P.op("dve", lambda e: e.tensor_copy(rT.ap(), pC.ap(p=(0, ncols))),
                     ins=[pC], outs=[rT])
                sbt = s0b if ncols == 64 else s1b
                for sub in range(4):
                    def emit_ta(e, sub=sub):
                        return e.transpose(pD.ap(c=(0, 1)),
                                           stat.ap(c=(sr[0] + sub * 128,
                                                      sr[0] + sub * 128 + 128), p=(0, 1)),
                                           id_f.ap(p=(0, 1), c=(0, 1)))
                    P.op("pe", emit_ta, ins=[(stat, "r"), id_f], outs=[pD])
                    P.op("act", lambda e, sub=sub: e.activation(
                        abcol.ap(c=(sub, sub + 1)), pD.ap(c=(0, 1)), AF.Identity),
                         ins=[pD], outs=[(abcol, "a", sub)])
                    def emit_tb(e, sub=sub):
                        return e.transpose(pD.ap(c=(1, 2)),
                                           stat.ap(c=(sv[0] + sub * 128,
                                                      sv[0] + sub * 128 + 128), p=(0, 1)),
                                           id_f.ap(p=(0, 1), c=(0, 1)))
                    P.op("pe", emit_tb, ins=[(stat, "v"), id_f], outs=[pD])
                    P.op("act", lambda e, sub=sub: e.activation(
                        abcol.ap(c=(4 + sub, 5 + sub)), pD.ap(c=(1, 2)), AF.Identity),
                         ins=[pD], outs=[(abcol, "b", sub)])
                    def emit_tr(e, sub=sub, ncols=ncols):
                        return e.transpose(pD.ap(c=(128, 128 + ncols), p=(0, 128)),
                                           rT.ap(c=(sub * 128, sub * 128 + 128)),
                                           id_f.ap(p=(0, ncols), c=(0, ncols)))
                    P.op("pe", emit_tr, ins=[rT, id_f], outs=[pD])
                    P.op("act", lambda e, ncols=ncols: e.activation(
                        rtok.ap(c=(0, ncols)), pD.ap(c=(128, 128 + ncols)), AF.Identity),
                         ins=[pD], outs=[rtok])
                    P.op("dve", lambda e, sub=sub, ncols=ncols: e.tensor_scalar(
                        rtok.ap(c=(0, ncols)), rtok.ap(c=(0, ncols)),
                        abcol.ap(c=(sub, sub + 1)), None, ALU.mult),
                         ins=[rtok, (abcol, "a", sub)], outs=[rtok])
                    P.op("dve", lambda e, sub=sub, ncols=ncols, sbt=sbt: e.tensor_scalar(
                        t1r.ap(c=(0, ncols)), sbt.ap(), abcol.ap(c=(4 + sub, 5 + sub)),
                        None, ALU.mult),
                         ins=[sbt, (abcol, "b", sub)], outs=[t1r])
                    P.op("dve", lambda e, ncols=ncols: e.tensor_add(
                        rtok.ap(c=(0, ncols)), rtok.ap(c=(0, ncols)),
                        t1r.ap(c=(0, ncols))),
                         ins=[rtok, t1r], outs=[rtok])
                    sub_out(sub, rtok)

        ABa.push()
        xnb = ABa.allocs(NCH, 1024, name="xnb")
        for t2 in range(2):
            AFa.push()
            ABa.push()

            def loader(i, dst, t2=t2):
                P.dma("sync", dst.ap(), xT[i * 128:(i + 1) * 128, t2 * 512:(t2 + 1) * 512],
                      outs=[dst])

            def write_xn(i, tmp, bcb, t2=t2):
                xn = xnb[i]
                P.op("dve", lambda e, xn=xn, tmp=tmp, bcb=bcb, t2=t2: e.tensor_add(
                    xn.ap(c=(t2 * 512, t2 * 512 + 512)), tmp.ap(), bcb.ap()),
                     ins=[tmp, bcb], outs=[(xn, t2)])
                P.dma("pool", g1_in[i * 128:(i + 1) * 128, t2 * 512:(t2 + 1) * 512],
                      xn.ap(c=(t2 * 512, t2 * 512 + 512)), ins=[(xn, t2)],
                      outs=[("g1in", i, t2)])
            def sub_out0(sub, rtok, t2=t2):
                idx = t2 * 4 + sub
                P.op("dve", lambda e, v=sc0v[idx], rtok=rtok: e.tensor_copy(
                    v.ap(), rtok.ap(c=(0, 64))), ins=[rtok], outs=[sc0v[idx]])
            layer_norm(loader, write_xn, router=(wr0_f, 64), sub_out=sub_out0)
            AFa.pop()
            ABa.pop()
            P.barrier()

        def emit_cc1(e):
            return e.collective_compute(
                "AllGather", ALU.bypass, replica_groups=[list(range(NCORE))],
                ins=[g1_in], outs=[g1_out])
        P.op("pool", emit_cc1,
             ins=[("g1in", i, t2) for i in range(NCH) for t2 in range(2)],
             outs=[("g1out",)], sem="cc1", inc=1)

        ABa.push()
        gt_sb = ABa.alloc(1024, p=64, name="GT")
        hg_sb = ABa.allocs(8, 1024, name="hg")
        AFa.push()
        sc_cur = AFa.alloc(64, name="sccur")
        mxc = AFa.alloc(8, name="mxc")
        gat = AFa.alloc(64, name="gat")
        for t2 in range(2):
            for sub in range(4):
                stt = sc0v[t2 * 4 + sub]
                P.op("dve", lambda e, stt=stt: e.tensor_copy(sc_cur.ap(), stt.ap()),
                     ins=[stt], outs=[sc_cur])
                if dbg_stok is not None:
                    r0 = t2 * 512 + sub * 128
                    P.dma("sync", dbg_stok[r0:r0 + 128, :], stt.ap(),
                          ins=[stt], outs=[("tap", "dbg_stok", t2, sub)])
                for it in range(16):
                    P.op("dve", lambda e: e.reduce_max(mxc.ap(c=(0, 1)), sc_cur.ap(), axis=AX),
                         ins=[sc_cur], outs=[(mxc, "mx")])
                    if it == 0:
                        P.op("dve", lambda e: e.tensor_scalar_mul(mxc.ap(c=(1, 2)),
                                                                  mxc.ap(c=(0, 1)), -1.0),
                             ins=[(mxc, "mx")], outs=[(mxc, "nm0")])
                    if it < 15:
                        P.op("dve", lambda e: e.tensor_scalar(gat.ap(), sc_cur.ap(),
                                                              mxc.ap(c=(0, 1)), None,
                                                              ALU.is_ge),
                             ins=[sc_cur, (mxc, "mx")], outs=[gat])
                        P.op("dve", lambda e: e.scalar_tensor_tensor(
                            sc_cur.ap(), gat.ap(), -1e30, sc_cur.ap(), ALU.mult, ALU.add),
                             ins=[gat, sc_cur], outs=[sc_cur])
                P.op("act", lambda e, stt=stt: e.activation(gat.ap(), stt.ap(), AF.Exp,
                                                            bias=mxc.ap(c=(1, 2))),
                     ins=[stt, (mxc, "nm0")], outs=[gat])
                if dbg_e is not None:
                    r0 = t2 * 512 + sub * 128
                    P.dma("sync", dbg_e[r0:r0 + 128, :], gat.ap(),
                          ins=[gat], outs=[("tap", "dbg_e", t2, sub)])
                P.op("dve", lambda e, stt=stt: e.tensor_scalar(sc_cur.ap(), stt.ap(),
                                                               mxc.ap(c=(0, 1)), None,
                                                               ALU.is_ge),
                     ins=[stt, (mxc, "mx")], outs=[sc_cur])
                if dbg_cur is not None:
                    r0 = t2 * 512 + sub * 128
                    P.dma("sync", dbg_cur[r0:r0 + 128, :], sc_cur.ap(),
                          ins=[sc_cur], outs=[("tap", "dbg_cur", t2, sub)])
                P.op("dve", lambda e: e.tensor_mul(gat.ap(), gat.ap(), sc_cur.ap()),
                     ins=[gat, sc_cur], outs=[gat])
                P.op("dve", lambda e: e.reduce_sum(mxc.ap(c=(2, 3)), gat.ap(), axis=AX),
                     ins=[gat], outs=[(mxc, "s")])
                P.op("dve", lambda e: e.reciprocal(mxc.ap(c=(3, 4)), mxc.ap(c=(2, 3))),
                     ins=[(mxc, "s")], outs=[(mxc, "rs")])
                P.op("dve", lambda e: e.tensor_scalar(gat.ap(), gat.ap(), mxc.ap(c=(3, 4)),
                                                      None, ALU.mult),
                     ins=[gat, (mxc, "rs")], outs=[gat])
                if dbg_mx is not None:
                    r0 = t2 * 512 + sub * 128
                    P.dma("sync", dbg_mx[r0:r0 + 128, :], mxc.ap(),
                          ins=[(mxc, "mx"), (mxc, "nm0"), (mxc, "s"), (mxc, "rs")],
                          outs=[("tap", "dbg_mx", t2, sub)])

                if dbg_gtok is not None:
                    r0 = t2 * 512 + sub * 128
                    P.dma("sync", dbg_gtok[r0:r0 + 128, :], gat.ap(),
                          ins=[gat], outs=[("tap", "dbg_gtok", t2, sub)])

                def emit_trg(e):
                    return e.transpose(pB.ap(p=(0, 64), c=(64, 192)), gat.ap(), id_f.ap())
                P.op("pe", emit_trg, ins=[gat, id_f], outs=[pB])
                P.op("act", lambda e, t2=t2, sub=sub: e.activation(
                    gt_sb.ap(c=(t2 * 512 + sub * 128, t2 * 512 + sub * 128 + 128)),
                    pB.ap(c=(64, 192), p=(0, 64)), AF.Identity),
                     ins=[pB], outs=[(gt_sb, t2, sub)])
        if dbg_gt is not None:
            P.dma("sync", dbg_gt, gt_sb.ap(),
                  ins=[(gt_sb, t2_, s_) for t2_ in range(2) for s_ in range(4)],
                  outs=[("tap", "dbg_gt")])
        ABa.push()
        a0st = ABa.allocs(2, NCH * 128, name="a0st")
        a0f = AFa.alloc(1024, name="a0fst")
        hb_t = ABa.alloc(512, name="hbt")
        for j in range(8):
            stt = a0st[j % 2]
            for s in range(4):
                src = w_A0[s * 1024:(s + 1) * 1024, j * 128:(j + 1) * 128].rearrange(
                    "(c p) n -> p c n", p=128)
                P.dma("sync", a0f.ap().rearrange("p (c n) -> p c n", n=128), src,
                      outs=[a0f])
                P.op("dve", lambda e, stt=stt, s=s: e.tensor_copy(
                    stt.ap(c=(s * 1024, (s + 1) * 1024)), a0f.ap()),
                     ins=[a0f], outs=[(stt, "c", s)])
            for t2 in range(2):
                pp = pA if t2 == 0 else pB
                pairs = [(stt.ap(c=(i * 128, i * 128 + 128)),
                          xnb[i].ap(c=(t2 * 512, t2 * 512 + 512))) for i in range(NCH)]
                _mmgroup(P, pp.ap(), pairs,
                         ins=[(stt, "c", s) for s in range(4)] +
                             [(xnb[i], t2) for i in range(NCH)],
                         outs=[pp])
                P.op("act", lambda e, pp=pp: e.activation(hb_t.ap(), pp.ap(), AF.Identity),
                     ins=[pp], outs=[hb_t])
                if dbg_h is not None:
                    P.dma("sync", dbg_h[j * 128:(j + 1) * 128,
                                        t2 * 512:(t2 + 1) * 512], hb_t.ap(),
                          ins=[hb_t], outs=[("tap", "dbg_h", j, t2)])

                def emit_ge(e, j=j, t2=t2):
                    return e.matmul(pC.ap(), e0_sb.ap(c=(j * 128, j * 128 + 128)),
                                    gt_sb.ap(c=(t2 * 512, t2 * 512 + 512)),
                                    start=True, stop=True)
                P.op("pe", emit_ge,
                     ins=[e0_sb] + [(gt_sb, t2, s) for s in range(4)], outs=[pC])
                P.op("dve", lambda e, j=j, t2=t2: e.tensor_tensor(
                    hg_sb[j].ap(c=(t2 * 512, t2 * 512 + 512)), hb_t.ap(), pC.ap(), ALU.mult),
                     ins=[hb_t, pC], outs=[(hg_sb[j], t2)])
                if dbg_hg is not None:
                    P.dma("sync", dbg_hg[j * 128:(j + 1) * 128,
                                         t2 * 512:(t2 + 1) * 512],
                          hg_sb[j].ap(c=(t2 * 512, t2 * 512 + 512)),
                          ins=[(hg_sb[j], t2)], outs=[("tap", "dbg_hg", j, t2)])
        b0st = ABa.allocs(2, 8 * 128, name="b0st")
        b0f = AFa.alloc(1024, name="b0fst")
        f0_t = AFa.allocs(2, 512, name="f0t")
        for oc in range(32):
            stt = b0st[oc % 2]
            src = w_B0[:, oc * 128:(oc + 1) * 128].rearrange("(c p) n -> p c n", p=128)
            P.dma("sync", b0f.ap().rearrange("p (c n) -> p c n", n=128), src,
                  outs=[b0f])
            P.op("dve", lambda e, stt=stt: e.tensor_copy(stt.ap(), b0f.ap()),
                 ins=[b0f], outs=[stt])
            for t2 in range(2):
                pairs = [(stt.ap(c=(j * 128, j * 128 + 128)),
                          hg_sb[j].ap(c=(t2 * 512, t2 * 512 + 512))) for j in range(8)]
                _mmgroup(P, pD.ap(), pairs,
                         ins=[stt] + [(hg_sb[j], t2) for j in range(8)], outs=[pD])
                ft = f0_t[t2]
                P.op("act", lambda e, ft=ft: e.activation(ft.ap(), pD.ap(), AF.Identity),
                     ins=[pD], outs=[ft])
                P.dma("pool", scr_f0[oc * 128:(oc + 1) * 128, t2 * 512:(t2 + 1) * 512],
                      ft.ap(), ins=[ft], outs=[("f0d", oc, t2)])
        AFa.pop()
        AFa.pop()
        ABa.pop()
        ABa.pop()
        ABa.pop()
        P.barrier()

        ABa.push()
        wq_st = ABa.allocs(6, NCH * 128, name="wqkv")
        xg = [ABa.allocs(NCH, 512, name=f"xg{k}_") for k in range(2)]
        qkv_ev = ABa.allocs(2, 512, name="qkvev")
        AFa.push()
        wstg = AFa.alloc(1024, name="wstg")
        for grp in range(2):
            for jj in range(6):
                ob = grp * 6 + jj
                stt = wq_st[jj]
                for s in range(4):
                    src = w_qkv[s * 1024:(s + 1) * 1024, ob * 128:(ob + 1) * 128].rearrange(
                        "(c p) n -> p c n", p=128)
                    P.dma("sync", wstg.ap().rearrange("p (c n) -> p c n", n=128), src,
                          outs=[wstg])
                    P.op("dve", lambda e, stt=stt, s=s: e.tensor_copy(
                        stt.ap(c=(s * 1024, (s + 1) * 1024)), wstg.ap()),
                         ins=[wstg], outs=[(stt, "c", s)])
            for tt in range(16):
                r, half = tt // 2, tt % 2
                xgt = xg[tt % 2]
                for s in range(4):
                    src = g1_out[r * D + s * 1024: r * D + (s + 1) * 1024,
                                 half * 512:(half + 1) * 512].rearrange(
                        "(c p) n -> p c n", p=128)

                    def emit_d(e, src=src, xgt=xgt, s=s):
                        t0 = xgt[s * 8]
                        dst = t0.h[:, t0.c0:t0.c0 + 8 * 512].rearrange(
                            "p (c n) -> p c n", n=512)
                        return e.dma_start(out=dst, in_=src)
                    P.op("sync", emit_d, ins=[("g1out",)],
                         outs=[xgt[s * 8 + c] for c in range(8)], inc=16)
                for jj in range(6):
                    ob = grp * 6 + jj
                    stt = wq_st[jj]
                    pp = pA if jj % 2 == 0 else pB
                    pairs = [(stt.ap(c=(i * 128, i * 128 + 128)), xgt[i].ap())
                             for i in range(NCH)]
                    _mmgroup(P, pp.ap(), pairs,
                             ins=[(stt, "c", s) for s in range(4)] + list(xgt), outs=[pp])
                    ev = qkv_ev[jj % 2]
                    P.op("act", lambda e, ev=ev, pp=pp, ob=ob: e.activation(
                        ev.ap(), pp.ap(), AF.Identity, bias=bqkv_sb.ap(c=(ob, ob + 1))),
                         ins=[pp, bqkv_sb], outs=[ev])
                    P.dma("pool", qkvT_d[ob * 128:(ob + 1) * 128, tt * 512:(tt + 1) * 512],
                          ev.ap(), ins=[ev], outs=[("qkvd", ob, tt)])
        AFa.pop()
        ABa.pop()
        P.barrier()

        ABa.push()
        qT = ABa.allocs(2, 2048, name="qT")
        kT = ABa.allocs(2, 2048, name="kT")
        vT = ABa.allocs(2, 2048, name="vT")
        vext = ABa.allocs(2, 16 * 129, name="vext")
        pT_sb = ABa.allocs(16, 512, name="pTsb")
        av_bf = ABa.allocs(2, 128, name="avbf")
        AFa.push()
        rz = AFa.allocs(2, 1, name="rz")
        for vx in vext:
            P.op("dve", lambda e, vx=vx: e.memset(vx.ap(), 1.0), outs=[vx])
        for sh in range(16):
            s, h = sh // 4, sh % 4
            par = sh % 2
            qt_t, kt_t, vt_t, vx = qT[par], kT[par], vT[par], vext[par]
            P.dma("sync", qt_t.ap(),
                  qkvT_d[h * 128:(h + 1) * 128, s * 2048:(s + 1) * 2048],
                  ins=[("qkvd", h, 4 * s + k) for k in range(4)], outs=[qt_t])
            P.dma("sync", kt_t.ap(),
                  qkvT_d[(4 + h) * 128:(5 + h) * 128, s * 2048:(s + 1) * 2048],
                  ins=[("qkvd", 4 + h, 4 * s + k) for k in range(4)], outs=[kt_t])
            P.dma("sync", vt_t.ap(),
                  qkvT_d[(8 + h) * 128:(9 + h) * 128, s * 2048:(s + 1) * 2048],
                  ins=[("qkvd", 8 + h, 4 * s + k) for k in range(4)], outs=[vt_t])
            for kc in range(16):
                def emit_vtr(e, vt_t=vt_t, kc=kc):
                    return e.transpose(pT.ap(c=(0, 128)),
                                       vt_t.ap(c=(kc * 128, (kc + 1) * 128)), id_bf.ap())
                P.op("pe", emit_vtr, ins=[vt_t, id_bf], outs=[pT])
                P.op("act", lambda e, vx=vx, kc=kc: e.activation(
                    vx.ap(c=(kc * 129, kc * 129 + 128)), pT.ap(c=(0, 128)), AF.Identity),
                     ins=[pT], outs=[(vx, kc)])
            for qg in range(4):
                for kc in range(4 * qg + 4):
                    pp = pA if kc % 2 == 0 else pB
                    def emit_sc(e, kt_t=kt_t, qt_t=qt_t, kc=kc, qg=qg, pp=pp):
                        return e.matmul(pp.ap(), kt_t.ap(c=(kc * 128, (kc + 1) * 128)),
                                        qt_t.ap(c=(qg * 512, (qg + 1) * 512)),
                                        start=True, stop=True)
                    P.op("pe", emit_sc, ins=[kt_t, qt_t], outs=[pp])
                    pt = pT_sb[kc]
                    P.op("act", lambda e, pt=pt, pp=pp: e.activation(
                        pt.ap(), pp.ap(), AF.Exp, bias=cst.ap(c=(1, 2)), scale=RSQ),
                         ins=[pp, cst], outs=[pt])
                    if kc >= 4 * qg:
                        rr = kc - 4 * qg
                        P.op("dve", lambda e, pt=pt, rr=rr: e.tensor_tensor(
                            pt.ap(c=(rr * 128, rr * 128 + 128)),
                            pt.ap(c=(rr * 128, rr * 128 + 128)), mtri.ap(), ALU.mult),
                             ins=[pt, mtri], outs=[pt])
                for sub in range(4):
                    qt_i = qg * 4 + sub
                    pp = pC if sub % 2 == 0 else pD
                    pairs = [(pT_sb[kc].ap(c=(sub * 128, sub * 128 + 128)),
                              vx.ap(c=(kc * 129, kc * 129 + 129)))
                             for kc in range(qt_i + 1)]
                    _mmgroup(P, pp.ap(c=(0, 129)), pairs,
                             ins=[pT_sb[kc] for kc in range(qt_i + 1)] +
                                 [(vx, kc) for kc in range(qt_i + 1)] + [vx],
                             outs=[pp])
                    rzt = rz[sub % 2]
                    P.op("dve", lambda e, rzt=rzt, pp=pp: e.reciprocal(
                        rzt.ap(), pp.ap(c=(128, 129))), ins=[pp], outs=[rzt])
                    ab = av_bf[sub % 2]
                    P.op("act", lambda e, ab=ab, pp=pp, rzt=rzt: e.activation(
                        ab.ap(), pp.ap(c=(0, 128)), AF.Identity, scale=rzt.ap()),
                         ins=[pp, rzt], outs=[ab])
                    P.dma("pool",
                          a2a_in[s * 2048 + qt_i * 128: s * 2048 + (qt_i + 1) * 128,
                                 h * 128:(h + 1) * 128],
                          ab.ap(), ins=[ab], outs=[("a2ain", sh, qt_i)])
        AFa.pop()
        ABa.pop()

        def emit_cc2(e):
            with allow_ici_all_to_all():
                return e.collective_compute(
                    "AllToAll", ALU.bypass, replica_groups=[list(range(NCORE))],
                    ins=[a2a_in], outs=[a2a_out])
        P.op("pool", emit_cc2,
             ins=[("a2ain", sh, q) for sh in range(16) for q in range(16)],
             outs=[("a2aout",)], sem="cc2", inc=1)
        P.barrier()

        xn1 = ABa.allocs(NCH, 1024, name="xn1")
        ret_sb = ABa.allocs(8, 1024, name="ret")
        hg1_sb = ABa.alloc(1024, name="hg1")
        wsA = ABa.allocs(2, 16 * 128, name="wsA")
        wsB = ABa.allocs(2, 16 * 128, name="wsB")
        AFa.push()
        wfst = AFa.alloc(1024, name="wfst")
        to_t = AFa.allocs(2, 512, name="tot")
        xf_t = AFa.allocs(2, 512, name="xft")
        x1w = AFa.allocs(2, 512, name="x1w")

        def load_half(dram, oc, half, stt, tag):
            for s in range(2):
                r0 = half * 2048 + s * 1024
                src = dram[r0:r0 + 1024, oc * 128:(oc + 1) * 128].rearrange(
                    "(c p) n -> p c n", p=128)
                P.dma("sync", wfst.ap().rearrange("p (c n) -> p c n", n=128)
                      if False else wfst.ap(c=(0, 1024)).rearrange("p (c n) -> p c n", n=128),
                      src, outs=[wfst])
                P.op("dve", lambda e, stt=stt, s=s: e.tensor_copy(
                    stt.ap(c=(s * 1024, (s + 1) * 1024)), wfst.ap(c=(0, 1024))),
                     ins=[wfst], outs=[(stt, "c", s)])

        def proj_mm(P, dram, oc, bufs, tag, rhs_of_chunk, pp, final_stop=True):
            for half in range(2):
                stt = bufs[half]
                load_half(dram, oc, half, stt, tag)
                pairs = [(stt.ap(c=((i - half * 16) * 128, (i - half * 16) * 128 + 128)),
                          rhs_of_chunk(i)[0]) for i in range(half * 16, half * 16 + 16)]
                ins = [(stt, "c", s) for s in range(2)] + \
                      [rhs_of_chunk(i)[1] for i in range(half * 16, half * 16 + 16)]
                _mmgroup(P, pp.ap(), pairs, ins=ins, outs=[pp],
                         first=(half == 0), last=(final_stop and half == 1))

        for t2 in range(2):
            ABa.push()
            attT = ABa.allocs(NCH, 512, name="attT")
            a2t = ABa.allocs(2, 512, name="a2t")
            sqa6 = ABa.allocs(2, 512, name="sq6")
            xba6 = ABa.allocs(2, 512, name="xb6")
            if DBG_ATT_DRAM:
                for i_ in range(NCH):
                    P.dma("sync", attT[i_].ap(),
                          dbg_attin[i_ * 128:(i_ + 1) * 128,
                                    t2 * 512:(t2 + 1) * 512],
                          outs=[attT[i_]])
            else:
              for r in range(NCORE):
                for ts in range(4):
                    a2 = a2t[ts % 2]
                    row0 = r * 1024 + t2 * 512 + ts * 128
                    P.dma("sync", a2.ap(), a2a_out[row0:row0 + 128, :],
                          ins=[("a2aout",)], outs=[a2])
                    for rs in range(4):
                        def emit_at(e, a2=a2, rs=rs):
                            return e.transpose(pT.ap(c=(rs * 128, rs * 128 + 128)),
                                               a2.ap(c=(rs * 128, rs * 128 + 128)),
                                               id_bf.ap())
                        P.op("pe", emit_at, ins=[a2, id_bf], outs=[pT])
                        ch = 4 * r + rs
                        P.op("act", lambda e, ch=ch, ts=ts, rs=rs: e.activation(
                            attT[ch].ap(c=(ts * 128, ts * 128 + 128)),
                            pT.ap(c=(rs * 128, rs * 128 + 128)), AF.Identity),
                             ins=[pT], outs=[attT[ch]])
            if dbg_attT is not None and t2 == 0:
                for i_ in range(NCH):
                    P.dma("sync", dbg_attT[i_ * 128:(i_ + 1) * 128, :], attT[i_].ap(),
                          ins=[attT[i_]], outs=[("tap", "dbg_attT", i_)])
            for oc in range(32):
                pp = pA if oc % 2 == 0 else pB
                proj_mm(P, w_o0, oc, wsA, "wo0" + str(t2),
                        lambda i: (attT[i].ap(), attT[i]), pp)
                tt = to_t[oc % 2]
                P.op("act", lambda e, tt=tt, pp=pp, oc=oc: e.activation(
                    tt.ap(), pp.ap(), AF.Identity, bias=bo0_sb.ap(c=(oc, oc + 1))),
                     ins=[pp, bo0_sb], outs=[tt])
                if dbg_at is not None:
                    P.dma("sync", dbg_at[oc * 128:(oc + 1) * 128,
                                         t2 * 512:(t2 + 1) * 512], tt.ap(),
                          ins=[tt], outs=[("tap", "dbg_at", oc, t2)])
                if DBG_SKIP_X1:
                    continue
                xf = xf_t[oc % 2]
                P.dma("sync", xf.ap(),
                      xT[oc * 128:(oc + 1) * 128, t2 * 512:(t2 + 1) * 512], outs=[xf])
                x1t = x1w[oc % 2]
                P.op("dve", lambda e, x1t=x1t, tt=tt, xf=xf: e.tensor_add(
                    x1t.ap(), tt.ap(), xf.ap()), ins=[tt, xf], outs=[x1t])
                P.dma("sync", xf.ap(),
                      scr_f0[oc * 128:(oc + 1) * 128, t2 * 512:(t2 + 1) * 512],
                      ins=[("f0d", oc, t2)], outs=[xf])
                P.op("dve", lambda e, x1t=x1t, xf=xf: e.tensor_add(
                    x1t.ap(), x1t.ap(), xf.ap()), ins=[x1t, xf], outs=[x1t])
                P.dma("pool", scr_x1[oc * 128:(oc + 1) * 128, t2 * 512:(t2 + 1) * 512],
                      x1t.ap(), ins=[x1t], outs=[("x1d", oc, t2)])
                sq = sqa6[oc % 2]
                xb = xba6[oc % 2]
                P.op("act", lambda e, sq=sq, x1t=x1t: e.activation(sq.ap(), x1t.ap(),
                                                                   AF.Square),
                     ins=[x1t], outs=[sq])
                P.op("pool", lambda e, xb=xb, x1t=x1t: e.tensor_copy(xb.ap(), x1t.ap()),
                     ins=[x1t], outs=[xb])

                def emit_s(e, xb=xb, oc=oc):
                    return e.matmul(pE_.ap(p=(0, 1)), onec_bf.ap(), xb.ap(),
                                    start=(oc == 0), stop=(oc == 31),
                                    skip_group_check=True)
                P.op("pe", emit_s, ins=[onec_bf, xb],
                     outs=[pE_] if oc in (0, 31) else ())

                def emit_q(e, sq=sq, oc=oc):
                    return e.matmul(pF_.ap(p=(0, 1)), onec_bf.ap(), sq.ap(),
                                    start=(oc == 0), stop=(oc == 31),
                                    skip_group_check=True)
                P.op("pe", emit_q, ins=[onec_bf, sq],
                     outs=[pF_] if oc in (0, 31) else ())
            ABa.pop()
            P.barrier()
            AFa.push()
            ABa.push()

            def loader1(i, dst, t2=t2):
                P.dma("sync", dst.ap(),
                      scr_x1[i * 128:(i + 1) * 128, t2 * 512:(t2 + 1) * 512],
                      ins=[("x1d", i, t2)], outs=[dst])

            def write_xn1(i, tmp, bcb, t2=t2):
                xn = xn1[i]
                P.op("dve", lambda e, xn=xn, tmp=tmp, bcb=bcb, t2=t2: e.tensor_add(
                    xn.ap(c=(t2 * 512, t2 * 512 + 512)), tmp.ap(), bcb.ap()),
                     ins=[tmp, bcb], outs=[(xn, t2)])
            def sub_out1(sub, rtok):
                P.op("dve", lambda e, v=lg1v[sub], rtok=rtok: e.tensor_add(
                    v.ap(c=(0, 15)), rtok.ap(c=(0, 15)), rb1b.ap()),
                     ins=[rtok, rb1b], outs=[lg1v[sub]])
            layer_norm(loader1, write_xn1, stats_done=True,
                       router=(wr1_f, 15), sub_out=sub_out1)
            AFa.pop()
            ABa.pop()
            P.barrier()
            AFa.push()
            ABa.push()
            sigq = ABa.allocs(8, 512, name="sigq")
            qf32 = AFa.alloc(512, name="qf32")
            tmn = AFa.alloc(512, name="tmn")
            tex = AFa.alloc(512, name="tex")
            for ob in range(8):
                pp = pA if ob % 2 == 0 else pB
                proj_mm(P, w_q1, ob, wsA, "wq1" + str(t2),
                        lambda i: (xn1[i].ap(c=(t2 * 512, t2 * 512 + 512)), (xn1[i], t2)),
                        pp)
                P.op("act", lambda e, pp=pp, ob=ob: e.activation(
                    qf32.ap(), pp.ap(), AF.Identity, bias=bq1_sb.ap(c=(ob, ob + 1))),
                     ins=[pp, bq1_sb], outs=[qf32])
                P.op("dve", lambda e: e.tensor_scalar_min(tmn.ap(), qf32.ap(), 0.0),
                     ins=[qf32], outs=[tmn])
                P.op("act", lambda e: e.activation(tex.ap(), tmn.ap(), AF.Exp),
                     ins=[tmn], outs=[tex])
                P.op("dve", lambda e: e.tensor_scalar_max(tmn.ap(), qf32.ap(), 0.0),
                     ins=[qf32], outs=[tmn])
                P.op("dve", lambda e, sg=sigq[ob]: e.tensor_add(sg.ap(), tex.ap(), tmn.ap()),
                     ins=[tex, tmn], outs=[sigq[ob]])
            nrow = AFa.alloc(512, name="nrow")
            drow = AFa.alloc(512, p=1, name="drow")
            for hh in range(8):
                P.op("pe", lambda e, hh=hh: e.matmul(
                    pC.ap(), m1_bf.ap(c=(hh * 128, (hh + 1) * 128)), sigq[hh].ap(),
                    start=True, stop=True),
                     ins=M1K + [sigq[hh]], outs=[pC])
                P.op("pe", lambda e, hh=hh: e.matmul(
                    pD.ap(p=(0, 1)), z1_bf.ap(c=(hh, hh + 1)), sigq[hh].ap(),
                    start=True, stop=True),
                     ins=[z1_bf, sigq[hh]], outs=[pD])
                P.op("dve", lambda e: e.tensor_scalar_add(drow.ap(p=(0, 1)),
                                                          pD.ap(p=(0, 1)), DENEPS),
                     ins=[pD], outs=[drow])
                P.op("dve", lambda e: e.reciprocal(drow.ap(p=(0, 1)), drow.ap(p=(0, 1))),
                     ins=[drow], outs=[drow])
                P.op("pe", lambda e: e.matmul(pD.ap(), ones_f.ap(),
                                              drow.ap(p=(0, 1)), start=True, stop=True),
                     ins=[ones_f, drow], outs=[pD])
                P.op("act", lambda e: e.activation(nrow.ap(), pC.ap(), AF.Identity),
                     ins=[pC], outs=[nrow])
                P.op("dve", lambda e, hh=hh, t2=t2: e.tensor_tensor(
                    ret_sb[hh].ap(c=(t2 * 512, t2 * 512 + 512)), nrow.ap(), pD.ap(),
                    ALU.mult),
                     ins=[nrow, pD], outs=[(ret_sb[hh], t2)])
            e15 = AFa.alloc(16, name="e15")
            cur15 = AFa.alloc(16, name="cur15")
            m15 = AFa.alloc(8, name="m15")
            mp = AFa.alloc(16, name="mp")
            mpT = ABa.alloc(512, p=15, name="mpT")
            w8 = ABa.alloc(512, p=8, name="w8")
            for sub in range(4):
                lg = lg1v[sub]
                P.op("dve", lambda e, lg=lg: e.reduce_max(m15.ap(c=(0, 1)),
                                                          lg.ap(c=(0, 15)),
                                                          axis=AX, negate=True),
                     ins=[lg], outs=[(m15, "nm")])
                P.op("act", lambda e, lg=lg: e.activation(e15.ap(c=(0, 15)),
                                                          lg.ap(c=(0, 15)), AF.Exp,
                                                          bias=m15.ap(c=(0, 1))),
                     ins=[lg, (m15, "nm")], outs=[e15])
                P.op("dve", lambda e: e.reduce_sum(m15.ap(c=(1, 2)), e15.ap(c=(0, 15)),
                                                   axis=AX),
                     ins=[e15], outs=[(m15, "s")])
                P.op("dve", lambda e: e.reciprocal(m15.ap(c=(2, 3)), m15.ap(c=(1, 2))),
                     ins=[(m15, "s")], outs=[(m15, "rs")])
                P.op("dve", lambda e: e.tensor_copy(cur15.ap(c=(0, 15)), e15.ap(c=(0, 15))),
                     ins=[e15], outs=[cur15])
                for it in range(8):
                    P.op("dve", lambda e: e.reduce_max(m15.ap(c=(3, 4)),
                                                       cur15.ap(c=(0, 15)), axis=AX),
                         ins=[cur15], outs=[(m15, "mx")])
                    if it < 7:
                        P.op("dve", lambda e: e.tensor_scalar(mp.ap(c=(0, 15)),
                                                              cur15.ap(c=(0, 15)),
                                                              m15.ap(c=(3, 4)), None,
                                                              ALU.is_ge),
                             ins=[cur15, (m15, "mx")], outs=[mp])
                        P.op("dve", lambda e: e.scalar_tensor_tensor(
                            cur15.ap(c=(0, 15)), mp.ap(c=(0, 15)), -1e30,
                            cur15.ap(c=(0, 15)), ALU.mult, ALU.add),
                             ins=[mp, cur15], outs=[cur15])
                P.op("dve", lambda e: e.tensor_scalar(mp.ap(c=(0, 15)), e15.ap(c=(0, 15)),
                                                      m15.ap(c=(3, 4)), None, ALU.is_ge),
                     ins=[e15, (m15, "mx")], outs=[mp])
                P.op("dve", lambda e: e.scalar_tensor_tensor(
                    mp.ap(c=(0, 15)), e15.ap(c=(0, 15)), m15.ap(c=(2, 3)),
                    mp.ap(c=(0, 15)), ALU.mult, ALU.mult),
                     ins=[e15, (m15, "rs"), mp], outs=[mp])

                def emit_trm(e, sub=sub):
                    return e.transpose(pB.ap(p=(0, 15), c=(128, 256)), mp.ap(c=(0, 15)),
                                       id_f.ap())
                P.op("pe", emit_trm, ins=[mp, id_f], outs=[pB])
                P.op("act", lambda e, sub=sub: e.activation(
                    mpT.ap(c=(sub * 128, sub * 128 + 128)),
                    pB.ap(c=(128, 256), p=(0, 15)), AF.Identity),
                     ins=[pB], outs=[(mpT, sub)])
            P.op("pe", lambda e: e.matmul(pC.ap(p=(0, 8)), onehot_bf.ap(), mpT.ap(),
                                          start=True, stop=True),
                 ins=[onehot_bf] + [(mpT, s) for s in range(4)], outs=[pC])
            P.op("act", lambda e: e.activation(w8.ap(), pC.ap(p=(0, 8)), AF.Identity),
                 ins=[pC], outs=[w8])
            P.op("pe", lambda e: e.matmul(pC.ap(), e1_sb.ap(), w8.ap(),
                                          start=True, stop=True),
                 ins=[e1_sb, w8], outs=[pC])
            pairs = [(a1_bf.ap(c=(i * 128, i * 128 + 128)),
                      xn1[i].ap(c=(t2 * 512, t2 * 512 + 512))) for i in range(NCH)]
            _mmgroup(P, pD.ap(), pairs,
                     ins=A1K + [(xn1[i], t2) for i in range(NCH)], outs=[pD])
            h1b = AFa.alloc(512, name="h1b")
            P.op("act", lambda e: e.activation(h1b.ap(), pD.ap(), AF.Identity),
                 ins=[pD], outs=[h1b])
            P.op("dve", lambda e, t2=t2: e.tensor_tensor(
                hg1_sb.ap(c=(t2 * 512, t2 * 512 + 512)), h1b.ap(), pC.ap(), ALU.mult),
                 ins=[h1b, pC], outs=[(hg1_sb, t2)])
            ABa.pop()
            AFa.pop()
            P.barrier()

        for oc in range(32):
            for t2 in range(2):
                pp = pA if (oc * 2 + t2) % 2 == 0 else pB
                proj_mm(P, w_o1, oc, wsA, "wo1_%d_%d" % (oc, t2),
                        lambda i: (ret_sb[i // 4].ap(c=(t2 * 512, t2 * 512 + 512)),
                                   (ret_sb[i // 4], t2)),
                        pp, final_stop=False)
                for half in range(2):
                    stt = wsB[half]
                    load_half(w_b1, oc, half, stt, "wb1")
                    pairs = [(stt.ap(c=((i - half * 16) * 128,
                                        (i - half * 16) * 128 + 128)),
                              xn1[i].ap(c=(t2 * 512, t2 * 512 + 512)))
                             for i in range(half * 16, half * 16 + 16)]
                    ins = [(stt, "c", s) for s in range(2)] + \
                          [(xn1[i], t2) for i in range(half * 16, half * 16 + 16)]
                    if half == 1:
                        pairs.append((b1t_bf.ap(c=(oc * 128, (oc + 1) * 128)),
                                      hg1_sb.ap(c=(t2 * 512, t2 * 512 + 512))))
                        ins = ins + B1K + [(hg1_sb, t2)]
                    _mmgroup(P, pp.ap(), pairs, ins=ins, outs=[pp],
                             first=False, last=(half == 1))
                tt = to_t[t2]
                P.op("act", lambda e, tt=tt, pp=pp, oc=oc: e.activation(
                    tt.ap(), pp.ap(), AF.Identity, bias=bb1_sb.ap(c=(oc, oc + 1))),
                     ins=[pp, bb1_sb], outs=[tt])
                xf = xf_t[t2]
                P.dma("sync", xf.ap(),
                      scr_x1[oc * 128:(oc + 1) * 128, t2 * 512:(t2 + 1) * 512],
                      ins=[("x1d", oc, t2)], outs=[xf])
                ot = x1w[t2]
                P.op("dve", lambda e, ot=ot, tt=tt, xf=xf: e.tensor_add(
                    ot.ap(), tt.ap(), xf.ap()), ins=[tt, xf], outs=[ot])
                P.dma("pool", outT[oc * 128:(oc + 1) * 128, t2 * 512:(t2 + 1) * 512],
                      ot.ap(), ins=[ot], outs=[("out", oc, t2)])
        AFa.pop()
        P.barrier()
        for name, shape, dt, ap in tap_list:
            tout = nc.dram_tensor("tap_" + name, list(shape), dt,
                                  kind="ExternalOutput").ap()
            P.dma("sync", tout, ap, outs=[("tap", name)])
        P.barrier()
        P.realize(nc, stack)
    return nc



def _stage_inputs(inputs):
    f = np.float32
    x = np.asarray(inputs["L0_attn_x"], f).reshape(GTOK, D)
    Wq = np.asarray(inputs["L0_attn_Wq"], f)
    Wk = np.asarray(inputs["L0_attn_Wk"], f)
    Wv = np.asarray(inputs["L0_attn_Wv"], f)
    Wo = np.asarray(inputs["L0_attn_Wo"], f)
    bq = np.asarray(inputs["L0_attn_bq"], f)
    bk = np.asarray(inputs["L0_attn_bk"], f)
    bv = np.asarray(inputs["L0_attn_bv"], f)
    bo = np.asarray(inputs["L0_attn_bo"], f)
    A0 = np.asarray(inputs["L0_ffn_A"], f)
    B0 = np.asarray(inputs["L0_ffn_B"], f)
    Wr0 = np.asarray(inputs["L0_ffn_Wr"], f)
    Wq1 = np.asarray(inputs["L1_attn_Wq"], f)
    bq1 = np.asarray(inputs["L1_attn_bq"], f)
    Wo1 = np.asarray(inputs["L1_attn_Wo"], f)
    M1 = np.asarray(inputs["L1_attn_M"], f)
    z1 = np.asarray(inputs["L1_attn_z"], f)
    bW = np.asarray(inputs["L1_ffn_base_W"], f)
    bb = np.asarray(inputs["L1_ffn_base_b"], f)
    rW = np.asarray(inputs["L1_ffn_router_W"], f)
    rb = np.asarray(inputs["L1_ffn_router_b"], f)
    A1 = np.asarray(inputs["L1_ffn_A"], f)
    B1 = np.asarray(inputs["L1_ffn_B"], f)
    em = np.asarray(inputs["L1_ffn_expert_map"], np.int32)
    assert inputs["L1_attn_Wk"] is not None and inputs["L1_attn_Wv"] is not None

    C = np.ascontiguousarray
    bf = ml_dtypes.bfloat16
    xTf = C(x.T)
    sel = np.concatenate([np.arange(4 * g * 128, (4 * g + 1) * 128) for g in range(8)])
    E0 = np.zeros((64, 1024), bf)
    for k in range(64):
        E0[k, k * 16:(k + 1) * 16] = bf(2.0)
    E1 = np.zeros((8, 128), bf)
    for e_ in range(8):
        E1[e_, e_ * 16:(e_ + 1) * 16] = bf(2.0)
    shared = dict(
        w_o0=C(Wo.T), b_o0=C(bo.reshape(32, 128).T),
        w_A0=C(A0.reshape(1024, D).T),
        w_B0=C(B0.transpose(0, 2, 1).reshape(1024, D)),
        w_r0=C(Wr0.T),
        w_q1=C(Wq1[sel, :].T), b_q1=C(bq1[sel].reshape(8, 128).T),
        w_M1=C(M1.reshape(8 * 128, 128)), w_z1=C(z1.T),
        w_o1=C(Wo1.T), w_b1=C(bW.T), b_b1=C(bb.reshape(32, 128).T),
        w_r1=C(rW.T), b_r1=C(rb.reshape(15, 1)), b_r1row=C(rb.reshape(1, 15)),
        w_A1=C(A1.reshape(128, D).T),
        w_B1=C(B1.transpose(0, 2, 1).reshape(128, D)),
        emap=C(em.reshape(15, 1)),
        ident_bf=np.eye(128, dtype=bf), ident_f=np.eye(128, dtype=f),
        masktri=C((np.arange(128)[:, None] <= np.arange(128)[None, :]).astype(bf)),
        E0c=E0, E1c=E1,
        iota15=C(np.tile(np.arange(8, dtype=f), (15, 1))),
        cstc=C(np.tile(np.array([[LNEPS, EXPB]], dtype=f), (128, 1))),
    )
    in_maps = []
    for c in range(NCORE):
        cs = slice(c * 512, (c + 1) * 512)
        m = dict(shared)
        m["xT"] = C(xTf[:, c * TOK:(c + 1) * TOK])
        m["w_qkv"] = C(np.concatenate([Wq[cs, :].T, Wk[cs, :].T, Wv[cs, :].T], axis=1))
        m["b_qkv"] = C(np.concatenate([bq[cs], bk[cs], bv[cs]]).reshape(12, 128).T)
        in_maps.append(m)
    return in_maps


def kernel(**inputs):
    nc = build_program()
    in_maps = _stage_inputs(inputs)
    res = run_bass_kernel_spmd(nc, in_maps, core_ids=list(range(NCORE)))
    out = np.empty((D, GTOK), np.float32)
    for c in range(NCORE):
        out[:, c * TOK:(c + 1) * TOK] = np.asarray(res.results[c]["outT"], np.float32)
    return np.ascontiguousarray(out.T).reshape(4, SEQ, D)


# version 08
# speedup vs baseline: 1.1359x; 1.1359x over previous
import numpy as np
import ml_dtypes
from contextlib import ExitStack

import concourse.bass as bass
import concourse.mybir as mybir
from concourse.bass_utils import run_bass_kernel_spmd
from concourse.aot_env import allow_ici_all_to_all

F32 = mybir.dt.float32
BF16 = mybir.dt.bfloat16
I32 = mybir.dt.int32
ALU = mybir.AluOpType
AF = mybir.ActivationFunctionType
AX = mybir.AxisListType.X

D = 4096
NCH = 32
TOK = 1024
GTOK = 8192
NCORE = 8
SEQ = 2048
RSQ = 0.08838834764831845
EXPB = -5.0
LNEPS = 1e-5
DENEPS = 1e-6

ENGS = ("pe", "dve", "act", "pool", "sync")
TAPS = set()
DBG_SKIP_X1 = False
DBG_ATT_DRAM = False


class Tile:
    __slots__ = ("h", "p", "c0", "w", "name")

    def __init__(self, h, p, c0, w, name=""):
        self.h, self.p, self.c0, self.w, self.name = h, p, c0, w, name

    def __repr__(self):
        return f"T[{self.name}]"

    def ap(self, c=None, p=None):
        p0, p1 = (0, self.p) if p is None else p
        c0, c1 = (0, self.w) if c is None else c
        return self.h[p0:p1, self.c0 + c0:self.c0 + c1]


class Arena:
    def __init__(self, h, cols, name):
        self.h, self.cols, self.name = h, cols, name
        self.off = 0
        self.marks = []
        self.peak = 0

    def alloc(self, w, p=128, name=""):
        w_al = (w + 127) & ~127
        t = Tile(self.h, p, self.off, w, name)
        self.off += w_al
        self.peak = max(self.peak, self.off)
        assert self.off <= self.cols, \
            f"arena {self.name} overflow at {name}: {self.off} > {self.cols}"
        return t

    def allocs(self, n, w, p=128, name=""):
        return [self.alloc(w, p, f"{name}{i}") for i in range(n)]

    def push(self):
        self.marks.append(self.off)

    def pop(self):
        self.off = self.marks.pop()


class Prog:

    def __init__(self):
        self.cnt = {e: 0 for e in ENGS}
        self.known = {e: {} for e in ENGS}
        self.ops = {e: [] for e in ENGS}
        self.res = {}
        self.extra_sems = []

    def _r(self, key):
        r = self.res.get(key)
        if r is None:
            r = {"w": None, "r": {}}
            self.res[key] = r
        return r

    def _wait(self, eng, sem, val):
        if sem == eng:
            assert val <= self.cnt[eng]
        if self.known[eng].get(sem, 0) >= val:
            return
        self.known[eng][sem] = val
        self.ops[eng].append(("wait", sem, val))

    def op(self, eng, emit, ins=(), outs=(), inc=None, sem=None):
        if inc is None:
            inc = 16 if eng == "sync" else 1
        sem_name = sem or eng
        best = {}
        for k in ins:
            r = self._r(k)
            if r["w"]:
                s, v = r["w"]
                if best.get(s, 0) < v:
                    best[s] = v
        for k in outs:
            r = self._r(k)
            if r["w"]:
                s, v = r["w"]
                if best.get(s, 0) < v:
                    best[s] = v
            for s, v in r["r"].items():
                if best.get(s, 0) < v:
                    best[s] = v
        for s, v in sorted(best.items(), key=str):
            self._wait(eng, s, v)
        if sem is None:
            self.cnt[eng] += inc
            val = self.cnt[eng]
        else:
            val = inc
        self.ops[eng].append(("emit", emit, inc, sem_name))
        for k in ins:
            self._r(k)["r"][sem_name] = val
        for k in outs:
            r = self._r(k)
            r["w"] = (sem_name, val)
            r["r"] = {}
        return val

    def dma(self, q, dst, src, ins=(), outs=()):
        def emit(e, dst=dst, src=src):
            return e.dma_start(out=dst, in_=src)
        self.op(q, emit, ins=ins, outs=outs, inc=16)

    def barrier(self):
        for e in ENGS:
            for d in ENGS:
                if d != e and self.cnt[d] > 0:
                    self._wait(e, d, self.cnt[d])

    def realize(self, nc, stack):
        sems = {}
        for e in ENGS:
            sems[e] = stack.enter_context(nc.semaphore(f"s_{e}"))
        for s in self.extra_sems:
            sems[s] = stack.enter_context(nc.semaphore(f"s_{s}"))
        block = stack.enter_context(nc.Block())

        def runner(engname):
            def run(eng):
                for item in self.ops[engname]:
                    if item[0] == "wait":
                        eng.wait_ge(sems[item[1]], item[2])
                    else:
                        _, fn, inc, sem_name = item
                        fn(eng).then_inc(sems[sem_name], inc)
            return run

        block.tensor(runner("pe"))
        block.vector(runner("dve"))
        block.scalar(runner("act"))
        block.gpsimd(runner("pool"))
        block.sync(runner("sync"))


def _mmgroup(P, out_ap, pairs, ins, outs, first=True, last=True, sgc=False):
    n = len(pairs)

    def emit(e, pairs=pairs, out_ap=out_ap, n=n, first=first, last=last, sgc=sgc):
        lastins = None
        for i, (l, r) in enumerate(pairs):
            lastins = e.matmul(out_ap, l, r, start=(first and i == 0),
                               stop=(last and i == n - 1), skip_group_check=sgc)
        return lastins
    P.op("pe", emit, ins=ins, outs=outs)


def build_program():
    nc = bass.Bass("TRN2", target_bir_lowering=False, debug=False)

    di = {}

    def inp(name, shape, dt=F32):
        di[name] = nc.dram_tensor(name, list(shape), dt, kind="ExternalInput").ap()
        return di[name]

    xT = inp("xT", (D, TOK))
    w_qkv = inp("w_qkv", (D, 1536))
    b_qkv = inp("b_qkv", (128, 12))
    w_o0 = inp("w_o0", (D, D))
    b_o0 = inp("b_o0", (128, 32))
    w_A0 = inp("w_A0", (D, 1024))
    w_B0 = inp("w_B0", (1024, D))
    w_r0 = inp("w_r0", (D, 64))
    w_q1 = inp("w_q1", (D, 1024))
    b_q1 = inp("b_q1", (128, 8))
    w_M1 = inp("w_M1", (1024, 128))
    w_z1 = inp("w_z1", (128, 8))
    w_o1 = inp("w_o1", (D, D))
    w_b1 = inp("w_b1", (D, D))
    b_b1 = inp("b_b1", (128, 32))
    w_r1 = inp("w_r1", (D, 15))
    b_r1 = inp("b_r1", (15, 1))
    b_r1row = inp("b_r1row", (1, 15))
    w_A1 = inp("w_A1", (D, 128))
    w_B1 = inp("w_B1", (128, D))
    emap = inp("emap", (15, 1), I32)
    ident_bf = inp("ident_bf", (128, 128), BF16)
    ident_f = inp("ident_f", (128, 128), F32)
    masktri = inp("masktri", (128, 128), BF16)
    E0c = inp("E0c", (64, 1024), BF16)
    E1c = inp("E1c", (8, 128), BF16)
    iota15 = inp("iota15", (15, 8), F32)
    cstc = inp("cstc", (128, 2), F32)
    dbg_attin = inp("dbg_attin", (D, TOK), BF16) if DBG_ATT_DRAM else None

    outT = nc.dram_tensor("outT", [D, TOK], F32, kind="ExternalOutput").ap()

    tap_list = []

    def scr(name, shape, dt, shared=False):
        kw = dict(addr_space="Shared") if shared else {}
        ap = nc.dram_tensor(name, list(shape), dt, **kw).ap()
        if name in TAPS:
            tap_list.append((name, shape, dt, ap))
        return ap

    g1_in = scr("g1_in", (D, TOK), BF16)
    g1_out = scr("g1_out", (NCORE * D, TOK), BF16, shared=("g1_out" not in TAPS))
    qkvT_d = scr("qkvT_d", (1536, GTOK), BF16)
    a2a_in = scr("a2a_in", (GTOK, 512), BF16)
    a2a_out = scr("a2a_out", (GTOK, 512), BF16)
    scr_f0 = scr("scr_f0", (D, TOK), F32)
    scr_x1 = scr("scr_x1", (D, TOK), F32)
    dbg_gt = scr("dbg_gt", (64, TOK), BF16) if "dbg_gt" in TAPS else None
    dbg_h = scr("dbg_h", (1024, TOK), BF16) if "dbg_h" in TAPS else None
    dbg_hg = scr("dbg_hg", (1024, TOK), BF16) if "dbg_hg" in TAPS else None
    dbg_sc = scr("dbg_sc", (64, TOK), F32) if "dbg_sc" in TAPS else None
    dbg_gtok = scr("dbg_gtok", (TOK, 64), F32) if "dbg_gtok" in TAPS else None
    dbg_stok = scr("dbg_stok", (TOK, 64), F32) if "dbg_stok" in TAPS else None
    dbg_cur = scr("dbg_cur", (TOK, 64), F32) if "dbg_cur" in TAPS else None
    dbg_e = scr("dbg_e", (TOK, 64), F32) if "dbg_e" in TAPS else None
    dbg_mx = scr("dbg_mx", (TOK, 8), F32) if "dbg_mx" in TAPS else None
    dbg_at = scr("dbg_at", (D, TOK), F32) if "dbg_at" in TAPS else None
    dbg_attT = scr("dbg_attT", (D, 512), BF16) if "dbg_attT" in TAPS else None

    P = Prog()
    P.extra_sems = ["cc1", "cc2"]

    with ExitStack() as stack:
        E = stack.enter_context
        af_h = E(nc.sbuf_tensor("arena_f", [128, 10240], F32))
        ab_h = E(nc.sbuf_tensor("arena_b", [128, 69632], BF16))
        AFa = Arena(af_h, 10240, "f32")
        ABa = Arena(ab_h, 69632, "bf16")

        id_bf = Tile(E(nc.sbuf_tensor("id_bf", [128, 128], BF16)), 128, 0, 128)
        id_f = Tile(E(nc.sbuf_tensor("id_f", [128, 128], F32)), 128, 0, 128)
        mtri = Tile(E(nc.sbuf_tensor("mtri", [128, 128], BF16)), 128, 0, 128)
        e0_sb = Tile(E(nc.sbuf_tensor("e0_sb", [64, 1024], BF16)), 64, 0, 1024)
        e1_sb = Tile(E(nc.sbuf_tensor("e1_sb", [8, 128], BF16)), 8, 0, 128)
        iota_sb = Tile(E(nc.sbuf_tensor("iota_sb", [15, 8], F32)), 15, 0, 8)
        cst = Tile(E(nc.sbuf_tensor("cst", [128, 2], F32)), 128, 0, 2)
        em_i = Tile(E(nc.sbuf_tensor("em_i", [15, 1], I32)), 15, 0, 1)
        em_f = Tile(E(nc.sbuf_tensor("em_f", [15, 1], F32)), 15, 0, 1)
        onehot = Tile(E(nc.sbuf_tensor("onehot", [15, 8], F32)), 15, 0, 8)
        onehot_bf = Tile(E(nc.sbuf_tensor("onehot_bf", [15, 8], BF16)), 15, 0, 8)
        ones_bf = Tile(E(nc.sbuf_tensor("ones_b", [1, 128], BF16)), 1, 0, 128)
        onec_bf = Tile(E(nc.sbuf_tensor("onec_b", [128, 1], BF16)), 128, 0, 1)
        ones_f = Tile(E(nc.sbuf_tensor("ones_f", [1, 128], F32)), 1, 0, 128)
        bqkv_sb = Tile(E(nc.sbuf_tensor("bqkv", [128, 12], F32)), 128, 0, 12)
        bo0_sb = Tile(E(nc.sbuf_tensor("bo0", [128, 32], F32)), 128, 0, 32)
        bq1_sb = Tile(E(nc.sbuf_tensor("bq1", [128, 8], F32)), 128, 0, 8)
        bb1_sb = Tile(E(nc.sbuf_tensor("bb1", [128, 32], F32)), 128, 0, 32)

        z1_f = Tile(E(nc.sbuf_tensor("z1f", [128, 8], F32)), 128, 0, 8)
        z1_bf = Tile(E(nc.sbuf_tensor("z1b", [128, 8], BF16)), 128, 0, 8)
        m1_bf = Tile(E(nc.sbuf_tensor("m1b", [128, 1024], BF16)), 128, 0, 1024)
        wr1_f = Tile(E(nc.sbuf_tensor("wr1f", [128, 480], F32)), 128, 0, 480, "wr1f")
        onec_f = Tile(E(nc.sbuf_tensor("onec_f", [128, 1], F32)), 128, 0, 1, "onecf")
        s0b = Tile(E(nc.sbuf_tensor("s0b", [128, 64], F32)), 128, 0, 64, "s0b")
        s1b = Tile(E(nc.sbuf_tensor("s1b", [128, 15], F32)), 128, 0, 15, "s1b")
        rb1b = Tile(E(nc.sbuf_tensor("rb1b", [128, 15], F32)), 128, 0, 15, "rb1b")
        srow = Tile(E(nc.sbuf_tensor("srow", [1, 80], F32)), 1, 0, 80, "srow")
        sc0tok = Tile(E(nc.sbuf_tensor("sc0tok", [128, 8 * 64], F32)), 128, 0, 512, "sc0tok")
        abcol = Tile(E(nc.sbuf_tensor("abcol", [128, 8], F32)), 128, 0, 8, "abcol")
        sc0v = [Tile(sc0tok.h, 128, sc0tok.c0 + i * 64, 64, f"sc0v{i}")
                for i in range(8)]
        lg1tok = Tile(E(nc.sbuf_tensor("lg1tok", [128, 64], F32)), 128, 0, 64, "lg1t")
        lg1v = [Tile(lg1tok.h, 128, lg1tok.c0 + i * 16, 16, f"lg1v{i}")
                for i in range(4)]
        a1_bf = Tile(E(nc.sbuf_tensor("a1b", [128, 4096], BF16)), 128, 0, 4096)
        b1t_bf = Tile(E(nc.sbuf_tensor("b1tb", [128, 4096], BF16)), 128, 0, 4096)

        pA = Tile(E(nc.psum_tensor("pA", [128, 512], F32)), 128, 0, 512)
        pB = Tile(E(nc.psum_tensor("pB", [128, 512], F32)), 128, 0, 512)
        pC = Tile(E(nc.psum_tensor("pC", [128, 512], F32)), 128, 0, 512)
        pD = Tile(E(nc.psum_tensor("pD", [128, 512], F32)), 128, 0, 512)
        pE_ = Tile(E(nc.psum_tensor("pE", [128, 512], F32)), 128, 0, 512)
        pF_ = Tile(E(nc.psum_tensor("pF", [128, 512], F32)), 128, 0, 512)
        pT = Tile(E(nc.psum_tensor("pT", [128, 512], BF16)), 128, 0, 512)

        P.dma("sync", id_bf.ap(), ident_bf, outs=[id_bf])
        P.dma("sync", id_f.ap(), ident_f, outs=[id_f])
        P.dma("sync", mtri.ap(), masktri, outs=[mtri])
        P.dma("sync", e0_sb.ap(), E0c, outs=[e0_sb])
        P.dma("sync", e1_sb.ap(), E1c, outs=[e1_sb])
        P.dma("sync", iota_sb.ap(), iota15, outs=[iota_sb])
        P.dma("sync", cst.ap(), cstc, outs=[cst])
        P.dma("sync", em_i.ap(), emap, outs=[em_i])
        P.dma("sync", bqkv_sb.ap(), b_qkv, outs=[bqkv_sb])
        P.dma("sync", bo0_sb.ap(), b_o0, outs=[bo0_sb])
        P.dma("sync", bq1_sb.ap(), b_q1, outs=[bq1_sb])
        P.dma("sync", bb1_sb.ap(), b_b1, outs=[bb1_sb])

        P.dma("sync", z1_f.ap(), w_z1, outs=[z1_f])
        P.op("dve", lambda e: e.tensor_copy(em_f.ap(), em_i.ap()), ins=[em_i], outs=[em_f])
        P.op("dve", lambda e: e.tensor_scalar(onehot.ap(), iota_sb.ap(), em_f.ap(), None,
                                              ALU.is_equal),
             ins=[iota_sb, em_f], outs=[onehot])
        P.op("dve", lambda e: e.tensor_copy(onehot_bf.ap(), onehot.ap()),
             ins=[onehot], outs=[onehot_bf])
        P.op("dve", lambda e: e.memset(ones_bf.ap(), 1.0), outs=[ones_bf])
        P.op("dve", lambda e: e.memset(onec_bf.ap(), 1.0), outs=[onec_bf])
        P.op("dve", lambda e: e.memset(ones_f.ap(), 1.0), outs=[ones_f])
        P.op("dve", lambda e: e.memset(onec_f.ap(), 1.0), outs=[onec_f])
        P.op("dve", lambda e: e.tensor_copy(z1_bf.ap(), z1_f.ap()), ins=[z1_f], outs=[z1_bf])

        AFa.push()
        wst0 = AFa.alloc(2048, name="ph0stage")

        def stage_convert(dram, nrows, ncols, dst_tile, tag, eng="pool"):
            nch = nrows // 128
            per = max(1, 2048 // ncols)
            s = 0
            rnd = 0
            while s < nch:
                cur = min(per, nch - s)
                src = dram[s * 128:(s + cur) * 128, 0:ncols].rearrange(
                    "(c p) n -> p c n", p=128)
                P.dma("sync", wst0.ap(c=(0, cur * ncols)).rearrange(
                    "p (c n) -> p c n", n=ncols), src, outs=[wst0])
                P.op(eng, lambda e, s=s, cur=cur: e.tensor_copy(
                    dst_tile.ap(c=(s * ncols, (s + cur) * ncols)),
                    wst0.ap(c=(0, cur * ncols))),
                     ins=[wst0], outs=[(dst_tile, "c", s)])
                s += cur
                rnd += 1

        srcw = w_r1.rearrange("(c p) n -> p c n", p=128)
        P.dma("sync", wr1_f.ap().rearrange("p (c n) -> p c n", n=15), srcw,
              outs=[wr1_f])
        def emit_s1(e):
            last = None
            for i in range(NCH):
                last = e.matmul(pD.ap(p=(0, 1), c=(0, 15)), onec_f.ap(),
                                wr1_f.ap(c=(i * 15, (i + 1) * 15)),
                                start=(i == 0), stop=(i == NCH - 1))
            return last
        P.op("pe", emit_s1, ins=[onec_f, wr1_f], outs=[pD])
        P.op("dve", lambda e: e.tensor_copy(srow.ap(p=(0, 1), c=(64, 79)),
                                            pD.ap(p=(0, 1), c=(0, 15))),
             ins=[pD], outs=[(srow, 1)])
        P.op("pe", lambda e: e.matmul(pD.ap(c=(0, 15)), ones_f.ap(),
                                      srow.ap(p=(0, 1), c=(64, 79)),
                                      start=True, stop=True),
             ins=[ones_f, (srow, 1)], outs=[pD])
        P.op("act", lambda e: e.activation(s1b.ap(), pD.ap(c=(0, 15)), AF.Identity),
             ins=[pD], outs=[s1b])
        P.dma("sync", srow.ap(p=(0, 1), c=(64, 79)), b_r1row, outs=[(srow, 1)])
        P.op("pe", lambda e: e.matmul(pC.ap(c=(0, 15)), ones_f.ap(),
                                      srow.ap(p=(0, 1), c=(64, 79)),
                                      start=True, stop=True),
             ins=[ones_f, (srow, 1)], outs=[pC])
        P.op("act", lambda e: e.activation(rb1b.ap(), pC.ap(c=(0, 15)), AF.Identity),
             ins=[pC], outs=[rb1b])
        stage_convert(w_A1, D, 128, a1_bf, "a1")
        for s2 in range(2):
            P.dma("sync", wst0.ap(), w_B1[:, s2 * 2048:(s2 + 1) * 2048], outs=[wst0])
            P.op("pool", lambda e, s2=s2: e.tensor_copy(
                b1t_bf.ap(c=(s2 * 2048, (s2 + 1) * 2048)), wst0.ap()),
                 ins=[wst0], outs=[(b1t_bf, "c", 0)])
        stage_convert(w_M1, 1024, 128, m1_bf, "m1")
        AFa.pop()
        AFa.push()
        wr0_f = AFa.alloc(2048, name="wr0f")
        for s4 in range(4):
            srcw = w_r0[s4 * 1024:(s4 + 1) * 1024, :].rearrange("(c p) n -> p c n", p=128)
            P.dma("sync", wr0_f.ap(c=(s4 * 512, (s4 + 1) * 512)).rearrange(
                "p (c n) -> p c n", n=64), srcw, outs=[(wr0_f, s4)])
        def emit_s0(e):
            last = None
            for i in range(NCH):
                last = e.matmul(pC.ap(p=(0, 1), c=(0, 64)), onec_f.ap(),
                                wr0_f.ap(c=(i * 64, (i + 1) * 64)),
                                start=(i == 0), stop=(i == NCH - 1))
            return last
        P.op("pe", emit_s0, ins=[onec_f] + [(wr0_f, s4) for s4 in range(4)], outs=[pC])
        P.op("dve", lambda e: e.tensor_copy(srow.ap(p=(0, 1), c=(0, 64)),
                                            pC.ap(p=(0, 1), c=(0, 64))),
             ins=[pC], outs=[(srow, 0)])
        P.op("pe", lambda e: e.matmul(pC.ap(c=(0, 64)), ones_f.ap(),
                                      srow.ap(p=(0, 1), c=(0, 64)),
                                      start=True, stop=True),
             ins=[ones_f, (srow, 0)], outs=[pC])
        P.op("act", lambda e: e.activation(s0b.ap(), pC.ap(c=(0, 64)), AF.Identity),
             ins=[pC], outs=[s0b])
        P.barrier()

        def rk(t, nrows, ncols):
            nch = nrows // 128
            per = max(1, 2048 // ncols)
            return [(t, "c", s) for s in range(0, nch, per)]

        A1K = rk(a1_bf, D, 128)
        B1K = rk(b1t_bf, 128, D)
        M1K = rk(m1_bf, 1024, 128)

        def layer_norm(load_chunk, write_xn, stats_done=False, router=None,
                       sub_out=None):
            xa = AFa.allocs(2, 512, name="lnx")
            sqa = ABa.allocs(2, 512, name="lnsq")
            xba = ABa.allocs(2, 512, name="lnxb")
            stat = AFa.alloc(2048, p=1, name="lnstat")
            bca = AFa.alloc(512, name="lnbca")
            bcb = AFa.alloc(512, name="lnbcb")
            tmp = AFa.alloc(512, name="lntmp")
            if not stats_done:
                for i in range(NCH):
                    xt = xa[i % 2]
                    load_chunk(i, xt)
                    sq = sqa[i % 2]
                    xb = xba[i % 2]
                    P.op("act", lambda e, s=sq, x=xt: e.activation(s.ap(), x.ap(), AF.Square),
                         ins=[xt], outs=[sq])
                    P.op("pool", lambda e, b=xb, x=xt: e.tensor_copy(b.ap(), x.ap()),
                         ins=[xt], outs=[xb])

                    def emit_s(e, xb=xb, i=i):
                        return e.matmul(pE_.ap(p=(0, 1)), onec_bf.ap(), xb.ap(),
                                        start=(i == 0), stop=(i == NCH - 1),
                                        skip_group_check=True)
                    P.op("pe", emit_s, ins=[onec_bf, xb],
                         outs=[pE_] if i in (0, NCH - 1) else ())

                    def emit_q(e, sq=sq, i=i):
                        return e.matmul(pF_.ap(p=(0, 1)), onec_bf.ap(), sq.ap(),
                                        start=(i == 0), stop=(i == NCH - 1),
                                        skip_group_check=True)
                    P.op("pe", emit_q, ins=[onec_bf, sq],
                         outs=[pF_] if i in (0, NCH - 1) else ())
            sm, sq_, sv, sr = (0, 512), (512, 1024), (1024, 1536), (1536, 2048)
            P.op("dve", lambda e: e.tensor_scalar_mul(stat.ap(c=sm, p=(0, 1)),
                                                      pE_.ap(p=(0, 1)), 1.0 / D),
                 ins=[pE_], outs=[(stat, "m")])
            P.op("dve", lambda e: e.tensor_scalar_mul(stat.ap(c=sq_, p=(0, 1)),
                                                      pF_.ap(p=(0, 1)), 1.0 / D),
                 ins=[pF_], outs=[(stat, "q")])
            P.op("act", lambda e: e.activation(stat.ap(c=sv, p=(0, 1)),
                                               stat.ap(c=sm, p=(0, 1)), AF.Square),
                 ins=[(stat, "m")], outs=[(stat, "v")])
            P.op("dve", lambda e: e.tensor_sub(stat.ap(c=sv, p=(0, 1)),
                                               stat.ap(c=sq_, p=(0, 1)),
                                               stat.ap(c=sv, p=(0, 1))),
                 ins=[(stat, "q"), (stat, "v")], outs=[(stat, "v")])
            P.op("act", lambda e: e.activation(stat.ap(c=sr, p=(0, 1)),
                                               stat.ap(c=sv, p=(0, 1)), AF.Ln,
                                               bias=cst.ap(p=(0, 1), c=(0, 1))),
                 ins=[(stat, "v"), cst], outs=[(stat, "r")])
            P.op("act", lambda e: e.activation(stat.ap(c=sr, p=(0, 1)),
                                               stat.ap(c=sr, p=(0, 1)), AF.Exp, scale=-0.5),
                 ins=[(stat, "r")], outs=[(stat, "r")])
            P.op("dve", lambda e: e.scalar_tensor_tensor(stat.ap(c=sv, p=(0, 1)),
                                                         stat.ap(c=sm, p=(0, 1)), -1.0,
                                                         stat.ap(c=sr, p=(0, 1)),
                                                         ALU.mult, ALU.mult),
                 ins=[(stat, "m"), (stat, "r")], outs=[(stat, "v")])
            P.op("pe", lambda e: e.matmul(pE_.ap(), ones_f.ap(),
                                          stat.ap(c=sr, p=(0, 1)),
                                          start=True, stop=True),
                 ins=[ones_f, (stat, "r")], outs=[pE_])
            P.op("pe", lambda e: e.matmul(pF_.ap(), ones_f.ap(),
                                          stat.ap(c=sv, p=(0, 1)),
                                          start=True, stop=True),
                 ins=[ones_f, (stat, "v")], outs=[pF_])
            P.op("act", lambda e: e.activation(bca.ap(), pE_.ap(), AF.Identity),
                 ins=[pE_], outs=[bca])
            P.op("act", lambda e: e.activation(bcb.ap(), pF_.ap(), AF.Identity),
                 ins=[pF_], outs=[bcb])
            for i in range(NCH):
                xt = xa[i % 2]
                load_chunk(i, xt)
                if router is not None:
                    wrt, ncols = router
                    def emit_r(e, xt=xt, i=i, wrt=wrt, ncols=ncols):
                        return e.matmul(pC.ap(p=(0, ncols), c=(0, 512)),
                                        wrt.ap(c=(i * ncols, (i + 1) * ncols)),
                                        xt.ap(), start=(i == 0), stop=(i == NCH - 1),
                                        skip_group_check=True)
                    P.op("pe", emit_r, ins=[wrt, xt],
                         outs=[pC] if i in (0, NCH - 1) else ())
                P.op("dve", lambda e, x=xt: e.tensor_mul(tmp.ap(), x.ap(), bca.ap()),
                     ins=[xt, bca], outs=[tmp])
                write_xn(i, tmp, bcb)
            if router is not None:
                wrt, ncols = router
                rT = AFa.alloc(512, p=ncols, name="routT")
                rtok = AFa.alloc(64, name="rtok")
                t1r = AFa.alloc(64, name="t1r")
                P.op("dve", lambda e: e.tensor_copy(rT.ap(), pC.ap(p=(0, ncols))),
                     ins=[pC], outs=[rT])
                sbt = s0b if ncols == 64 else s1b
                for sub in range(4):
                    def emit_ta(e, sub=sub):
                        return e.transpose(pD.ap(c=(0, 1)),
                                           stat.ap(c=(sr[0] + sub * 128,
                                                      sr[0] + sub * 128 + 128), p=(0, 1)),
                                           id_f.ap(p=(0, 1), c=(0, 1)))
                    P.op("pe", emit_ta, ins=[(stat, "r"), id_f], outs=[pD])
                    P.op("act", lambda e, sub=sub: e.activation(
                        abcol.ap(c=(sub, sub + 1)), pD.ap(c=(0, 1)), AF.Identity),
                         ins=[pD], outs=[(abcol, "a", sub)])
                    def emit_tb(e, sub=sub):
                        return e.transpose(pD.ap(c=(1, 2)),
                                           stat.ap(c=(sv[0] + sub * 128,
                                                      sv[0] + sub * 128 + 128), p=(0, 1)),
                                           id_f.ap(p=(0, 1), c=(0, 1)))
                    P.op("pe", emit_tb, ins=[(stat, "v"), id_f], outs=[pD])
                    P.op("act", lambda e, sub=sub: e.activation(
                        abcol.ap(c=(4 + sub, 5 + sub)), pD.ap(c=(1, 2)), AF.Identity),
                         ins=[pD], outs=[(abcol, "b", sub)])
                    def emit_tr(e, sub=sub, ncols=ncols):
                        return e.transpose(pD.ap(c=(128, 128 + ncols), p=(0, 128)),
                                           rT.ap(c=(sub * 128, sub * 128 + 128)),
                                           id_f.ap(p=(0, ncols), c=(0, ncols)))
                    P.op("pe", emit_tr, ins=[rT, id_f], outs=[pD])
                    P.op("act", lambda e, ncols=ncols: e.activation(
                        rtok.ap(c=(0, ncols)), pD.ap(c=(128, 128 + ncols)), AF.Identity),
                         ins=[pD], outs=[rtok])
                    P.op("dve", lambda e, sub=sub, ncols=ncols: e.tensor_scalar(
                        rtok.ap(c=(0, ncols)), rtok.ap(c=(0, ncols)),
                        abcol.ap(c=(sub, sub + 1)), None, ALU.mult),
                         ins=[rtok, (abcol, "a", sub)], outs=[rtok])
                    P.op("dve", lambda e, sub=sub, ncols=ncols, sbt=sbt: e.tensor_scalar(
                        t1r.ap(c=(0, ncols)), sbt.ap(), abcol.ap(c=(4 + sub, 5 + sub)),
                        None, ALU.mult),
                         ins=[sbt, (abcol, "b", sub)], outs=[t1r])
                    P.op("dve", lambda e, ncols=ncols: e.tensor_add(
                        rtok.ap(c=(0, ncols)), rtok.ap(c=(0, ncols)),
                        t1r.ap(c=(0, ncols))),
                         ins=[rtok, t1r], outs=[rtok])
                    sub_out(sub, rtok)

        ABa.push()
        xnb = ABa.allocs(NCH, 1024, name="xnb")
        for t2 in range(2):
            AFa.push()
            ABa.push()

            def loader(i, dst, t2=t2):
                P.dma("sync", dst.ap(), xT[i * 128:(i + 1) * 128, t2 * 512:(t2 + 1) * 512],
                      outs=[dst])

            def write_xn(i, tmp, bcb, t2=t2):
                xn = xnb[i]
                P.op("dve", lambda e, xn=xn, tmp=tmp, bcb=bcb, t2=t2: e.tensor_add(
                    xn.ap(c=(t2 * 512, t2 * 512 + 512)), tmp.ap(), bcb.ap()),
                     ins=[tmp, bcb], outs=[(xn, t2)])
                P.dma("pool", g1_in[i * 128:(i + 1) * 128, t2 * 512:(t2 + 1) * 512],
                      xn.ap(c=(t2 * 512, t2 * 512 + 512)), ins=[(xn, t2)],
                      outs=[("g1in", i, t2)])
            def sub_out0(sub, rtok, t2=t2):
                idx = t2 * 4 + sub
                P.op("dve", lambda e, v=sc0v[idx], rtok=rtok: e.tensor_copy(
                    v.ap(), rtok.ap(c=(0, 64))), ins=[rtok], outs=[sc0v[idx]])
            layer_norm(loader, write_xn, router=(wr0_f, 64), sub_out=sub_out0)
            AFa.pop()
            ABa.pop()
            P.barrier()

        def emit_cc1(e):
            return e.collective_compute(
                "AllGather", ALU.bypass, replica_groups=[list(range(NCORE))],
                ins=[g1_in], outs=[g1_out])
        P.barrier()
        P.op("pool", emit_cc1,
             ins=[("g1in", i, t2) for i in range(NCH) for t2 in range(2)],
             outs=[("g1out",)], sem="cc1", inc=1)
        P._wait("sync", "cc1", 1)
        P._wait("pool", "cc1", 1)

        ABa.push()
        gt_sb = ABa.alloc(1024, p=64, name="GT")
        hg_sb = ABa.allocs(8, 1024, name="hg")
        AFa.push()
        sc_cur = AFa.alloc(64, name="sccur")
        mxc = AFa.alloc(8, name="mxc")
        gat = AFa.alloc(64, name="gat")
        for t2 in range(2):
            for sub in range(4):
                stt = sc0v[t2 * 4 + sub]
                P.op("dve", lambda e, stt=stt: e.tensor_copy(sc_cur.ap(), stt.ap()),
                     ins=[stt], outs=[sc_cur])
                if dbg_stok is not None:
                    r0 = t2 * 512 + sub * 128
                    P.dma("sync", dbg_stok[r0:r0 + 128, :], stt.ap(),
                          ins=[stt], outs=[("tap", "dbg_stok", t2, sub)])
                for it in range(16):
                    P.op("dve", lambda e: e.reduce_max(mxc.ap(c=(0, 1)), sc_cur.ap(), axis=AX),
                         ins=[sc_cur], outs=[(mxc, "mx")])
                    if it == 0:
                        P.op("dve", lambda e: e.tensor_scalar_mul(mxc.ap(c=(1, 2)),
                                                                  mxc.ap(c=(0, 1)), -1.0),
                             ins=[(mxc, "mx")], outs=[(mxc, "nm0")])
                    if it < 15:
                        P.op("dve", lambda e: e.tensor_scalar(gat.ap(), sc_cur.ap(),
                                                              mxc.ap(c=(0, 1)), None,
                                                              ALU.is_ge),
                             ins=[sc_cur, (mxc, "mx")], outs=[gat])
                        P.op("dve", lambda e: e.scalar_tensor_tensor(
                            sc_cur.ap(), gat.ap(), -1e30, sc_cur.ap(), ALU.mult, ALU.add),
                             ins=[gat, sc_cur], outs=[sc_cur])
                P.op("act", lambda e, stt=stt: e.activation(gat.ap(), stt.ap(), AF.Exp,
                                                            bias=mxc.ap(c=(1, 2))),
                     ins=[stt, (mxc, "nm0")], outs=[gat])
                if dbg_e is not None:
                    r0 = t2 * 512 + sub * 128
                    P.dma("sync", dbg_e[r0:r0 + 128, :], gat.ap(),
                          ins=[gat], outs=[("tap", "dbg_e", t2, sub)])
                P.op("dve", lambda e, stt=stt: e.tensor_scalar(sc_cur.ap(), stt.ap(),
                                                               mxc.ap(c=(0, 1)), None,
                                                               ALU.is_ge),
                     ins=[stt, (mxc, "mx")], outs=[sc_cur])
                if dbg_cur is not None:
                    r0 = t2 * 512 + sub * 128
                    P.dma("sync", dbg_cur[r0:r0 + 128, :], sc_cur.ap(),
                          ins=[sc_cur], outs=[("tap", "dbg_cur", t2, sub)])
                P.op("dve", lambda e: e.tensor_mul(gat.ap(), gat.ap(), sc_cur.ap()),
                     ins=[gat, sc_cur], outs=[gat])
                P.op("dve", lambda e: e.reduce_sum(mxc.ap(c=(2, 3)), gat.ap(), axis=AX),
                     ins=[gat], outs=[(mxc, "s")])
                P.op("dve", lambda e: e.reciprocal(mxc.ap(c=(3, 4)), mxc.ap(c=(2, 3))),
                     ins=[(mxc, "s")], outs=[(mxc, "rs")])
                P.op("dve", lambda e: e.tensor_scalar(gat.ap(), gat.ap(), mxc.ap(c=(3, 4)),
                                                      None, ALU.mult),
                     ins=[gat, (mxc, "rs")], outs=[gat])
                if dbg_mx is not None:
                    r0 = t2 * 512 + sub * 128
                    P.dma("sync", dbg_mx[r0:r0 + 128, :], mxc.ap(),
                          ins=[(mxc, "mx"), (mxc, "nm0"), (mxc, "s"), (mxc, "rs")],
                          outs=[("tap", "dbg_mx", t2, sub)])

                if dbg_gtok is not None:
                    r0 = t2 * 512 + sub * 128
                    P.dma("sync", dbg_gtok[r0:r0 + 128, :], gat.ap(),
                          ins=[gat], outs=[("tap", "dbg_gtok", t2, sub)])

                def emit_trg(e):
                    return e.transpose(pB.ap(p=(0, 64), c=(64, 192)), gat.ap(), id_f.ap())
                P.op("pe", emit_trg, ins=[gat, id_f], outs=[pB])
                P.op("act", lambda e, t2=t2, sub=sub: e.activation(
                    gt_sb.ap(c=(t2 * 512 + sub * 128, t2 * 512 + sub * 128 + 128)),
                    pB.ap(c=(64, 192), p=(0, 64)), AF.Identity),
                     ins=[pB], outs=[(gt_sb, t2, sub)])
        if dbg_gt is not None:
            P.dma("sync", dbg_gt, gt_sb.ap(),
                  ins=[(gt_sb, t2_, s_) for t2_ in range(2) for s_ in range(4)],
                  outs=[("tap", "dbg_gt")])
        ABa.push()
        a0st = ABa.allocs(2, NCH * 128, name="a0st")
        a0f = AFa.alloc(1024, name="a0fst")
        hb_t = ABa.alloc(512, name="hbt")
        for j in range(8):
            stt = a0st[j % 2]
            for s in range(4):
                src = w_A0[s * 1024:(s + 1) * 1024, j * 128:(j + 1) * 128].rearrange(
                    "(c p) n -> p c n", p=128)
                P.dma("sync", a0f.ap().rearrange("p (c n) -> p c n", n=128), src,
                      outs=[a0f])
                P.op("dve", lambda e, stt=stt, s=s: e.tensor_copy(
                    stt.ap(c=(s * 1024, (s + 1) * 1024)), a0f.ap()),
                     ins=[a0f], outs=[(stt, "c", s)])
            for t2 in range(2):
                pp = pA if t2 == 0 else pB
                pairs = [(stt.ap(c=(i * 128, i * 128 + 128)),
                          xnb[i].ap(c=(t2 * 512, t2 * 512 + 512))) for i in range(NCH)]
                _mmgroup(P, pp.ap(), pairs,
                         ins=[(stt, "c", s) for s in range(4)] +
                             [(xnb[i], t2) for i in range(NCH)],
                         outs=[pp])
                P.op("act", lambda e, pp=pp: e.activation(hb_t.ap(), pp.ap(), AF.Identity),
                     ins=[pp], outs=[hb_t])
                if dbg_h is not None:
                    P.dma("sync", dbg_h[j * 128:(j + 1) * 128,
                                        t2 * 512:(t2 + 1) * 512], hb_t.ap(),
                          ins=[hb_t], outs=[("tap", "dbg_h", j, t2)])

                def emit_ge(e, j=j, t2=t2):
                    return e.matmul(pC.ap(), e0_sb.ap(c=(j * 128, j * 128 + 128)),
                                    gt_sb.ap(c=(t2 * 512, t2 * 512 + 512)),
                                    start=True, stop=True)
                P.op("pe", emit_ge,
                     ins=[e0_sb] + [(gt_sb, t2, s) for s in range(4)], outs=[pC])
                P.op("dve", lambda e, j=j, t2=t2: e.tensor_tensor(
                    hg_sb[j].ap(c=(t2 * 512, t2 * 512 + 512)), hb_t.ap(), pC.ap(), ALU.mult),
                     ins=[hb_t, pC], outs=[(hg_sb[j], t2)])
                if dbg_hg is not None:
                    P.dma("sync", dbg_hg[j * 128:(j + 1) * 128,
                                         t2 * 512:(t2 + 1) * 512],
                          hg_sb[j].ap(c=(t2 * 512, t2 * 512 + 512)),
                          ins=[(hg_sb[j], t2)], outs=[("tap", "dbg_hg", j, t2)])
        b0st = ABa.allocs(2, 8 * 128, name="b0st")
        b0f = AFa.alloc(1024, name="b0fst")
        f0_t = AFa.allocs(2, 512, name="f0t")
        for oc in range(32):
            stt = b0st[oc % 2]
            src = w_B0[:, oc * 128:(oc + 1) * 128].rearrange("(c p) n -> p c n", p=128)
            P.dma("sync", b0f.ap().rearrange("p (c n) -> p c n", n=128), src,
                  outs=[b0f])
            P.op("dve", lambda e, stt=stt: e.tensor_copy(stt.ap(), b0f.ap()),
                 ins=[b0f], outs=[stt])
            for t2 in range(2):
                pairs = [(stt.ap(c=(j * 128, j * 128 + 128)),
                          hg_sb[j].ap(c=(t2 * 512, t2 * 512 + 512))) for j in range(8)]
                _mmgroup(P, pD.ap(), pairs,
                         ins=[stt] + [(hg_sb[j], t2) for j in range(8)], outs=[pD])
                ft = f0_t[t2]
                P.op("act", lambda e, ft=ft: e.activation(ft.ap(), pD.ap(), AF.Identity),
                     ins=[pD], outs=[ft])
                P.dma("pool", scr_f0[oc * 128:(oc + 1) * 128, t2 * 512:(t2 + 1) * 512],
                      ft.ap(), ins=[ft], outs=[("f0d", oc, t2)])
        AFa.pop()
        AFa.pop()
        ABa.pop()
        ABa.pop()
        ABa.pop()
        P.barrier()

        ABa.push()
        wq_st = ABa.allocs(6, NCH * 128, name="wqkv")
        xg = [ABa.allocs(NCH, 512, name=f"xg{k}_") for k in range(2)]
        qkv_ev = ABa.allocs(2, 512, name="qkvev")
        AFa.push()
        wstg = AFa.alloc(1024, name="wstg")
        for grp in range(2):
            for jj in range(6):
                ob = grp * 6 + jj
                stt = wq_st[jj]
                for s in range(4):
                    src = w_qkv[s * 1024:(s + 1) * 1024, ob * 128:(ob + 1) * 128].rearrange(
                        "(c p) n -> p c n", p=128)
                    P.dma("sync", wstg.ap().rearrange("p (c n) -> p c n", n=128), src,
                          outs=[wstg])
                    P.op("dve", lambda e, stt=stt, s=s: e.tensor_copy(
                        stt.ap(c=(s * 1024, (s + 1) * 1024)), wstg.ap()),
                         ins=[wstg], outs=[(stt, "c", s)])
            for tt in range(16):
                r, half = tt // 2, tt % 2
                xgt = xg[tt % 2]
                for s in range(4):
                    src = g1_out[r * D + s * 1024: r * D + (s + 1) * 1024,
                                 half * 512:(half + 1) * 512].rearrange(
                        "(c p) n -> p c n", p=128)

                    def emit_d(e, src=src, xgt=xgt, s=s):
                        t0 = xgt[s * 8]
                        dst = t0.h[:, t0.c0:t0.c0 + 8 * 512].rearrange(
                            "p (c n) -> p c n", n=512)
                        return e.dma_start(out=dst, in_=src)
                    P.op("sync", emit_d, ins=[("g1out",)],
                         outs=[xgt[s * 8 + c] for c in range(8)], inc=16)
                for jj in range(6):
                    ob = grp * 6 + jj
                    stt = wq_st[jj]
                    pp = pA if jj % 2 == 0 else pB
                    pairs = [(stt.ap(c=(i * 128, i * 128 + 128)), xgt[i].ap())
                             for i in range(NCH)]
                    _mmgroup(P, pp.ap(), pairs,
                             ins=[(stt, "c", s) for s in range(4)] + list(xgt), outs=[pp])
                    ev = qkv_ev[jj % 2]
                    P.op("act", lambda e, ev=ev, pp=pp, ob=ob: e.activation(
                        ev.ap(), pp.ap(), AF.Identity, bias=bqkv_sb.ap(c=(ob, ob + 1))),
                         ins=[pp, bqkv_sb], outs=[ev])
                    P.dma("pool", qkvT_d[ob * 128:(ob + 1) * 128, tt * 512:(tt + 1) * 512],
                          ev.ap(), ins=[ev], outs=[("qkvd", ob, tt)])
        AFa.pop()
        ABa.pop()
        P.barrier()

        ABa.push()
        qT = ABa.allocs(2, 2048, name="qT")
        kT = ABa.allocs(2, 2048, name="kT")
        vT = ABa.allocs(2, 2048, name="vT")
        vext = ABa.allocs(2, 16 * 129, name="vext")
        pT_sb = ABa.allocs(16, 512, name="pTsb")
        av_bf = ABa.allocs(2, 128, name="avbf")
        AFa.push()
        rz = AFa.allocs(2, 1, name="rz")
        for vx in vext:
            P.op("dve", lambda e, vx=vx: e.memset(vx.ap(), 1.0), outs=[vx])
        for sh in range(16):
            s, h = sh // 4, sh % 4
            par = sh % 2
            qt_t, kt_t, vt_t, vx = qT[par], kT[par], vT[par], vext[par]
            P.dma("sync", qt_t.ap(),
                  qkvT_d[h * 128:(h + 1) * 128, s * 2048:(s + 1) * 2048],
                  ins=[("qkvd", h, 4 * s + k) for k in range(4)], outs=[qt_t])
            P.dma("sync", kt_t.ap(),
                  qkvT_d[(4 + h) * 128:(5 + h) * 128, s * 2048:(s + 1) * 2048],
                  ins=[("qkvd", 4 + h, 4 * s + k) for k in range(4)], outs=[kt_t])
            P.dma("sync", vt_t.ap(),
                  qkvT_d[(8 + h) * 128:(9 + h) * 128, s * 2048:(s + 1) * 2048],
                  ins=[("qkvd", 8 + h, 4 * s + k) for k in range(4)], outs=[vt_t])
            for kc in range(16):
                def emit_vtr(e, vt_t=vt_t, kc=kc):
                    return e.transpose(pT.ap(c=(0, 128)),
                                       vt_t.ap(c=(kc * 128, (kc + 1) * 128)), id_bf.ap())
                P.op("pe", emit_vtr, ins=[vt_t, id_bf], outs=[pT])
                P.op("act", lambda e, vx=vx, kc=kc: e.activation(
                    vx.ap(c=(kc * 129, kc * 129 + 128)), pT.ap(c=(0, 128)), AF.Identity),
                     ins=[pT], outs=[(vx, kc)])
            for qg in range(4):
                for kc in range(4 * qg + 4):
                    pp = pA if kc % 2 == 0 else pB
                    def emit_sc(e, kt_t=kt_t, qt_t=qt_t, kc=kc, qg=qg, pp=pp):
                        return e.matmul(pp.ap(), kt_t.ap(c=(kc * 128, (kc + 1) * 128)),
                                        qt_t.ap(c=(qg * 512, (qg + 1) * 512)),
                                        start=True, stop=True)
                    P.op("pe", emit_sc, ins=[kt_t, qt_t], outs=[pp])
                    pt = pT_sb[kc]
                    P.op("act", lambda e, pt=pt, pp=pp: e.activation(
                        pt.ap(), pp.ap(), AF.Exp, bias=cst.ap(c=(1, 2)), scale=RSQ),
                         ins=[pp, cst], outs=[pt])
                    if kc >= 4 * qg:
                        rr = kc - 4 * qg
                        P.op("dve", lambda e, pt=pt, rr=rr: e.tensor_tensor(
                            pt.ap(c=(rr * 128, rr * 128 + 128)),
                            pt.ap(c=(rr * 128, rr * 128 + 128)), mtri.ap(), ALU.mult),
                             ins=[pt, mtri], outs=[pt])
                for sub in range(4):
                    qt_i = qg * 4 + sub
                    pp = pC if sub % 2 == 0 else pD
                    pairs = [(pT_sb[kc].ap(c=(sub * 128, sub * 128 + 128)),
                              vx.ap(c=(kc * 129, kc * 129 + 129)))
                             for kc in range(qt_i + 1)]
                    _mmgroup(P, pp.ap(c=(0, 129)), pairs,
                             ins=[pT_sb[kc] for kc in range(qt_i + 1)] +
                                 [(vx, kc) for kc in range(qt_i + 1)] + [vx],
                             outs=[pp])
                    rzt = rz[sub % 2]
                    P.op("dve", lambda e, rzt=rzt, pp=pp: e.reciprocal(
                        rzt.ap(), pp.ap(c=(128, 129))), ins=[pp], outs=[rzt])
                    ab = av_bf[sub % 2]
                    P.op("act", lambda e, ab=ab, pp=pp, rzt=rzt: e.activation(
                        ab.ap(), pp.ap(c=(0, 128)), AF.Identity, scale=rzt.ap()),
                         ins=[pp, rzt], outs=[ab])
                    P.dma("pool",
                          a2a_in[s * 2048 + qt_i * 128: s * 2048 + (qt_i + 1) * 128,
                                 h * 128:(h + 1) * 128],
                          ab.ap(), ins=[ab], outs=[("a2ain", sh, qt_i)])
        AFa.pop()
        ABa.pop()

        def emit_cc2(e):
            with allow_ici_all_to_all():
                return e.collective_compute(
                    "AllToAll", ALU.bypass, replica_groups=[list(range(NCORE))],
                    ins=[a2a_in], outs=[a2a_out])
        P.barrier()
        P.op("pool", emit_cc2,
             ins=[("a2ain", sh, q) for sh in range(16) for q in range(16)],
             outs=[("a2aout",)], sem="cc2", inc=1)
        P._wait("sync", "cc2", 1)
        P._wait("pool", "cc2", 1)
        P.barrier()

        xn1 = ABa.allocs(NCH, 1024, name="xn1")
        ret_sb = ABa.allocs(8, 1024, name="ret")
        hg1_sb = ABa.alloc(1024, name="hg1")
        wsA = ABa.allocs(2, 16 * 128, name="wsA")
        wsB = ABa.allocs(2, 16 * 128, name="wsB")
        AFa.push()
        wfst = AFa.alloc(1024, name="wfst")
        to_t = AFa.allocs(2, 512, name="tot")
        xf_t = AFa.allocs(2, 512, name="xft")
        x1w = AFa.allocs(2, 512, name="x1w")

        def load_half(dram, oc, half, stt, tag):
            for s in range(2):
                r0 = half * 2048 + s * 1024
                src = dram[r0:r0 + 1024, oc * 128:(oc + 1) * 128].rearrange(
                    "(c p) n -> p c n", p=128)
                P.dma("sync", wfst.ap().rearrange("p (c n) -> p c n", n=128)
                      if False else wfst.ap(c=(0, 1024)).rearrange("p (c n) -> p c n", n=128),
                      src, outs=[wfst])
                P.op("dve", lambda e, stt=stt, s=s: e.tensor_copy(
                    stt.ap(c=(s * 1024, (s + 1) * 1024)), wfst.ap(c=(0, 1024))),
                     ins=[wfst], outs=[(stt, "c", s)])

        def proj_mm(P, dram, oc, bufs, tag, rhs_of_chunk, pp, final_stop=True):
            for half in range(2):
                stt = bufs[half]
                load_half(dram, oc, half, stt, tag)
                pairs = [(stt.ap(c=((i - half * 16) * 128, (i - half * 16) * 128 + 128)),
                          rhs_of_chunk(i)[0]) for i in range(half * 16, half * 16 + 16)]
                ins = [(stt, "c", s) for s in range(2)] + \
                      [rhs_of_chunk(i)[1] for i in range(half * 16, half * 16 + 16)]
                _mmgroup(P, pp.ap(), pairs, ins=ins, outs=[pp],
                         first=(half == 0), last=(final_stop and half == 1))

        for t2 in range(2):
            ABa.push()
            attT = ABa.allocs(NCH, 512, name="attT")
            a2t = ABa.allocs(2, 512, name="a2t")
            sqa6 = ABa.allocs(2, 512, name="sq6")
            xba6 = ABa.allocs(2, 512, name="xb6")
            if DBG_ATT_DRAM:
                for i_ in range(NCH):
                    P.dma("sync", attT[i_].ap(),
                          dbg_attin[i_ * 128:(i_ + 1) * 128,
                                    t2 * 512:(t2 + 1) * 512],
                          outs=[attT[i_]])
            else:
              for r in range(NCORE):
                for ts in range(4):
                    a2 = a2t[ts % 2]
                    row0 = r * 1024 + t2 * 512 + ts * 128
                    P.dma("sync", a2.ap(), a2a_out[row0:row0 + 128, :],
                          ins=[("a2aout",)], outs=[a2])
                    for rs in range(4):
                        def emit_at(e, a2=a2, rs=rs):
                            return e.transpose(pT.ap(c=(rs * 128, rs * 128 + 128)),
                                               a2.ap(c=(rs * 128, rs * 128 + 128)),
                                               id_bf.ap())
                        P.op("pe", emit_at, ins=[a2, id_bf], outs=[pT])
                        ch = 4 * r + rs
                        P.op("act", lambda e, ch=ch, ts=ts, rs=rs: e.activation(
                            attT[ch].ap(c=(ts * 128, ts * 128 + 128)),
                            pT.ap(c=(rs * 128, rs * 128 + 128)), AF.Identity),
                             ins=[pT], outs=[attT[ch]])
            if dbg_attT is not None and t2 == 0:
                for i_ in range(NCH):
                    P.dma("sync", dbg_attT[i_ * 128:(i_ + 1) * 128, :], attT[i_].ap(),
                          ins=[attT[i_]], outs=[("tap", "dbg_attT", i_)])
            for oc in range(32):
                pp = pA if oc % 2 == 0 else pB
                proj_mm(P, w_o0, oc, wsA, "wo0" + str(t2),
                        lambda i: (attT[i].ap(), attT[i]), pp)
                tt = to_t[oc % 2]
                P.op("act", lambda e, tt=tt, pp=pp, oc=oc: e.activation(
                    tt.ap(), pp.ap(), AF.Identity, bias=bo0_sb.ap(c=(oc, oc + 1))),
                     ins=[pp, bo0_sb], outs=[tt])
                if dbg_at is not None:
                    P.dma("sync", dbg_at[oc * 128:(oc + 1) * 128,
                                         t2 * 512:(t2 + 1) * 512], tt.ap(),
                          ins=[tt], outs=[("tap", "dbg_at", oc, t2)])
                if DBG_SKIP_X1:
                    continue
                xf = xf_t[oc % 2]
                P.dma("sync", xf.ap(),
                      xT[oc * 128:(oc + 1) * 128, t2 * 512:(t2 + 1) * 512], outs=[xf])
                x1t = x1w[oc % 2]
                P.op("dve", lambda e, x1t=x1t, tt=tt, xf=xf: e.tensor_add(
                    x1t.ap(), tt.ap(), xf.ap()), ins=[tt, xf], outs=[x1t])
                P.dma("sync", xf.ap(),
                      scr_f0[oc * 128:(oc + 1) * 128, t2 * 512:(t2 + 1) * 512],
                      ins=[("f0d", oc, t2)], outs=[xf])
                P.op("dve", lambda e, x1t=x1t, xf=xf: e.tensor_add(
                    x1t.ap(), x1t.ap(), xf.ap()), ins=[x1t, xf], outs=[x1t])
                P.dma("pool", scr_x1[oc * 128:(oc + 1) * 128, t2 * 512:(t2 + 1) * 512],
                      x1t.ap(), ins=[x1t], outs=[("x1d", oc, t2)])
                sq = sqa6[oc % 2]
                xb = xba6[oc % 2]
                P.op("act", lambda e, sq=sq, x1t=x1t: e.activation(sq.ap(), x1t.ap(),
                                                                   AF.Square),
                     ins=[x1t], outs=[sq])
                P.op("pool", lambda e, xb=xb, x1t=x1t: e.tensor_copy(xb.ap(), x1t.ap()),
                     ins=[x1t], outs=[xb])

                def emit_s(e, xb=xb, oc=oc):
                    return e.matmul(pE_.ap(p=(0, 1)), onec_bf.ap(), xb.ap(),
                                    start=(oc == 0), stop=(oc == 31),
                                    skip_group_check=True)
                P.op("pe", emit_s, ins=[onec_bf, xb],
                     outs=[pE_] if oc in (0, 31) else ())

                def emit_q(e, sq=sq, oc=oc):
                    return e.matmul(pF_.ap(p=(0, 1)), onec_bf.ap(), sq.ap(),
                                    start=(oc == 0), stop=(oc == 31),
                                    skip_group_check=True)
                P.op("pe", emit_q, ins=[onec_bf, sq],
                     outs=[pF_] if oc in (0, 31) else ())
            ABa.pop()
            P.barrier()
            AFa.push()
            ABa.push()

            def loader1(i, dst, t2=t2):
                P.dma("sync", dst.ap(),
                      scr_x1[i * 128:(i + 1) * 128, t2 * 512:(t2 + 1) * 512],
                      ins=[("x1d", i, t2)], outs=[dst])

            def write_xn1(i, tmp, bcb, t2=t2):
                xn = xn1[i]
                P.op("dve", lambda e, xn=xn, tmp=tmp, bcb=bcb, t2=t2: e.tensor_add(
                    xn.ap(c=(t2 * 512, t2 * 512 + 512)), tmp.ap(), bcb.ap()),
                     ins=[tmp, bcb], outs=[(xn, t2)])
            def sub_out1(sub, rtok):
                P.op("dve", lambda e, v=lg1v[sub], rtok=rtok: e.tensor_add(
                    v.ap(c=(0, 15)), rtok.ap(c=(0, 15)), rb1b.ap()),
                     ins=[rtok, rb1b], outs=[lg1v[sub]])
            layer_norm(loader1, write_xn1, stats_done=True,
                       router=(wr1_f, 15), sub_out=sub_out1)
            AFa.pop()
            ABa.pop()
            P.barrier()
            AFa.push()
            ABa.push()
            sigq = ABa.allocs(8, 512, name="sigq")
            qf32 = AFa.alloc(512, name="qf32")
            tmn = AFa.alloc(512, name="tmn")
            tex = AFa.alloc(512, name="tex")
            for ob in range(8):
                pp = pA if ob % 2 == 0 else pB
                proj_mm(P, w_q1, ob, wsA, "wq1" + str(t2),
                        lambda i: (xn1[i].ap(c=(t2 * 512, t2 * 512 + 512)), (xn1[i], t2)),
                        pp)
                P.op("act", lambda e, pp=pp, ob=ob: e.activation(
                    qf32.ap(), pp.ap(), AF.Identity, bias=bq1_sb.ap(c=(ob, ob + 1))),
                     ins=[pp, bq1_sb], outs=[qf32])
                P.op("dve", lambda e: e.tensor_scalar_min(tmn.ap(), qf32.ap(), 0.0),
                     ins=[qf32], outs=[tmn])
                P.op("act", lambda e: e.activation(tex.ap(), tmn.ap(), AF.Exp),
                     ins=[tmn], outs=[tex])
                P.op("dve", lambda e: e.tensor_scalar_max(tmn.ap(), qf32.ap(), 0.0),
                     ins=[qf32], outs=[tmn])
                P.op("dve", lambda e, sg=sigq[ob]: e.tensor_add(sg.ap(), tex.ap(), tmn.ap()),
                     ins=[tex, tmn], outs=[sigq[ob]])
            nrow = AFa.alloc(512, name="nrow")
            drow = AFa.alloc(512, p=1, name="drow")
            for hh in range(8):
                P.op("pe", lambda e, hh=hh: e.matmul(
                    pC.ap(), m1_bf.ap(c=(hh * 128, (hh + 1) * 128)), sigq[hh].ap(),
                    start=True, stop=True),
                     ins=M1K + [sigq[hh]], outs=[pC])
                P.op("pe", lambda e, hh=hh: e.matmul(
                    pD.ap(p=(0, 1)), z1_bf.ap(c=(hh, hh + 1)), sigq[hh].ap(),
                    start=True, stop=True),
                     ins=[z1_bf, sigq[hh]], outs=[pD])
                P.op("dve", lambda e: e.tensor_scalar_add(drow.ap(p=(0, 1)),
                                                          pD.ap(p=(0, 1)), DENEPS),
                     ins=[pD], outs=[drow])
                P.op("dve", lambda e: e.reciprocal(drow.ap(p=(0, 1)), drow.ap(p=(0, 1))),
                     ins=[drow], outs=[drow])
                P.op("pe", lambda e: e.matmul(pD.ap(), ones_f.ap(),
                                              drow.ap(p=(0, 1)), start=True, stop=True),
                     ins=[ones_f, drow], outs=[pD])
                P.op("act", lambda e: e.activation(nrow.ap(), pC.ap(), AF.Identity),
                     ins=[pC], outs=[nrow])
                P.op("dve", lambda e, hh=hh, t2=t2: e.tensor_tensor(
                    ret_sb[hh].ap(c=(t2 * 512, t2 * 512 + 512)), nrow.ap(), pD.ap(),
                    ALU.mult),
                     ins=[nrow, pD], outs=[(ret_sb[hh], t2)])
            e15 = AFa.alloc(16, name="e15")
            cur15 = AFa.alloc(16, name="cur15")
            m15 = AFa.alloc(8, name="m15")
            mp = AFa.alloc(16, name="mp")
            mpT = ABa.alloc(512, p=15, name="mpT")
            w8 = ABa.alloc(512, p=8, name="w8")
            for sub in range(4):
                lg = lg1v[sub]
                P.op("dve", lambda e, lg=lg: e.reduce_max(m15.ap(c=(0, 1)),
                                                          lg.ap(c=(0, 15)),
                                                          axis=AX, negate=True),
                     ins=[lg], outs=[(m15, "nm")])
                P.op("act", lambda e, lg=lg: e.activation(e15.ap(c=(0, 15)),
                                                          lg.ap(c=(0, 15)), AF.Exp,
                                                          bias=m15.ap(c=(0, 1))),
                     ins=[lg, (m15, "nm")], outs=[e15])
                P.op("dve", lambda e: e.reduce_sum(m15.ap(c=(1, 2)), e15.ap(c=(0, 15)),
                                                   axis=AX),
                     ins=[e15], outs=[(m15, "s")])
                P.op("dve", lambda e: e.reciprocal(m15.ap(c=(2, 3)), m15.ap(c=(1, 2))),
                     ins=[(m15, "s")], outs=[(m15, "rs")])
                P.op("dve", lambda e: e.tensor_copy(cur15.ap(c=(0, 15)), e15.ap(c=(0, 15))),
                     ins=[e15], outs=[cur15])
                for it in range(8):
                    P.op("dve", lambda e: e.reduce_max(m15.ap(c=(3, 4)),
                                                       cur15.ap(c=(0, 15)), axis=AX),
                         ins=[cur15], outs=[(m15, "mx")])
                    if it < 7:
                        P.op("dve", lambda e: e.tensor_scalar(mp.ap(c=(0, 15)),
                                                              cur15.ap(c=(0, 15)),
                                                              m15.ap(c=(3, 4)), None,
                                                              ALU.is_ge),
                             ins=[cur15, (m15, "mx")], outs=[mp])
                        P.op("dve", lambda e: e.scalar_tensor_tensor(
                            cur15.ap(c=(0, 15)), mp.ap(c=(0, 15)), -1e30,
                            cur15.ap(c=(0, 15)), ALU.mult, ALU.add),
                             ins=[mp, cur15], outs=[cur15])
                P.op("dve", lambda e: e.tensor_scalar(mp.ap(c=(0, 15)), e15.ap(c=(0, 15)),
                                                      m15.ap(c=(3, 4)), None, ALU.is_ge),
                     ins=[e15, (m15, "mx")], outs=[mp])
                P.op("dve", lambda e: e.scalar_tensor_tensor(
                    mp.ap(c=(0, 15)), e15.ap(c=(0, 15)), m15.ap(c=(2, 3)),
                    mp.ap(c=(0, 15)), ALU.mult, ALU.mult),
                     ins=[e15, (m15, "rs"), mp], outs=[mp])

                def emit_trm(e, sub=sub):
                    return e.transpose(pB.ap(p=(0, 15), c=(128, 256)), mp.ap(c=(0, 15)),
                                       id_f.ap())
                P.op("pe", emit_trm, ins=[mp, id_f], outs=[pB])
                P.op("act", lambda e, sub=sub: e.activation(
                    mpT.ap(c=(sub * 128, sub * 128 + 128)),
                    pB.ap(c=(128, 256), p=(0, 15)), AF.Identity),
                     ins=[pB], outs=[(mpT, sub)])
            P.op("pe", lambda e: e.matmul(pC.ap(p=(0, 8)), onehot_bf.ap(), mpT.ap(),
                                          start=True, stop=True),
                 ins=[onehot_bf] + [(mpT, s) for s in range(4)], outs=[pC])
            P.op("act", lambda e: e.activation(w8.ap(), pC.ap(p=(0, 8)), AF.Identity),
                 ins=[pC], outs=[w8])
            P.op("pe", lambda e: e.matmul(pC.ap(), e1_sb.ap(), w8.ap(),
                                          start=True, stop=True),
                 ins=[e1_sb, w8], outs=[pC])
            pairs = [(a1_bf.ap(c=(i * 128, i * 128 + 128)),
                      xn1[i].ap(c=(t2 * 512, t2 * 512 + 512))) for i in range(NCH)]
            _mmgroup(P, pD.ap(), pairs,
                     ins=A1K + [(xn1[i], t2) for i in range(NCH)], outs=[pD])
            h1b = AFa.alloc(512, name="h1b")
            P.op("act", lambda e: e.activation(h1b.ap(), pD.ap(), AF.Identity),
                 ins=[pD], outs=[h1b])
            P.op("dve", lambda e, t2=t2: e.tensor_tensor(
                hg1_sb.ap(c=(t2 * 512, t2 * 512 + 512)), h1b.ap(), pC.ap(), ALU.mult),
                 ins=[h1b, pC], outs=[(hg1_sb, t2)])
            ABa.pop()
            AFa.pop()
            P.barrier()

        for oc in range(32):
            for half in range(2):
                stt = wsA[half]
                load_half(w_o1, oc, half, stt, "wo1")
                for t2 in range(2):
                    pp = pA if t2 == 0 else pB
                    pairs = [(stt.ap(c=((i - half * 16) * 128,
                                        (i - half * 16) * 128 + 128)),
                              ret_sb[i // 4].ap(c=(t2 * 512, t2 * 512 + 512)))
                             for i in range(half * 16, half * 16 + 16)]
                    ins = [(stt, "c", s) for s in range(2)] + \
                          [(ret_sb[i // 4], t2) for i in range(half * 16, half * 16 + 16, 4)]
                    _mmgroup(P, pp.ap(), pairs, ins=ins, outs=[pp],
                             first=(half == 0), last=False)
            for half in range(2):
                stt = wsB[half]
                load_half(w_b1, oc, half, stt, "wb1")
                for t2 in range(2):
                    pp = pA if t2 == 0 else pB
                    pairs = [(stt.ap(c=((i - half * 16) * 128,
                                        (i - half * 16) * 128 + 128)),
                              xn1[i].ap(c=(t2 * 512, t2 * 512 + 512)))
                             for i in range(half * 16, half * 16 + 16)]
                    ins = [(stt, "c", s) for s in range(2)] + \
                          [(xn1[i], t2) for i in range(half * 16, half * 16 + 16)]
                    if half == 1:
                        pairs.append((b1t_bf.ap(c=(oc * 128, (oc + 1) * 128)),
                                      hg1_sb.ap(c=(t2 * 512, t2 * 512 + 512))))
                        ins = ins + B1K + [(hg1_sb, t2)]
                    _mmgroup(P, pp.ap(), pairs, ins=ins, outs=[pp],
                             first=False, last=(half == 1))
            for t2 in range(2):
                pp = pA if t2 == 0 else pB
                tt = to_t[t2]
                P.op("act", lambda e, tt=tt, pp=pp, oc=oc: e.activation(
                    tt.ap(), pp.ap(), AF.Identity, bias=bb1_sb.ap(c=(oc, oc + 1))),
                     ins=[pp, bb1_sb], outs=[tt])
                xf = xf_t[t2]
                P.dma("sync", xf.ap(),
                      scr_x1[oc * 128:(oc + 1) * 128, t2 * 512:(t2 + 1) * 512],
                      ins=[("x1d", oc, t2)], outs=[xf])
                ot = x1w[t2]
                P.op("dve", lambda e, ot=ot, tt=tt, xf=xf: e.tensor_add(
                    ot.ap(), tt.ap(), xf.ap()), ins=[tt, xf], outs=[ot])
                P.dma("pool", outT[oc * 128:(oc + 1) * 128, t2 * 512:(t2 + 1) * 512],
                      ot.ap(), ins=[ot], outs=[("out", oc, t2)])
        AFa.pop()
        P.barrier()
        for name, shape, dt, ap in tap_list:
            tout = nc.dram_tensor("tap_" + name, list(shape), dt,
                                  kind="ExternalOutput").ap()
            P.dma("sync", tout, ap, outs=[("tap", name)])
        P.barrier()
        P.realize(nc, stack)
    return nc



def _stage_inputs(inputs):
    f = np.float32
    x = np.asarray(inputs["L0_attn_x"], f).reshape(GTOK, D)
    Wq = np.asarray(inputs["L0_attn_Wq"], f)
    Wk = np.asarray(inputs["L0_attn_Wk"], f)
    Wv = np.asarray(inputs["L0_attn_Wv"], f)
    Wo = np.asarray(inputs["L0_attn_Wo"], f)
    bq = np.asarray(inputs["L0_attn_bq"], f)
    bk = np.asarray(inputs["L0_attn_bk"], f)
    bv = np.asarray(inputs["L0_attn_bv"], f)
    bo = np.asarray(inputs["L0_attn_bo"], f)
    A0 = np.asarray(inputs["L0_ffn_A"], f)
    B0 = np.asarray(inputs["L0_ffn_B"], f)
    Wr0 = np.asarray(inputs["L0_ffn_Wr"], f)
    Wq1 = np.asarray(inputs["L1_attn_Wq"], f)
    bq1 = np.asarray(inputs["L1_attn_bq"], f)
    Wo1 = np.asarray(inputs["L1_attn_Wo"], f)
    M1 = np.asarray(inputs["L1_attn_M"], f)
    z1 = np.asarray(inputs["L1_attn_z"], f)
    bW = np.asarray(inputs["L1_ffn_base_W"], f)
    bb = np.asarray(inputs["L1_ffn_base_b"], f)
    rW = np.asarray(inputs["L1_ffn_router_W"], f)
    rb = np.asarray(inputs["L1_ffn_router_b"], f)
    A1 = np.asarray(inputs["L1_ffn_A"], f)
    B1 = np.asarray(inputs["L1_ffn_B"], f)
    em = np.asarray(inputs["L1_ffn_expert_map"], np.int32)
    assert inputs["L1_attn_Wk"] is not None and inputs["L1_attn_Wv"] is not None

    C = np.ascontiguousarray
    bf = ml_dtypes.bfloat16
    xTf = C(x.T)
    sel = np.concatenate([np.arange(4 * g * 128, (4 * g + 1) * 128) for g in range(8)])
    E0 = np.zeros((64, 1024), bf)
    for k in range(64):
        E0[k, k * 16:(k + 1) * 16] = bf(2.0)
    E1 = np.zeros((8, 128), bf)
    for e_ in range(8):
        E1[e_, e_ * 16:(e_ + 1) * 16] = bf(2.0)
    shared = dict(
        w_o0=C(Wo.T), b_o0=C(bo.reshape(32, 128).T),
        w_A0=C(A0.reshape(1024, D).T),
        w_B0=C(B0.transpose(0, 2, 1).reshape(1024, D)),
        w_r0=C(Wr0.T),
        w_q1=C(Wq1[sel, :].T), b_q1=C(bq1[sel].reshape(8, 128).T),
        w_M1=C(M1.reshape(8 * 128, 128)), w_z1=C(z1.T),
        w_o1=C(Wo1.T), w_b1=C(bW.T), b_b1=C(bb.reshape(32, 128).T),
        w_r1=C(rW.T), b_r1=C(rb.reshape(15, 1)), b_r1row=C(rb.reshape(1, 15)),
        w_A1=C(A1.reshape(128, D).T),
        w_B1=C(B1.transpose(0, 2, 1).reshape(128, D)),
        emap=C(em.reshape(15, 1)),
        ident_bf=np.eye(128, dtype=bf), ident_f=np.eye(128, dtype=f),
        masktri=C((np.arange(128)[:, None] <= np.arange(128)[None, :]).astype(bf)),
        E0c=E0, E1c=E1,
        iota15=C(np.tile(np.arange(8, dtype=f), (15, 1))),
        cstc=C(np.tile(np.array([[LNEPS, EXPB]], dtype=f), (128, 1))),
    )
    in_maps = []
    for c in range(NCORE):
        cs = slice(c * 512, (c + 1) * 512)
        m = dict(shared)
        m["xT"] = C(xTf[:, c * TOK:(c + 1) * TOK])
        m["w_qkv"] = C(np.concatenate([Wq[cs, :].T, Wk[cs, :].T, Wv[cs, :].T], axis=1))
        m["b_qkv"] = C(np.concatenate([bq[cs], bk[cs], bv[cs]]).reshape(12, 128).T)
        in_maps.append(m)
    return in_maps


def kernel(**inputs):
    nc = build_program()
    in_maps = _stage_inputs(inputs)
    res = run_bass_kernel_spmd(nc, in_maps, core_ids=list(range(NCORE)))
    out = np.empty((D, GTOK), np.float32)
    for c in range(NCORE):
        out[:, c * TOK:(c + 1) * TOK] = np.asarray(res.results[c]["outT"], np.float32)
    return np.ascontiguousarray(out.T).reshape(4, SEQ, D)


# version 09
# speedup vs baseline: 1.1404x; 1.0040x over previous
import numpy as np
import ml_dtypes
from contextlib import ExitStack

import concourse.bass as bass
import concourse.mybir as mybir
from concourse.bass_utils import run_bass_kernel_spmd
from concourse.aot_env import allow_ici_all_to_all

F32 = mybir.dt.float32
BF16 = mybir.dt.bfloat16
I32 = mybir.dt.int32
ALU = mybir.AluOpType
AF = mybir.ActivationFunctionType
AX = mybir.AxisListType.X

D = 4096
NCH = 32
TOK = 1024
GTOK = 8192
NCORE = 8
SEQ = 2048
RSQ = 0.08838834764831845
EXPB = -5.0
LNEPS = 1e-5
DENEPS = 1e-6

ENGS = ("pe", "dve", "act", "pool", "sync")
TAPS = set()
DBG_SKIP_X1 = False
DBG_ATT_DRAM = False


class Tile:
    __slots__ = ("h", "p", "c0", "w", "name")

    def __init__(self, h, p, c0, w, name=""):
        self.h, self.p, self.c0, self.w, self.name = h, p, c0, w, name

    def __repr__(self):
        return f"T[{self.name}]"

    def ap(self, c=None, p=None):
        p0, p1 = (0, self.p) if p is None else p
        c0, c1 = (0, self.w) if c is None else c
        return self.h[p0:p1, self.c0 + c0:self.c0 + c1]


class Arena:
    def __init__(self, h, cols, name):
        self.h, self.cols, self.name = h, cols, name
        self.off = 0
        self.marks = []
        self.peak = 0

    def alloc(self, w, p=128, name=""):
        w_al = (w + 127) & ~127
        t = Tile(self.h, p, self.off, w, name)
        self.off += w_al
        self.peak = max(self.peak, self.off)
        assert self.off <= self.cols, \
            f"arena {self.name} overflow at {name}: {self.off} > {self.cols}"
        return t

    def allocs(self, n, w, p=128, name=""):
        return [self.alloc(w, p, f"{name}{i}") for i in range(n)]

    def push(self):
        self.marks.append(self.off)

    def pop(self):
        self.off = self.marks.pop()


class Prog:

    def __init__(self):
        self.cnt = {e: 0 for e in ENGS}
        self.known = {e: {} for e in ENGS}
        self.ops = {e: [] for e in ENGS}
        self.res = {}
        self.extra_sems = []

    def _r(self, key):
        r = self.res.get(key)
        if r is None:
            r = {"w": None, "r": {}}
            self.res[key] = r
        return r

    def _wait(self, eng, sem, val):
        if sem == eng:
            assert val <= self.cnt[eng]
        if self.known[eng].get(sem, 0) >= val:
            return
        self.known[eng][sem] = val
        self.ops[eng].append(("wait", sem, val))

    def op(self, eng, emit, ins=(), outs=(), inc=None, sem=None):
        if inc is None:
            inc = 16 if eng == "sync" else 1
        sem_name = sem or eng
        best = {}
        for k in ins:
            r = self._r(k)
            if r["w"]:
                s, v = r["w"]
                if best.get(s, 0) < v:
                    best[s] = v
        for k in outs:
            r = self._r(k)
            if r["w"]:
                s, v = r["w"]
                if best.get(s, 0) < v:
                    best[s] = v
            for s, v in r["r"].items():
                if best.get(s, 0) < v:
                    best[s] = v
        for s, v in sorted(best.items(), key=str):
            self._wait(eng, s, v)
        if sem is None:
            self.cnt[eng] += inc
            val = self.cnt[eng]
        else:
            val = inc
        self.ops[eng].append(("emit", emit, inc, sem_name))
        for k in ins:
            self._r(k)["r"][sem_name] = val
        for k in outs:
            r = self._r(k)
            r["w"] = (sem_name, val)
            r["r"] = {}
        return val

    def dma(self, q, dst, src, ins=(), outs=()):
        def emit(e, dst=dst, src=src):
            return e.dma_start(out=dst, in_=src)
        self.op(q, emit, ins=ins, outs=outs, inc=16)

    def barrier(self):
        for e in ENGS:
            for d in ENGS:
                if d != e and self.cnt[d] > 0:
                    self._wait(e, d, self.cnt[d])

    def realize(self, nc, stack):
        sems = {}
        for e in ENGS:
            sems[e] = stack.enter_context(nc.semaphore(f"s_{e}"))
        for s in self.extra_sems:
            sems[s] = stack.enter_context(nc.semaphore(f"s_{s}"))
        block = stack.enter_context(nc.Block())

        def runner(engname):
            def run(eng):
                for item in self.ops[engname]:
                    if item[0] == "wait":
                        eng.wait_ge(sems[item[1]], item[2])
                    else:
                        _, fn, inc, sem_name = item
                        fn(eng).then_inc(sems[sem_name], inc)
            return run

        block.tensor(runner("pe"))
        block.vector(runner("dve"))
        block.scalar(runner("act"))
        block.gpsimd(runner("pool"))
        block.sync(runner("sync"))


def _mmgroup(P, out_ap, pairs, ins, outs, first=True, last=True, sgc=False):
    n = len(pairs)

    def emit(e, pairs=pairs, out_ap=out_ap, n=n, first=first, last=last, sgc=sgc):
        lastins = None
        for i, (l, r) in enumerate(pairs):
            lastins = e.matmul(out_ap, l, r, start=(first and i == 0),
                               stop=(last and i == n - 1), skip_group_check=sgc)
        return lastins
    P.op("pe", emit, ins=ins, outs=outs)


def build_program():
    nc = bass.Bass("TRN2", target_bir_lowering=False, debug=False)

    di = {}

    def inp(name, shape, dt=F32):
        di[name] = nc.dram_tensor(name, list(shape), dt, kind="ExternalInput").ap()
        return di[name]

    xT = inp("xT", (D, TOK))
    w_qkv = inp("w_qkv", (D, 1536))
    b_qkv = inp("b_qkv", (128, 12))
    w_o0 = inp("w_o0", (D, D))
    b_o0 = inp("b_o0", (128, 32))
    w_A0 = inp("w_A0", (D, 1024))
    w_B0 = inp("w_B0", (1024, D))
    w_r0 = inp("w_r0", (D, 64))
    w_q1 = inp("w_q1", (D, 1024))
    b_q1 = inp("b_q1", (128, 8))
    w_M1 = inp("w_M1", (1024, 128))
    w_z1 = inp("w_z1", (128, 8))
    w_o1 = inp("w_o1", (D, D))
    w_b1 = inp("w_b1", (D, D))
    b_b1 = inp("b_b1", (128, 32))
    w_r1 = inp("w_r1", (D, 15))
    b_r1 = inp("b_r1", (15, 1))
    b_r1row = inp("b_r1row", (1, 15))
    w_A1 = inp("w_A1", (D, 128))
    w_B1 = inp("w_B1", (128, D))
    emap = inp("emap", (15, 1), I32)
    ident_bf = inp("ident_bf", (128, 128), BF16)
    ident_f = inp("ident_f", (128, 128), F32)
    masktri = inp("masktri", (128, 128), BF16)
    E0c = inp("E0c", (64, 1024), BF16)
    E1c = inp("E1c", (8, 128), BF16)
    iota15 = inp("iota15", (15, 8), F32)
    cstc = inp("cstc", (128, 2), F32)
    dbg_attin = inp("dbg_attin", (D, TOK), BF16) if DBG_ATT_DRAM else None

    outT = nc.dram_tensor("outT", [D, TOK], F32, kind="ExternalOutput").ap()

    tap_list = []

    def scr(name, shape, dt, shared=False):
        kw = dict(addr_space="Shared") if shared else {}
        ap = nc.dram_tensor(name, list(shape), dt, **kw).ap()
        if name in TAPS:
            tap_list.append((name, shape, dt, ap))
        return ap

    g1_in = scr("g1_in", (D, TOK), BF16)
    g1_out = scr("g1_out", (NCORE * D, TOK), BF16, shared=("g1_out" not in TAPS))
    qkvT_d = scr("qkvT_d", (1536, GTOK), BF16)
    a2a_in = scr("a2a_in", (GTOK, 512), BF16)
    a2a_out = scr("a2a_out", (GTOK, 512), BF16)
    scr_f0 = scr("scr_f0", (D, TOK), F32)
    scr_x1 = scr("scr_x1", (D, TOK), F32)
    dbg_gt = scr("dbg_gt", (64, TOK), BF16) if "dbg_gt" in TAPS else None
    dbg_h = scr("dbg_h", (1024, TOK), BF16) if "dbg_h" in TAPS else None
    dbg_hg = scr("dbg_hg", (1024, TOK), BF16) if "dbg_hg" in TAPS else None
    dbg_sc = scr("dbg_sc", (64, TOK), F32) if "dbg_sc" in TAPS else None
    dbg_gtok = scr("dbg_gtok", (TOK, 64), F32) if "dbg_gtok" in TAPS else None
    dbg_stok = scr("dbg_stok", (TOK, 64), F32) if "dbg_stok" in TAPS else None
    dbg_cur = scr("dbg_cur", (TOK, 64), F32) if "dbg_cur" in TAPS else None
    dbg_e = scr("dbg_e", (TOK, 64), F32) if "dbg_e" in TAPS else None
    dbg_mx = scr("dbg_mx", (TOK, 8), F32) if "dbg_mx" in TAPS else None
    dbg_at = scr("dbg_at", (D, TOK), F32) if "dbg_at" in TAPS else None
    dbg_attT = scr("dbg_attT", (D, 512), BF16) if "dbg_attT" in TAPS else None

    P = Prog()
    P.extra_sems = ["cc1", "cc2"]

    with ExitStack() as stack:
        E = stack.enter_context
        af_h = E(nc.sbuf_tensor("arena_f", [128, 10240], F32))
        ab_h = E(nc.sbuf_tensor("arena_b", [128, 69632], BF16))
        AFa = Arena(af_h, 10240, "f32")
        ABa = Arena(ab_h, 69632, "bf16")

        id_bf = Tile(E(nc.sbuf_tensor("id_bf", [128, 128], BF16)), 128, 0, 128)
        id_f = Tile(E(nc.sbuf_tensor("id_f", [128, 128], F32)), 128, 0, 128)
        mtri = Tile(E(nc.sbuf_tensor("mtri", [128, 128], BF16)), 128, 0, 128)
        e0_sb = Tile(E(nc.sbuf_tensor("e0_sb", [64, 1024], BF16)), 64, 0, 1024)
        e1_sb = Tile(E(nc.sbuf_tensor("e1_sb", [8, 128], BF16)), 8, 0, 128)
        iota_sb = Tile(E(nc.sbuf_tensor("iota_sb", [15, 8], F32)), 15, 0, 8)
        cst = Tile(E(nc.sbuf_tensor("cst", [128, 2], F32)), 128, 0, 2)
        em_i = Tile(E(nc.sbuf_tensor("em_i", [15, 1], I32)), 15, 0, 1)
        em_f = Tile(E(nc.sbuf_tensor("em_f", [15, 1], F32)), 15, 0, 1)
        onehot = Tile(E(nc.sbuf_tensor("onehot", [15, 8], F32)), 15, 0, 8)
        onehot_bf = Tile(E(nc.sbuf_tensor("onehot_bf", [15, 8], BF16)), 15, 0, 8)
        ones_bf = Tile(E(nc.sbuf_tensor("ones_b", [1, 128], BF16)), 1, 0, 128)
        onec_bf = Tile(E(nc.sbuf_tensor("onec_b", [128, 1], BF16)), 128, 0, 1)
        ones_f = Tile(E(nc.sbuf_tensor("ones_f", [1, 128], F32)), 1, 0, 128)
        bqkv_sb = Tile(E(nc.sbuf_tensor("bqkv", [128, 12], F32)), 128, 0, 12)
        bo0_sb = Tile(E(nc.sbuf_tensor("bo0", [128, 32], F32)), 128, 0, 32)
        bq1_sb = Tile(E(nc.sbuf_tensor("bq1", [128, 8], F32)), 128, 0, 8)
        bb1_sb = Tile(E(nc.sbuf_tensor("bb1", [128, 32], F32)), 128, 0, 32)

        z1_f = Tile(E(nc.sbuf_tensor("z1f", [128, 8], F32)), 128, 0, 8)
        z1_bf = Tile(E(nc.sbuf_tensor("z1b", [128, 8], BF16)), 128, 0, 8)
        m1_bf = Tile(E(nc.sbuf_tensor("m1b", [128, 1024], BF16)), 128, 0, 1024)
        wr1_f = Tile(E(nc.sbuf_tensor("wr1f", [128, 480], F32)), 128, 0, 480, "wr1f")
        onec_f = Tile(E(nc.sbuf_tensor("onec_f", [128, 1], F32)), 128, 0, 1, "onecf")
        s0b = Tile(E(nc.sbuf_tensor("s0b", [128, 64], F32)), 128, 0, 64, "s0b")
        s1b = Tile(E(nc.sbuf_tensor("s1b", [128, 15], F32)), 128, 0, 15, "s1b")
        rb1b = Tile(E(nc.sbuf_tensor("rb1b", [128, 15], F32)), 128, 0, 15, "rb1b")
        srow = Tile(E(nc.sbuf_tensor("srow", [1, 80], F32)), 1, 0, 80, "srow")
        sc0tok = Tile(E(nc.sbuf_tensor("sc0tok", [128, 8 * 64], F32)), 128, 0, 512, "sc0tok")
        abcol = Tile(E(nc.sbuf_tensor("abcol", [128, 8], F32)), 128, 0, 8, "abcol")
        sc0v = [Tile(sc0tok.h, 128, sc0tok.c0 + i * 64, 64, f"sc0v{i}")
                for i in range(8)]
        lg1tok = Tile(E(nc.sbuf_tensor("lg1tok", [128, 64], F32)), 128, 0, 64, "lg1t")
        lg1v = [Tile(lg1tok.h, 128, lg1tok.c0 + i * 16, 16, f"lg1v{i}")
                for i in range(4)]
        a1_bf = Tile(E(nc.sbuf_tensor("a1b", [128, 4096], BF16)), 128, 0, 4096)
        b1t_bf = Tile(E(nc.sbuf_tensor("b1tb", [128, 4096], BF16)), 128, 0, 4096)

        pA = Tile(E(nc.psum_tensor("pA", [128, 512], F32)), 128, 0, 512)
        pB = Tile(E(nc.psum_tensor("pB", [128, 512], F32)), 128, 0, 512)
        pC = Tile(E(nc.psum_tensor("pC", [128, 512], F32)), 128, 0, 512)
        pD = Tile(E(nc.psum_tensor("pD", [128, 512], F32)), 128, 0, 512)
        pE_ = Tile(E(nc.psum_tensor("pE", [128, 512], F32)), 128, 0, 512)
        pF_ = Tile(E(nc.psum_tensor("pF", [128, 512], F32)), 128, 0, 512)
        pT = Tile(E(nc.psum_tensor("pT", [128, 512], BF16)), 128, 0, 512)

        P.dma("sync", id_bf.ap(), ident_bf, outs=[id_bf])
        P.dma("sync", id_f.ap(), ident_f, outs=[id_f])
        P.dma("sync", mtri.ap(), masktri, outs=[mtri])
        P.dma("sync", e0_sb.ap(), E0c, outs=[e0_sb])
        P.dma("sync", e1_sb.ap(), E1c, outs=[e1_sb])
        P.dma("sync", iota_sb.ap(), iota15, outs=[iota_sb])
        P.dma("sync", cst.ap(), cstc, outs=[cst])
        P.dma("sync", em_i.ap(), emap, outs=[em_i])
        P.dma("sync", bqkv_sb.ap(), b_qkv, outs=[bqkv_sb])
        P.dma("sync", bo0_sb.ap(), b_o0, outs=[bo0_sb])
        P.dma("sync", bq1_sb.ap(), b_q1, outs=[bq1_sb])
        P.dma("sync", bb1_sb.ap(), b_b1, outs=[bb1_sb])

        P.dma("sync", z1_f.ap(), w_z1, outs=[z1_f])
        P.op("dve", lambda e: e.tensor_copy(em_f.ap(), em_i.ap()), ins=[em_i], outs=[em_f])
        P.op("dve", lambda e: e.tensor_scalar(onehot.ap(), iota_sb.ap(), em_f.ap(), None,
                                              ALU.is_equal),
             ins=[iota_sb, em_f], outs=[onehot])
        P.op("dve", lambda e: e.tensor_copy(onehot_bf.ap(), onehot.ap()),
             ins=[onehot], outs=[onehot_bf])
        P.op("dve", lambda e: e.memset(ones_bf.ap(), 1.0), outs=[ones_bf])
        P.op("dve", lambda e: e.memset(onec_bf.ap(), 1.0), outs=[onec_bf])
        P.op("dve", lambda e: e.memset(ones_f.ap(), 1.0), outs=[ones_f])
        P.op("dve", lambda e: e.memset(onec_f.ap(), 1.0), outs=[onec_f])
        P.op("dve", lambda e: e.tensor_copy(z1_bf.ap(), z1_f.ap()), ins=[z1_f], outs=[z1_bf])

        AFa.push()
        wst0 = AFa.alloc(2048, name="ph0stage")

        def stage_convert(dram, nrows, ncols, dst_tile, tag, eng="pool"):
            nch = nrows // 128
            per = max(1, 2048 // ncols)
            s = 0
            rnd = 0
            while s < nch:
                cur = min(per, nch - s)
                src = dram[s * 128:(s + cur) * 128, 0:ncols].rearrange(
                    "(c p) n -> p c n", p=128)
                P.dma("sync", wst0.ap(c=(0, cur * ncols)).rearrange(
                    "p (c n) -> p c n", n=ncols), src, outs=[wst0])
                P.op(eng, lambda e, s=s, cur=cur: e.tensor_copy(
                    dst_tile.ap(c=(s * ncols, (s + cur) * ncols)),
                    wst0.ap(c=(0, cur * ncols))),
                     ins=[wst0], outs=[(dst_tile, "c", s)])
                s += cur
                rnd += 1

        srcw = w_r1.rearrange("(c p) n -> p c n", p=128)
        P.dma("sync", wr1_f.ap().rearrange("p (c n) -> p c n", n=15), srcw,
              outs=[wr1_f])
        def emit_s1(e):
            last = None
            for i in range(NCH):
                last = e.matmul(pD.ap(p=(0, 1), c=(0, 15)), onec_f.ap(),
                                wr1_f.ap(c=(i * 15, (i + 1) * 15)),
                                start=(i == 0), stop=(i == NCH - 1))
            return last
        P.op("pe", emit_s1, ins=[onec_f, wr1_f], outs=[pD])
        P.op("dve", lambda e: e.tensor_copy(srow.ap(p=(0, 1), c=(64, 79)),
                                            pD.ap(p=(0, 1), c=(0, 15))),
             ins=[pD], outs=[(srow, 1)])
        P.op("pe", lambda e: e.matmul(pD.ap(c=(0, 15)), ones_f.ap(),
                                      srow.ap(p=(0, 1), c=(64, 79)),
                                      start=True, stop=True),
             ins=[ones_f, (srow, 1)], outs=[pD])
        P.op("act", lambda e: e.activation(s1b.ap(), pD.ap(c=(0, 15)), AF.Identity),
             ins=[pD], outs=[s1b])
        P.dma("sync", srow.ap(p=(0, 1), c=(64, 79)), b_r1row, outs=[(srow, 1)])
        P.op("pe", lambda e: e.matmul(pC.ap(c=(0, 15)), ones_f.ap(),
                                      srow.ap(p=(0, 1), c=(64, 79)),
                                      start=True, stop=True),
             ins=[ones_f, (srow, 1)], outs=[pC])
        P.op("act", lambda e: e.activation(rb1b.ap(), pC.ap(c=(0, 15)), AF.Identity),
             ins=[pC], outs=[rb1b])
        stage_convert(w_A1, D, 128, a1_bf, "a1")
        for s2 in range(2):
            P.dma("sync", wst0.ap(), w_B1[:, s2 * 2048:(s2 + 1) * 2048], outs=[wst0])
            P.op("pool", lambda e, s2=s2: e.tensor_copy(
                b1t_bf.ap(c=(s2 * 2048, (s2 + 1) * 2048)), wst0.ap()),
                 ins=[wst0], outs=[(b1t_bf, "c", 0)])
        stage_convert(w_M1, 1024, 128, m1_bf, "m1")
        AFa.pop()
        P.barrier()
        AFa.push()
        wr0_f = AFa.alloc(2048, name="wr0f")
        for s4 in range(4):
            srcw = w_r0[s4 * 1024:(s4 + 1) * 1024, :].rearrange("(c p) n -> p c n", p=128)
            P.dma("sync", wr0_f.ap(c=(s4 * 512, (s4 + 1) * 512)).rearrange(
                "p (c n) -> p c n", n=64), srcw, outs=[(wr0_f, s4)])
        def emit_s0(e):
            last = None
            for i in range(NCH):
                last = e.matmul(pC.ap(p=(0, 1), c=(0, 64)), onec_f.ap(),
                                wr0_f.ap(c=(i * 64, (i + 1) * 64)),
                                start=(i == 0), stop=(i == NCH - 1))
            return last
        P.op("pe", emit_s0, ins=[onec_f] + [(wr0_f, s4) for s4 in range(4)], outs=[pC])
        P.op("dve", lambda e: e.tensor_copy(srow.ap(p=(0, 1), c=(0, 64)),
                                            pC.ap(p=(0, 1), c=(0, 64))),
             ins=[pC], outs=[(srow, 0)])
        P.op("pe", lambda e: e.matmul(pC.ap(c=(0, 64)), ones_f.ap(),
                                      srow.ap(p=(0, 1), c=(0, 64)),
                                      start=True, stop=True),
             ins=[ones_f, (srow, 0)], outs=[pC])
        P.op("act", lambda e: e.activation(s0b.ap(), pC.ap(c=(0, 64)), AF.Identity),
             ins=[pC], outs=[s0b])
        P.barrier()

        def rk(t, nrows, ncols):
            nch = nrows // 128
            per = max(1, 2048 // ncols)
            return [(t, "c", s) for s in range(0, nch, per)]

        A1K = rk(a1_bf, D, 128)
        B1K = rk(b1t_bf, 128, D)
        M1K = rk(m1_bf, 1024, 128)

        def layer_norm(load_chunk, write_xn, stats_done=False, router=None,
                       sub_out=None):
            xa = AFa.allocs(2, 512, name="lnx")
            sqa = ABa.allocs(2, 512, name="lnsq")
            xba = ABa.allocs(2, 512, name="lnxb")
            stat = AFa.alloc(2048, p=1, name="lnstat")
            bca = AFa.alloc(512, name="lnbca")
            bcb = AFa.alloc(512, name="lnbcb")
            tmp = AFa.alloc(512, name="lntmp")
            if not stats_done:
                for i in range(NCH):
                    xt = xa[i % 2]
                    load_chunk(i, xt)
                    sq = sqa[i % 2]
                    xb = xba[i % 2]
                    P.op("act", lambda e, s=sq, x=xt: e.activation(s.ap(), x.ap(), AF.Square),
                         ins=[xt], outs=[sq])
                    P.op("pool", lambda e, b=xb, x=xt: e.tensor_copy(b.ap(), x.ap()),
                         ins=[xt], outs=[xb])

                    def emit_s(e, xb=xb, i=i):
                        return e.matmul(pE_.ap(p=(0, 1)), onec_bf.ap(), xb.ap(),
                                        start=(i == 0), stop=(i == NCH - 1),
                                        skip_group_check=True)
                    P.op("pe", emit_s, ins=[onec_bf, xb],
                         outs=[pE_] if i in (0, NCH - 1) else ())

                    def emit_q(e, sq=sq, i=i):
                        return e.matmul(pF_.ap(p=(0, 1)), onec_bf.ap(), sq.ap(),
                                        start=(i == 0), stop=(i == NCH - 1),
                                        skip_group_check=True)
                    P.op("pe", emit_q, ins=[onec_bf, sq],
                         outs=[pF_] if i in (0, NCH - 1) else ())
            sm, sq_, sv, sr = (0, 512), (512, 1024), (1024, 1536), (1536, 2048)
            P.op("dve", lambda e: e.tensor_scalar_mul(stat.ap(c=sm, p=(0, 1)),
                                                      pE_.ap(p=(0, 1)), 1.0 / D),
                 ins=[pE_], outs=[(stat, "m")])
            P.op("dve", lambda e: e.tensor_scalar_mul(stat.ap(c=sq_, p=(0, 1)),
                                                      pF_.ap(p=(0, 1)), 1.0 / D),
                 ins=[pF_], outs=[(stat, "q")])
            P.op("act", lambda e: e.activation(stat.ap(c=sv, p=(0, 1)),
                                               stat.ap(c=sm, p=(0, 1)), AF.Square),
                 ins=[(stat, "m")], outs=[(stat, "v")])
            P.op("dve", lambda e: e.tensor_sub(stat.ap(c=sv, p=(0, 1)),
                                               stat.ap(c=sq_, p=(0, 1)),
                                               stat.ap(c=sv, p=(0, 1))),
                 ins=[(stat, "q"), (stat, "v")], outs=[(stat, "v")])
            P.op("act", lambda e: e.activation(stat.ap(c=sr, p=(0, 1)),
                                               stat.ap(c=sv, p=(0, 1)), AF.Ln,
                                               bias=cst.ap(p=(0, 1), c=(0, 1))),
                 ins=[(stat, "v"), cst], outs=[(stat, "r")])
            P.op("act", lambda e: e.activation(stat.ap(c=sr, p=(0, 1)),
                                               stat.ap(c=sr, p=(0, 1)), AF.Exp, scale=-0.5),
                 ins=[(stat, "r")], outs=[(stat, "r")])
            P.op("dve", lambda e: e.scalar_tensor_tensor(stat.ap(c=sv, p=(0, 1)),
                                                         stat.ap(c=sm, p=(0, 1)), -1.0,
                                                         stat.ap(c=sr, p=(0, 1)),
                                                         ALU.mult, ALU.mult),
                 ins=[(stat, "m"), (stat, "r")], outs=[(stat, "v")])
            P.op("pe", lambda e: e.matmul(pE_.ap(), ones_f.ap(),
                                          stat.ap(c=sr, p=(0, 1)),
                                          start=True, stop=True),
                 ins=[ones_f, (stat, "r")], outs=[pE_])
            P.op("pe", lambda e: e.matmul(pF_.ap(), ones_f.ap(),
                                          stat.ap(c=sv, p=(0, 1)),
                                          start=True, stop=True),
                 ins=[ones_f, (stat, "v")], outs=[pF_])
            P.op("act", lambda e: e.activation(bca.ap(), pE_.ap(), AF.Identity),
                 ins=[pE_], outs=[bca])
            P.op("act", lambda e: e.activation(bcb.ap(), pF_.ap(), AF.Identity),
                 ins=[pF_], outs=[bcb])
            for i in range(NCH):
                xt = xa[i % 2]
                load_chunk(i, xt)
                if router is not None:
                    wrt, ncols = router
                    def emit_r(e, xt=xt, i=i, wrt=wrt, ncols=ncols):
                        return e.matmul(pC.ap(p=(0, ncols), c=(0, 512)),
                                        wrt.ap(c=(i * ncols, (i + 1) * ncols)),
                                        xt.ap(), start=(i == 0), stop=(i == NCH - 1),
                                        skip_group_check=True)
                    P.op("pe", emit_r, ins=[wrt, xt],
                         outs=[pC] if i in (0, NCH - 1) else ())
                P.op("dve", lambda e, x=xt: e.tensor_mul(tmp.ap(), x.ap(), bca.ap()),
                     ins=[xt, bca], outs=[tmp])
                write_xn(i, tmp, bcb)
            if router is not None:
                wrt, ncols = router
                rT = AFa.alloc(512, p=ncols, name="routT")
                rtok = AFa.alloc(64, name="rtok")
                t1r = AFa.alloc(64, name="t1r")
                P.op("dve", lambda e: e.tensor_copy(rT.ap(), pC.ap(p=(0, ncols))),
                     ins=[pC], outs=[rT])
                sbt = s0b if ncols == 64 else s1b
                for sub in range(4):
                    def emit_ta(e, sub=sub):
                        return e.transpose(pD.ap(c=(0, 1)),
                                           stat.ap(c=(sr[0] + sub * 128,
                                                      sr[0] + sub * 128 + 128), p=(0, 1)),
                                           id_f.ap(p=(0, 1), c=(0, 1)))
                    P.op("pe", emit_ta, ins=[(stat, "r"), id_f], outs=[pD])
                    P.op("act", lambda e, sub=sub: e.activation(
                        abcol.ap(c=(sub, sub + 1)), pD.ap(c=(0, 1)), AF.Identity),
                         ins=[pD], outs=[(abcol, "a", sub)])
                    def emit_tb(e, sub=sub):
                        return e.transpose(pD.ap(c=(1, 2)),
                                           stat.ap(c=(sv[0] + sub * 128,
                                                      sv[0] + sub * 128 + 128), p=(0, 1)),
                                           id_f.ap(p=(0, 1), c=(0, 1)))
                    P.op("pe", emit_tb, ins=[(stat, "v"), id_f], outs=[pD])
                    P.op("act", lambda e, sub=sub: e.activation(
                        abcol.ap(c=(4 + sub, 5 + sub)), pD.ap(c=(1, 2)), AF.Identity),
                         ins=[pD], outs=[(abcol, "b", sub)])
                    def emit_tr(e, sub=sub, ncols=ncols):
                        return e.transpose(pD.ap(c=(128, 128 + ncols), p=(0, 128)),
                                           rT.ap(c=(sub * 128, sub * 128 + 128)),
                                           id_f.ap(p=(0, ncols), c=(0, ncols)))
                    P.op("pe", emit_tr, ins=[rT, id_f], outs=[pD])
                    P.op("act", lambda e, ncols=ncols: e.activation(
                        rtok.ap(c=(0, ncols)), pD.ap(c=(128, 128 + ncols)), AF.Identity),
                         ins=[pD], outs=[rtok])
                    P.op("dve", lambda e, sub=sub, ncols=ncols: e.tensor_scalar(
                        rtok.ap(c=(0, ncols)), rtok.ap(c=(0, ncols)),
                        abcol.ap(c=(sub, sub + 1)), None, ALU.mult),
                         ins=[rtok, (abcol, "a", sub)], outs=[rtok])
                    P.op("dve", lambda e, sub=sub, ncols=ncols, sbt=sbt: e.tensor_scalar(
                        t1r.ap(c=(0, ncols)), sbt.ap(), abcol.ap(c=(4 + sub, 5 + sub)),
                        None, ALU.mult),
                         ins=[sbt, (abcol, "b", sub)], outs=[t1r])
                    P.op("dve", lambda e, ncols=ncols: e.tensor_add(
                        rtok.ap(c=(0, ncols)), rtok.ap(c=(0, ncols)),
                        t1r.ap(c=(0, ncols))),
                         ins=[rtok, t1r], outs=[rtok])
                    sub_out(sub, rtok)

        ABa.push()
        xnb = ABa.allocs(NCH, 1024, name="xnb")
        for t2 in range(2):
            AFa.push()
            ABa.push()

            def loader(i, dst, t2=t2):
                P.dma("sync", dst.ap(), xT[i * 128:(i + 1) * 128, t2 * 512:(t2 + 1) * 512],
                      outs=[dst])

            def write_xn(i, tmp, bcb, t2=t2):
                xn = xnb[i]
                P.op("dve", lambda e, xn=xn, tmp=tmp, bcb=bcb, t2=t2: e.tensor_add(
                    xn.ap(c=(t2 * 512, t2 * 512 + 512)), tmp.ap(), bcb.ap()),
                     ins=[tmp, bcb], outs=[(xn, t2)])
                P.dma("pool", g1_in[i * 128:(i + 1) * 128, t2 * 512:(t2 + 1) * 512],
                      xn.ap(c=(t2 * 512, t2 * 512 + 512)), ins=[(xn, t2)],
                      outs=[("g1in", i, t2)])
            def sub_out0(sub, rtok, t2=t2):
                idx = t2 * 4 + sub
                P.op("dve", lambda e, v=sc0v[idx], rtok=rtok: e.tensor_copy(
                    v.ap(), rtok.ap(c=(0, 64))), ins=[rtok], outs=[sc0v[idx]])
            layer_norm(loader, write_xn, router=(wr0_f, 64), sub_out=sub_out0)
            AFa.pop()
            ABa.pop()
            P.barrier()

        def emit_cc1(e):
            return e.collective_compute(
                "AllGather", ALU.bypass, replica_groups=[list(range(NCORE))],
                ins=[g1_in], outs=[g1_out])
        P.barrier()
        P.op("pool", emit_cc1,
             ins=[("g1in", i, t2) for i in range(NCH) for t2 in range(2)],
             outs=[("g1out",)], sem="cc1", inc=1)
        P._wait("sync", "cc1", 1)
        P._wait("pool", "cc1", 1)

        ABa.push()
        gt_sb = ABa.alloc(1024, p=64, name="GT")
        hg_sb = ABa.allocs(8, 1024, name="hg")
        AFa.push()
        sc_cur = AFa.alloc(64, name="sccur")
        mxc = AFa.alloc(8, name="mxc")
        gat = AFa.alloc(64, name="gat")
        for t2 in range(2):
            for sub in range(4):
                stt = sc0v[t2 * 4 + sub]
                P.op("dve", lambda e, stt=stt: e.tensor_copy(sc_cur.ap(), stt.ap()),
                     ins=[stt], outs=[sc_cur])
                if dbg_stok is not None:
                    r0 = t2 * 512 + sub * 128
                    P.dma("sync", dbg_stok[r0:r0 + 128, :], stt.ap(),
                          ins=[stt], outs=[("tap", "dbg_stok", t2, sub)])
                for it in range(16):
                    P.op("dve", lambda e: e.reduce_max(mxc.ap(c=(0, 1)), sc_cur.ap(), axis=AX),
                         ins=[sc_cur], outs=[(mxc, "mx")])
                    if it == 0:
                        P.op("dve", lambda e: e.tensor_scalar_mul(mxc.ap(c=(1, 2)),
                                                                  mxc.ap(c=(0, 1)), -1.0),
                             ins=[(mxc, "mx")], outs=[(mxc, "nm0")])
                    if it < 15:
                        P.op("dve", lambda e: e.tensor_scalar(gat.ap(), sc_cur.ap(),
                                                              mxc.ap(c=(0, 1)), None,
                                                              ALU.is_ge),
                             ins=[sc_cur, (mxc, "mx")], outs=[gat])
                        P.op("dve", lambda e: e.scalar_tensor_tensor(
                            sc_cur.ap(), gat.ap(), -1e30, sc_cur.ap(), ALU.mult, ALU.add),
                             ins=[gat, sc_cur], outs=[sc_cur])
                P.op("act", lambda e, stt=stt: e.activation(gat.ap(), stt.ap(), AF.Exp,
                                                            bias=mxc.ap(c=(1, 2))),
                     ins=[stt, (mxc, "nm0")], outs=[gat])
                if dbg_e is not None:
                    r0 = t2 * 512 + sub * 128
                    P.dma("sync", dbg_e[r0:r0 + 128, :], gat.ap(),
                          ins=[gat], outs=[("tap", "dbg_e", t2, sub)])
                P.op("dve", lambda e, stt=stt: e.tensor_scalar(sc_cur.ap(), stt.ap(),
                                                               mxc.ap(c=(0, 1)), None,
                                                               ALU.is_ge),
                     ins=[stt, (mxc, "mx")], outs=[sc_cur])
                if dbg_cur is not None:
                    r0 = t2 * 512 + sub * 128
                    P.dma("sync", dbg_cur[r0:r0 + 128, :], sc_cur.ap(),
                          ins=[sc_cur], outs=[("tap", "dbg_cur", t2, sub)])
                P.op("dve", lambda e: e.tensor_mul(gat.ap(), gat.ap(), sc_cur.ap()),
                     ins=[gat, sc_cur], outs=[gat])
                P.op("dve", lambda e: e.reduce_sum(mxc.ap(c=(2, 3)), gat.ap(), axis=AX),
                     ins=[gat], outs=[(mxc, "s")])
                P.op("dve", lambda e: e.reciprocal(mxc.ap(c=(3, 4)), mxc.ap(c=(2, 3))),
                     ins=[(mxc, "s")], outs=[(mxc, "rs")])
                P.op("dve", lambda e: e.tensor_scalar(gat.ap(), gat.ap(), mxc.ap(c=(3, 4)),
                                                      None, ALU.mult),
                     ins=[gat, (mxc, "rs")], outs=[gat])
                if dbg_mx is not None:
                    r0 = t2 * 512 + sub * 128
                    P.dma("sync", dbg_mx[r0:r0 + 128, :], mxc.ap(),
                          ins=[(mxc, "mx"), (mxc, "nm0"), (mxc, "s"), (mxc, "rs")],
                          outs=[("tap", "dbg_mx", t2, sub)])

                if dbg_gtok is not None:
                    r0 = t2 * 512 + sub * 128
                    P.dma("sync", dbg_gtok[r0:r0 + 128, :], gat.ap(),
                          ins=[gat], outs=[("tap", "dbg_gtok", t2, sub)])

                def emit_trg(e):
                    return e.transpose(pB.ap(p=(0, 64), c=(64, 192)), gat.ap(), id_f.ap())
                P.op("pe", emit_trg, ins=[gat, id_f], outs=[pB])
                P.op("act", lambda e, t2=t2, sub=sub: e.activation(
                    gt_sb.ap(c=(t2 * 512 + sub * 128, t2 * 512 + sub * 128 + 128)),
                    pB.ap(c=(64, 192), p=(0, 64)), AF.Identity),
                     ins=[pB], outs=[(gt_sb, t2, sub)])
        if dbg_gt is not None:
            P.dma("sync", dbg_gt, gt_sb.ap(),
                  ins=[(gt_sb, t2_, s_) for t2_ in range(2) for s_ in range(4)],
                  outs=[("tap", "dbg_gt")])
        ABa.push()
        a0st = ABa.allocs(2, NCH * 128, name="a0st")
        a0f = AFa.alloc(1024, name="a0fst")
        hb_t = ABa.alloc(512, name="hbt")
        for j in range(8):
            stt = a0st[j % 2]
            for s in range(4):
                src = w_A0[s * 1024:(s + 1) * 1024, j * 128:(j + 1) * 128].rearrange(
                    "(c p) n -> p c n", p=128)
                P.dma("sync", a0f.ap().rearrange("p (c n) -> p c n", n=128), src,
                      outs=[a0f])
                P.op("dve", lambda e, stt=stt, s=s: e.tensor_copy(
                    stt.ap(c=(s * 1024, (s + 1) * 1024)), a0f.ap()),
                     ins=[a0f], outs=[(stt, "c", s)])
            for t2 in range(2):
                pp = pA if t2 == 0 else pB
                pairs = [(stt.ap(c=(i * 128, i * 128 + 128)),
                          xnb[i].ap(c=(t2 * 512, t2 * 512 + 512))) for i in range(NCH)]
                _mmgroup(P, pp.ap(), pairs,
                         ins=[(stt, "c", s) for s in range(4)] +
                             [(xnb[i], t2) for i in range(NCH)],
                         outs=[pp])
                P.op("act", lambda e, pp=pp: e.activation(hb_t.ap(), pp.ap(), AF.Identity),
                     ins=[pp], outs=[hb_t])
                if dbg_h is not None:
                    P.dma("sync", dbg_h[j * 128:(j + 1) * 128,
                                        t2 * 512:(t2 + 1) * 512], hb_t.ap(),
                          ins=[hb_t], outs=[("tap", "dbg_h", j, t2)])

                def emit_ge(e, j=j, t2=t2):
                    return e.matmul(pC.ap(), e0_sb.ap(c=(j * 128, j * 128 + 128)),
                                    gt_sb.ap(c=(t2 * 512, t2 * 512 + 512)),
                                    start=True, stop=True)
                P.op("pe", emit_ge,
                     ins=[e0_sb] + [(gt_sb, t2, s) for s in range(4)], outs=[pC])
                P.op("dve", lambda e, j=j, t2=t2: e.tensor_tensor(
                    hg_sb[j].ap(c=(t2 * 512, t2 * 512 + 512)), hb_t.ap(), pC.ap(), ALU.mult),
                     ins=[hb_t, pC], outs=[(hg_sb[j], t2)])
                if dbg_hg is not None:
                    P.dma("sync", dbg_hg[j * 128:(j + 1) * 128,
                                         t2 * 512:(t2 + 1) * 512],
                          hg_sb[j].ap(c=(t2 * 512, t2 * 512 + 512)),
                          ins=[(hg_sb[j], t2)], outs=[("tap", "dbg_hg", j, t2)])
        b0st = ABa.allocs(2, 8 * 128, name="b0st")
        b0f = AFa.alloc(1024, name="b0fst")
        f0_t = AFa.allocs(2, 512, name="f0t")
        for oc in range(32):
            stt = b0st[oc % 2]
            src = w_B0[:, oc * 128:(oc + 1) * 128].rearrange("(c p) n -> p c n", p=128)
            P.dma("sync", b0f.ap().rearrange("p (c n) -> p c n", n=128), src,
                  outs=[b0f])
            P.op("dve", lambda e, stt=stt: e.tensor_copy(stt.ap(), b0f.ap()),
                 ins=[b0f], outs=[stt])
            for t2 in range(2):
                pairs = [(stt.ap(c=(j * 128, j * 128 + 128)),
                          hg_sb[j].ap(c=(t2 * 512, t2 * 512 + 512))) for j in range(8)]
                _mmgroup(P, pD.ap(), pairs,
                         ins=[stt] + [(hg_sb[j], t2) for j in range(8)], outs=[pD])
                ft = f0_t[t2]
                P.op("act", lambda e, ft=ft: e.activation(ft.ap(), pD.ap(), AF.Identity),
                     ins=[pD], outs=[ft])
                P.dma("pool", scr_f0[oc * 128:(oc + 1) * 128, t2 * 512:(t2 + 1) * 512],
                      ft.ap(), ins=[ft], outs=[("f0d", oc, t2)])
        AFa.pop()
        AFa.pop()
        ABa.pop()
        ABa.pop()
        ABa.pop()
        P.barrier()

        ABa.push()
        wq_st = ABa.allocs(6, NCH * 128, name="wqkv")
        xg = [ABa.allocs(NCH, 512, name=f"xg{k}_") for k in range(2)]
        qkv_ev = ABa.allocs(2, 512, name="qkvev")
        AFa.push()
        wstg = AFa.alloc(1024, name="wstg")
        for grp in range(2):
            for jj in range(6):
                ob = grp * 6 + jj
                stt = wq_st[jj]
                for s in range(4):
                    src = w_qkv[s * 1024:(s + 1) * 1024, ob * 128:(ob + 1) * 128].rearrange(
                        "(c p) n -> p c n", p=128)
                    P.dma("sync", wstg.ap().rearrange("p (c n) -> p c n", n=128), src,
                          outs=[wstg])
                    P.op("dve", lambda e, stt=stt, s=s: e.tensor_copy(
                        stt.ap(c=(s * 1024, (s + 1) * 1024)), wstg.ap()),
                         ins=[wstg], outs=[(stt, "c", s)])
            for tt in range(16):
                r, half = tt // 2, tt % 2
                xgt = xg[tt % 2]
                for s in range(4):
                    src = g1_out[r * D + s * 1024: r * D + (s + 1) * 1024,
                                 half * 512:(half + 1) * 512].rearrange(
                        "(c p) n -> p c n", p=128)

                    def emit_d(e, src=src, xgt=xgt, s=s):
                        t0 = xgt[s * 8]
                        dst = t0.h[:, t0.c0:t0.c0 + 8 * 512].rearrange(
                            "p (c n) -> p c n", n=512)
                        return e.dma_start(out=dst, in_=src)
                    P.op("sync", emit_d, ins=[("g1out",)],
                         outs=[xgt[s * 8 + c] for c in range(8)], inc=16)
                for jj in range(6):
                    ob = grp * 6 + jj
                    stt = wq_st[jj]
                    pp = pA if jj % 2 == 0 else pB
                    pairs = [(stt.ap(c=(i * 128, i * 128 + 128)), xgt[i].ap())
                             for i in range(NCH)]
                    _mmgroup(P, pp.ap(), pairs,
                             ins=[(stt, "c", s) for s in range(4)] + list(xgt), outs=[pp])
                    ev = qkv_ev[jj % 2]
                    P.op("act", lambda e, ev=ev, pp=pp, ob=ob: e.activation(
                        ev.ap(), pp.ap(), AF.Identity, bias=bqkv_sb.ap(c=(ob, ob + 1))),
                         ins=[pp, bqkv_sb], outs=[ev])
                    P.dma("pool", qkvT_d[ob * 128:(ob + 1) * 128, tt * 512:(tt + 1) * 512],
                          ev.ap(), ins=[ev], outs=[("qkvd", ob, tt)])
        AFa.pop()
        ABa.pop()
        P.barrier()

        ABa.push()
        qT = ABa.allocs(2, 2048, name="qT")
        kT = ABa.allocs(2, 2048, name="kT")
        vT = ABa.allocs(2, 2048, name="vT")
        vext = ABa.allocs(2, 16 * 129, name="vext")
        pT_sb = ABa.allocs(16, 512, name="pTsb")
        av_bf = ABa.allocs(2, 128, name="avbf")
        AFa.push()
        rz = AFa.allocs(2, 1, name="rz")
        for vx in vext:
            P.op("dve", lambda e, vx=vx: e.memset(vx.ap(), 1.0), outs=[vx])
        for sh in range(16):
            s, h = sh // 4, sh % 4
            par = sh % 2
            qt_t, kt_t, vt_t, vx = qT[par], kT[par], vT[par], vext[par]
            P.dma("sync", qt_t.ap(),
                  qkvT_d[h * 128:(h + 1) * 128, s * 2048:(s + 1) * 2048],
                  ins=[("qkvd", h, 4 * s + k) for k in range(4)], outs=[qt_t])
            P.dma("sync", kt_t.ap(),
                  qkvT_d[(4 + h) * 128:(5 + h) * 128, s * 2048:(s + 1) * 2048],
                  ins=[("qkvd", 4 + h, 4 * s + k) for k in range(4)], outs=[kt_t])
            P.dma("sync", vt_t.ap(),
                  qkvT_d[(8 + h) * 128:(9 + h) * 128, s * 2048:(s + 1) * 2048],
                  ins=[("qkvd", 8 + h, 4 * s + k) for k in range(4)], outs=[vt_t])
            for kc in range(16):
                def emit_vtr(e, vt_t=vt_t, kc=kc):
                    return e.transpose(pT.ap(c=(0, 128)),
                                       vt_t.ap(c=(kc * 128, (kc + 1) * 128)), id_bf.ap())
                P.op("pe", emit_vtr, ins=[vt_t, id_bf], outs=[pT])
                P.op("act", lambda e, vx=vx, kc=kc: e.activation(
                    vx.ap(c=(kc * 129, kc * 129 + 128)), pT.ap(c=(0, 128)), AF.Identity),
                     ins=[pT], outs=[(vx, kc)])
            for qg in range(4):
                for kc in range(4 * qg + 4):
                    pp = pA if kc % 2 == 0 else pB
                    def emit_sc(e, kt_t=kt_t, qt_t=qt_t, kc=kc, qg=qg, pp=pp):
                        return e.matmul(pp.ap(), kt_t.ap(c=(kc * 128, (kc + 1) * 128)),
                                        qt_t.ap(c=(qg * 512, (qg + 1) * 512)),
                                        start=True, stop=True)
                    P.op("pe", emit_sc, ins=[kt_t, qt_t], outs=[pp])
                    pt = pT_sb[kc]
                    P.op("act", lambda e, pt=pt, pp=pp: e.activation(
                        pt.ap(), pp.ap(), AF.Exp, bias=cst.ap(c=(1, 2)), scale=RSQ),
                         ins=[pp, cst], outs=[pt])
                    if kc >= 4 * qg:
                        rr = kc - 4 * qg
                        P.op("dve", lambda e, pt=pt, rr=rr: e.tensor_tensor(
                            pt.ap(c=(rr * 128, rr * 128 + 128)),
                            pt.ap(c=(rr * 128, rr * 128 + 128)), mtri.ap(), ALU.mult),
                             ins=[pt, mtri], outs=[pt])
                for sub in range(4):
                    qt_i = qg * 4 + sub
                    pp = pC if sub % 2 == 0 else pD
                    pairs = [(pT_sb[kc].ap(c=(sub * 128, sub * 128 + 128)),
                              vx.ap(c=(kc * 129, kc * 129 + 129)))
                             for kc in range(qt_i + 1)]
                    _mmgroup(P, pp.ap(c=(0, 129)), pairs,
                             ins=[pT_sb[kc] for kc in range(qt_i + 1)] +
                                 [(vx, kc) for kc in range(qt_i + 1)] + [vx],
                             outs=[pp])
                    rzt = rz[sub % 2]
                    P.op("dve", lambda e, rzt=rzt, pp=pp: e.reciprocal(
                        rzt.ap(), pp.ap(c=(128, 129))), ins=[pp], outs=[rzt])
                    ab = av_bf[sub % 2]
                    P.op("act", lambda e, ab=ab, pp=pp, rzt=rzt: e.activation(
                        ab.ap(), pp.ap(c=(0, 128)), AF.Identity, scale=rzt.ap()),
                         ins=[pp, rzt], outs=[ab])
                    P.dma("pool",
                          a2a_in[s * 2048 + qt_i * 128: s * 2048 + (qt_i + 1) * 128,
                                 h * 128:(h + 1) * 128],
                          ab.ap(), ins=[ab], outs=[("a2ain", sh, qt_i)])
        AFa.pop()
        ABa.pop()

        def emit_cc2(e):
            with allow_ici_all_to_all():
                return e.collective_compute(
                    "AllToAll", ALU.bypass, replica_groups=[list(range(NCORE))],
                    ins=[a2a_in], outs=[a2a_out])
        P.barrier()
        P.op("pool", emit_cc2,
             ins=[("a2ain", sh, q) for sh in range(16) for q in range(16)],
             outs=[("a2aout",)], sem="cc2", inc=1)
        P._wait("sync", "cc2", 1)
        P._wait("pool", "cc2", 1)
        P.barrier()

        xn1 = ABa.allocs(NCH, 1024, name="xn1")
        ret_sb = ABa.allocs(8, 1024, name="ret")
        hg1_sb = ABa.alloc(1024, name="hg1")
        wsA = ABa.allocs(2, 16 * 128, name="wsA")
        wsB = ABa.allocs(2, 16 * 128, name="wsB")
        AFa.push()
        wfst = AFa.alloc(1024, name="wfst")
        to_t = AFa.allocs(2, 512, name="tot")
        xf_t = AFa.allocs(2, 512, name="xft")
        x1w = AFa.allocs(2, 512, name="x1w")

        def load_half(dram, oc, half, stt, tag):
            for s in range(2):
                r0 = half * 2048 + s * 1024
                src = dram[r0:r0 + 1024, oc * 128:(oc + 1) * 128].rearrange(
                    "(c p) n -> p c n", p=128)
                P.dma("sync", wfst.ap().rearrange("p (c n) -> p c n", n=128)
                      if False else wfst.ap(c=(0, 1024)).rearrange("p (c n) -> p c n", n=128),
                      src, outs=[wfst])
                P.op("dve", lambda e, stt=stt, s=s: e.tensor_copy(
                    stt.ap(c=(s * 1024, (s + 1) * 1024)), wfst.ap(c=(0, 1024))),
                     ins=[wfst], outs=[(stt, "c", s)])

        def proj_mm(P, dram, oc, bufs, tag, rhs_of_chunk, pp, final_stop=True):
            for half in range(2):
                stt = bufs[half]
                load_half(dram, oc, half, stt, tag)
                pairs = [(stt.ap(c=((i - half * 16) * 128, (i - half * 16) * 128 + 128)),
                          rhs_of_chunk(i)[0]) for i in range(half * 16, half * 16 + 16)]
                ins = [(stt, "c", s) for s in range(2)] + \
                      [rhs_of_chunk(i)[1] for i in range(half * 16, half * 16 + 16)]
                _mmgroup(P, pp.ap(), pairs, ins=ins, outs=[pp],
                         first=(half == 0), last=(final_stop and half == 1))

        for t2 in range(2):
            ABa.push()
            attT = ABa.allocs(NCH, 512, name="attT")
            a2t = ABa.allocs(2, 512, name="a2t")
            sqa6 = ABa.allocs(2, 512, name="sq6")
            xba6 = ABa.allocs(2, 512, name="xb6")
            if DBG_ATT_DRAM:
                for i_ in range(NCH):
                    P.dma("sync", attT[i_].ap(),
                          dbg_attin[i_ * 128:(i_ + 1) * 128,
                                    t2 * 512:(t2 + 1) * 512],
                          outs=[attT[i_]])
            else:
              for r in range(NCORE):
                for ts in range(4):
                    a2 = a2t[ts % 2]
                    row0 = r * 1024 + t2 * 512 + ts * 128
                    P.dma("sync", a2.ap(), a2a_out[row0:row0 + 128, :],
                          ins=[("a2aout",)], outs=[a2])
                    for rs in range(4):
                        def emit_at(e, a2=a2, rs=rs):
                            return e.transpose(pT.ap(c=(rs * 128, rs * 128 + 128)),
                                               a2.ap(c=(rs * 128, rs * 128 + 128)),
                                               id_bf.ap())
                        P.op("pe", emit_at, ins=[a2, id_bf], outs=[pT])
                        ch = 4 * r + rs
                        P.op("act", lambda e, ch=ch, ts=ts, rs=rs: e.activation(
                            attT[ch].ap(c=(ts * 128, ts * 128 + 128)),
                            pT.ap(c=(rs * 128, rs * 128 + 128)), AF.Identity),
                             ins=[pT], outs=[attT[ch]])
            if dbg_attT is not None and t2 == 0:
                for i_ in range(NCH):
                    P.dma("sync", dbg_attT[i_ * 128:(i_ + 1) * 128, :], attT[i_].ap(),
                          ins=[attT[i_]], outs=[("tap", "dbg_attT", i_)])
            for oc in range(32):
                pp = pA if oc % 2 == 0 else pB
                proj_mm(P, w_o0, oc, wsA, "wo0" + str(t2),
                        lambda i: (attT[i].ap(), attT[i]), pp)
                tt = to_t[oc % 2]
                P.op("act", lambda e, tt=tt, pp=pp, oc=oc: e.activation(
                    tt.ap(), pp.ap(), AF.Identity, bias=bo0_sb.ap(c=(oc, oc + 1))),
                     ins=[pp, bo0_sb], outs=[tt])
                if dbg_at is not None:
                    P.dma("sync", dbg_at[oc * 128:(oc + 1) * 128,
                                         t2 * 512:(t2 + 1) * 512], tt.ap(),
                          ins=[tt], outs=[("tap", "dbg_at", oc, t2)])
                if DBG_SKIP_X1:
                    continue
                xf = xf_t[oc % 2]
                P.dma("sync", xf.ap(),
                      xT[oc * 128:(oc + 1) * 128, t2 * 512:(t2 + 1) * 512], outs=[xf])
                x1t = x1w[oc % 2]
                P.op("dve", lambda e, x1t=x1t, tt=tt, xf=xf: e.tensor_add(
                    x1t.ap(), tt.ap(), xf.ap()), ins=[tt, xf], outs=[x1t])
                P.dma("sync", xf.ap(),
                      scr_f0[oc * 128:(oc + 1) * 128, t2 * 512:(t2 + 1) * 512],
                      ins=[("f0d", oc, t2)], outs=[xf])
                P.op("dve", lambda e, x1t=x1t, xf=xf: e.tensor_add(
                    x1t.ap(), x1t.ap(), xf.ap()), ins=[x1t, xf], outs=[x1t])
                P.dma("pool", scr_x1[oc * 128:(oc + 1) * 128, t2 * 512:(t2 + 1) * 512],
                      x1t.ap(), ins=[x1t], outs=[("x1d", oc, t2)])
                sq = sqa6[oc % 2]
                xb = xba6[oc % 2]
                P.op("act", lambda e, sq=sq, x1t=x1t: e.activation(sq.ap(), x1t.ap(),
                                                                   AF.Square),
                     ins=[x1t], outs=[sq])
                P.op("pool", lambda e, xb=xb, x1t=x1t: e.tensor_copy(xb.ap(), x1t.ap()),
                     ins=[x1t], outs=[xb])

                def emit_s(e, xb=xb, oc=oc):
                    return e.matmul(pE_.ap(p=(0, 1)), onec_bf.ap(), xb.ap(),
                                    start=(oc == 0), stop=(oc == 31),
                                    skip_group_check=True)
                P.op("pe", emit_s, ins=[onec_bf, xb],
                     outs=[pE_] if oc in (0, 31) else ())

                def emit_q(e, sq=sq, oc=oc):
                    return e.matmul(pF_.ap(p=(0, 1)), onec_bf.ap(), sq.ap(),
                                    start=(oc == 0), stop=(oc == 31),
                                    skip_group_check=True)
                P.op("pe", emit_q, ins=[onec_bf, sq],
                     outs=[pF_] if oc in (0, 31) else ())
            ABa.pop()
            P.barrier()
            AFa.push()
            ABa.push()

            def loader1(i, dst, t2=t2):
                P.dma("sync", dst.ap(),
                      scr_x1[i * 128:(i + 1) * 128, t2 * 512:(t2 + 1) * 512],
                      ins=[("x1d", i, t2)], outs=[dst])

            def write_xn1(i, tmp, bcb, t2=t2):
                xn = xn1[i]
                P.op("dve", lambda e, xn=xn, tmp=tmp, bcb=bcb, t2=t2: e.tensor_add(
                    xn.ap(c=(t2 * 512, t2 * 512 + 512)), tmp.ap(), bcb.ap()),
                     ins=[tmp, bcb], outs=[(xn, t2)])
            def sub_out1(sub, rtok):
                P.op("dve", lambda e, v=lg1v[sub], rtok=rtok: e.tensor_add(
                    v.ap(c=(0, 15)), rtok.ap(c=(0, 15)), rb1b.ap()),
                     ins=[rtok, rb1b], outs=[lg1v[sub]])
            layer_norm(loader1, write_xn1, stats_done=True,
                       router=(wr1_f, 15), sub_out=sub_out1)
            AFa.pop()
            ABa.pop()
            P.barrier()
            AFa.push()
            ABa.push()
            sigq = ABa.allocs(8, 512, name="sigq")
            qf32 = AFa.alloc(512, name="qf32")
            tmn = AFa.alloc(512, name="tmn")
            tex = AFa.alloc(512, name="tex")
            for ob in range(8):
                pp = pA if ob % 2 == 0 else pB
                proj_mm(P, w_q1, ob, wsA, "wq1" + str(t2),
                        lambda i: (xn1[i].ap(c=(t2 * 512, t2 * 512 + 512)), (xn1[i], t2)),
                        pp)
                P.op("act", lambda e, pp=pp, ob=ob: e.activation(
                    qf32.ap(), pp.ap(), AF.Identity, bias=bq1_sb.ap(c=(ob, ob + 1))),
                     ins=[pp, bq1_sb], outs=[qf32])
                P.op("dve", lambda e: e.tensor_scalar_min(tmn.ap(), qf32.ap(), 0.0),
                     ins=[qf32], outs=[tmn])
                P.op("act", lambda e: e.activation(tex.ap(), tmn.ap(), AF.Exp),
                     ins=[tmn], outs=[tex])
                P.op("dve", lambda e: e.tensor_scalar_max(tmn.ap(), qf32.ap(), 0.0),
                     ins=[qf32], outs=[tmn])
                P.op("dve", lambda e, sg=sigq[ob]: e.tensor_add(sg.ap(), tex.ap(), tmn.ap()),
                     ins=[tex, tmn], outs=[sigq[ob]])
            nrow = AFa.alloc(512, name="nrow")
            drow = AFa.alloc(512, p=1, name="drow")
            for hh in range(8):
                P.op("pe", lambda e, hh=hh: e.matmul(
                    pC.ap(), m1_bf.ap(c=(hh * 128, (hh + 1) * 128)), sigq[hh].ap(),
                    start=True, stop=True),
                     ins=M1K + [sigq[hh]], outs=[pC])
                P.op("pe", lambda e, hh=hh: e.matmul(
                    pD.ap(p=(0, 1)), z1_bf.ap(c=(hh, hh + 1)), sigq[hh].ap(),
                    start=True, stop=True),
                     ins=[z1_bf, sigq[hh]], outs=[pD])
                P.op("dve", lambda e: e.tensor_scalar_add(drow.ap(p=(0, 1)),
                                                          pD.ap(p=(0, 1)), DENEPS),
                     ins=[pD], outs=[drow])
                P.op("dve", lambda e: e.reciprocal(drow.ap(p=(0, 1)), drow.ap(p=(0, 1))),
                     ins=[drow], outs=[drow])
                P.op("pe", lambda e: e.matmul(pD.ap(), ones_f.ap(),
                                              drow.ap(p=(0, 1)), start=True, stop=True),
                     ins=[ones_f, drow], outs=[pD])
                P.op("act", lambda e: e.activation(nrow.ap(), pC.ap(), AF.Identity),
                     ins=[pC], outs=[nrow])
                P.op("dve", lambda e, hh=hh, t2=t2: e.tensor_tensor(
                    ret_sb[hh].ap(c=(t2 * 512, t2 * 512 + 512)), nrow.ap(), pD.ap(),
                    ALU.mult),
                     ins=[nrow, pD], outs=[(ret_sb[hh], t2)])
            e15 = AFa.alloc(16, name="e15")
            cur15 = AFa.alloc(16, name="cur15")
            m15 = AFa.alloc(8, name="m15")
            mp = AFa.alloc(16, name="mp")
            mpT = ABa.alloc(512, p=15, name="mpT")
            w8 = ABa.alloc(512, p=8, name="w8")
            for sub in range(4):
                lg = lg1v[sub]
                P.op("dve", lambda e, lg=lg: e.reduce_max(m15.ap(c=(0, 1)),
                                                          lg.ap(c=(0, 15)),
                                                          axis=AX, negate=True),
                     ins=[lg], outs=[(m15, "nm")])
                P.op("act", lambda e, lg=lg: e.activation(e15.ap(c=(0, 15)),
                                                          lg.ap(c=(0, 15)), AF.Exp,
                                                          bias=m15.ap(c=(0, 1))),
                     ins=[lg, (m15, "nm")], outs=[e15])
                P.op("dve", lambda e: e.reduce_sum(m15.ap(c=(1, 2)), e15.ap(c=(0, 15)),
                                                   axis=AX),
                     ins=[e15], outs=[(m15, "s")])
                P.op("dve", lambda e: e.reciprocal(m15.ap(c=(2, 3)), m15.ap(c=(1, 2))),
                     ins=[(m15, "s")], outs=[(m15, "rs")])
                P.op("dve", lambda e: e.tensor_copy(cur15.ap(c=(0, 15)), e15.ap(c=(0, 15))),
                     ins=[e15], outs=[cur15])
                for it in range(8):
                    P.op("dve", lambda e: e.reduce_max(m15.ap(c=(3, 4)),
                                                       cur15.ap(c=(0, 15)), axis=AX),
                         ins=[cur15], outs=[(m15, "mx")])
                    if it < 7:
                        P.op("dve", lambda e: e.tensor_scalar(mp.ap(c=(0, 15)),
                                                              cur15.ap(c=(0, 15)),
                                                              m15.ap(c=(3, 4)), None,
                                                              ALU.is_ge),
                             ins=[cur15, (m15, "mx")], outs=[mp])
                        P.op("dve", lambda e: e.scalar_tensor_tensor(
                            cur15.ap(c=(0, 15)), mp.ap(c=(0, 15)), -1e30,
                            cur15.ap(c=(0, 15)), ALU.mult, ALU.add),
                             ins=[mp, cur15], outs=[cur15])
                P.op("dve", lambda e: e.tensor_scalar(mp.ap(c=(0, 15)), e15.ap(c=(0, 15)),
                                                      m15.ap(c=(3, 4)), None, ALU.is_ge),
                     ins=[e15, (m15, "mx")], outs=[mp])
                P.op("dve", lambda e: e.scalar_tensor_tensor(
                    mp.ap(c=(0, 15)), e15.ap(c=(0, 15)), m15.ap(c=(2, 3)),
                    mp.ap(c=(0, 15)), ALU.mult, ALU.mult),
                     ins=[e15, (m15, "rs"), mp], outs=[mp])

                def emit_trm(e, sub=sub):
                    return e.transpose(pB.ap(p=(0, 15), c=(128, 256)), mp.ap(c=(0, 15)),
                                       id_f.ap())
                P.op("pe", emit_trm, ins=[mp, id_f], outs=[pB])
                P.op("act", lambda e, sub=sub: e.activation(
                    mpT.ap(c=(sub * 128, sub * 128 + 128)),
                    pB.ap(c=(128, 256), p=(0, 15)), AF.Identity),
                     ins=[pB], outs=[(mpT, sub)])
            P.op("pe", lambda e: e.matmul(pC.ap(p=(0, 8)), onehot_bf.ap(), mpT.ap(),
                                          start=True, stop=True),
                 ins=[onehot_bf] + [(mpT, s) for s in range(4)], outs=[pC])
            P.op("act", lambda e: e.activation(w8.ap(), pC.ap(p=(0, 8)), AF.Identity),
                 ins=[pC], outs=[w8])
            P.op("pe", lambda e: e.matmul(pC.ap(), e1_sb.ap(), w8.ap(),
                                          start=True, stop=True),
                 ins=[e1_sb, w8], outs=[pC])
            pairs = [(a1_bf.ap(c=(i * 128, i * 128 + 128)),
                      xn1[i].ap(c=(t2 * 512, t2 * 512 + 512))) for i in range(NCH)]
            _mmgroup(P, pD.ap(), pairs,
                     ins=A1K + [(xn1[i], t2) for i in range(NCH)], outs=[pD])
            h1b = AFa.alloc(512, name="h1b")
            P.op("act", lambda e: e.activation(h1b.ap(), pD.ap(), AF.Identity),
                 ins=[pD], outs=[h1b])
            P.op("dve", lambda e, t2=t2: e.tensor_tensor(
                hg1_sb.ap(c=(t2 * 512, t2 * 512 + 512)), h1b.ap(), pC.ap(), ALU.mult),
                 ins=[h1b, pC], outs=[(hg1_sb, t2)])
            ABa.pop()
            AFa.pop()
            P.barrier()

        for oc in range(32):
            for half in range(2):
                stt = wsA[half]
                load_half(w_o1, oc, half, stt, "wo1")
                for t2 in range(2):
                    pp = pA if t2 == 0 else pB
                    pairs = [(stt.ap(c=((i - half * 16) * 128,
                                        (i - half * 16) * 128 + 128)),
                              ret_sb[i // 4].ap(c=(t2 * 512, t2 * 512 + 512)))
                             for i in range(half * 16, half * 16 + 16)]
                    ins = [(stt, "c", s) for s in range(2)] + \
                          [(ret_sb[i // 4], t2) for i in range(half * 16, half * 16 + 16, 4)]
                    _mmgroup(P, pp.ap(), pairs, ins=ins, outs=[pp],
                             first=(half == 0), last=False)
            for half in range(2):
                stt = wsB[half]
                load_half(w_b1, oc, half, stt, "wb1")
                for t2 in range(2):
                    pp = pA if t2 == 0 else pB
                    pairs = [(stt.ap(c=((i - half * 16) * 128,
                                        (i - half * 16) * 128 + 128)),
                              xn1[i].ap(c=(t2 * 512, t2 * 512 + 512)))
                             for i in range(half * 16, half * 16 + 16)]
                    ins = [(stt, "c", s) for s in range(2)] + \
                          [(xn1[i], t2) for i in range(half * 16, half * 16 + 16)]
                    if half == 1:
                        pairs.append((b1t_bf.ap(c=(oc * 128, (oc + 1) * 128)),
                                      hg1_sb.ap(c=(t2 * 512, t2 * 512 + 512))))
                        ins = ins + B1K + [(hg1_sb, t2)]
                    _mmgroup(P, pp.ap(), pairs, ins=ins, outs=[pp],
                             first=False, last=(half == 1))
            for t2 in range(2):
                pp = pA if t2 == 0 else pB
                tt = to_t[t2]
                P.op("act", lambda e, tt=tt, pp=pp, oc=oc: e.activation(
                    tt.ap(), pp.ap(), AF.Identity, bias=bb1_sb.ap(c=(oc, oc + 1))),
                     ins=[pp, bb1_sb], outs=[tt])
                xf = xf_t[t2]
                P.dma("sync", xf.ap(),
                      scr_x1[oc * 128:(oc + 1) * 128, t2 * 512:(t2 + 1) * 512],
                      ins=[("x1d", oc, t2)], outs=[xf])
                ot = x1w[t2]
                P.op("dve", lambda e, ot=ot, tt=tt, xf=xf: e.tensor_add(
                    ot.ap(), tt.ap(), xf.ap()), ins=[tt, xf], outs=[ot])
                P.dma("pool", outT[oc * 128:(oc + 1) * 128, t2 * 512:(t2 + 1) * 512],
                      ot.ap(), ins=[ot], outs=[("out", oc, t2)])
        AFa.pop()
        P.barrier()
        for name, shape, dt, ap in tap_list:
            tout = nc.dram_tensor("tap_" + name, list(shape), dt,
                                  kind="ExternalOutput").ap()
            P.dma("sync", tout, ap, outs=[("tap", name)])
        P.barrier()
        P.realize(nc, stack)
    return nc



def _stage_inputs(inputs):
    f = np.float32
    x = np.asarray(inputs["L0_attn_x"], f).reshape(GTOK, D)
    Wq = np.asarray(inputs["L0_attn_Wq"], f)
    Wk = np.asarray(inputs["L0_attn_Wk"], f)
    Wv = np.asarray(inputs["L0_attn_Wv"], f)
    Wo = np.asarray(inputs["L0_attn_Wo"], f)
    bq = np.asarray(inputs["L0_attn_bq"], f)
    bk = np.asarray(inputs["L0_attn_bk"], f)
    bv = np.asarray(inputs["L0_attn_bv"], f)
    bo = np.asarray(inputs["L0_attn_bo"], f)
    A0 = np.asarray(inputs["L0_ffn_A"], f)
    B0 = np.asarray(inputs["L0_ffn_B"], f)
    Wr0 = np.asarray(inputs["L0_ffn_Wr"], f)
    Wq1 = np.asarray(inputs["L1_attn_Wq"], f)
    bq1 = np.asarray(inputs["L1_attn_bq"], f)
    Wo1 = np.asarray(inputs["L1_attn_Wo"], f)
    M1 = np.asarray(inputs["L1_attn_M"], f)
    z1 = np.asarray(inputs["L1_attn_z"], f)
    bW = np.asarray(inputs["L1_ffn_base_W"], f)
    bb = np.asarray(inputs["L1_ffn_base_b"], f)
    rW = np.asarray(inputs["L1_ffn_router_W"], f)
    rb = np.asarray(inputs["L1_ffn_router_b"], f)
    A1 = np.asarray(inputs["L1_ffn_A"], f)
    B1 = np.asarray(inputs["L1_ffn_B"], f)
    em = np.asarray(inputs["L1_ffn_expert_map"], np.int32)
    assert inputs["L1_attn_Wk"] is not None and inputs["L1_attn_Wv"] is not None

    C = np.ascontiguousarray
    bf = ml_dtypes.bfloat16
    xTf = C(x.T)
    sel = np.concatenate([np.arange(4 * g * 128, (4 * g + 1) * 128) for g in range(8)])
    E0 = np.zeros((64, 1024), bf)
    for k in range(64):
        E0[k, k * 16:(k + 1) * 16] = bf(2.0)
    E1 = np.zeros((8, 128), bf)
    for e_ in range(8):
        E1[e_, e_ * 16:(e_ + 1) * 16] = bf(2.0)
    shared = dict(
        w_o0=C(Wo.T), b_o0=C(bo.reshape(32, 128).T),
        w_A0=C(A0.reshape(1024, D).T),
        w_B0=C(B0.transpose(0, 2, 1).reshape(1024, D)),
        w_r0=C(Wr0.T),
        w_q1=C(Wq1[sel, :].T), b_q1=C(bq1[sel].reshape(8, 128).T),
        w_M1=C(M1.reshape(8 * 128, 128)), w_z1=C(z1.T),
        w_o1=C(Wo1.T), w_b1=C(bW.T), b_b1=C(bb.reshape(32, 128).T),
        w_r1=C(rW.T), b_r1=C(rb.reshape(15, 1)), b_r1row=C(rb.reshape(1, 15)),
        w_A1=C(A1.reshape(128, D).T),
        w_B1=C(B1.transpose(0, 2, 1).reshape(128, D)),
        emap=C(em.reshape(15, 1)),
        ident_bf=np.eye(128, dtype=bf), ident_f=np.eye(128, dtype=f),
        masktri=C((np.arange(128)[:, None] <= np.arange(128)[None, :]).astype(bf)),
        E0c=E0, E1c=E1,
        iota15=C(np.tile(np.arange(8, dtype=f), (15, 1))),
        cstc=C(np.tile(np.array([[LNEPS, EXPB]], dtype=f), (128, 1))),
    )
    in_maps = []
    for c in range(NCORE):
        cs = slice(c * 512, (c + 1) * 512)
        m = dict(shared)
        m["xT"] = C(xTf[:, c * TOK:(c + 1) * TOK])
        m["w_qkv"] = C(np.concatenate([Wq[cs, :].T, Wk[cs, :].T, Wv[cs, :].T], axis=1))
        m["b_qkv"] = C(np.concatenate([bq[cs], bk[cs], bv[cs]]).reshape(12, 128).T)
        in_maps.append(m)
    return in_maps


def kernel(**inputs):
    nc = build_program()
    in_maps = _stage_inputs(inputs)
    res = run_bass_kernel_spmd(nc, in_maps, core_ids=list(range(NCORE)))
    out = np.empty((D, GTOK), np.float32)
    for c in range(NCORE):
        out[:, c * TOK:(c + 1) * TOK] = np.asarray(res.results[c]["outT"], np.float32)
    return np.ascontiguousarray(out.T).reshape(4, SEQ, D)
